# version 37
# baseline (speedup 1.0000x reference)
import contextlib
import numpy as np
import ml_dtypes
import concourse.bass as bass
import concourse.mybir as mybir
from concourse.bass_isa import ReduceOp
from concourse.bass_utils import run_bass_kernel_spmd

F32 = mybir.dt.float32
F32R = mybir.dt.float32r
BF16 = mybir.dt.bfloat16
AF = mybir.ActivationFunctionType
ALU = mybir.AluOpType
AX = mybir.AxisListType

NCORES = 8
TOK = 2048
T = 512
NT = TOK // T
DM = 1024
ND = DM // 128
E = 2048
NE = E // 128
HF = 4096
NH = HF // 128
R = 64
INV_D = 1.0 / DM
INV_E = 1.0 / E
INV_H = 1.0 / HF


class Sem:
    def __init__(self, h):
        self.h = h
        self.n = 0

    def inc(self, instr, amt=1):
        instr.then_inc(self.h, amt)
        self.n += amt
        return self.n




_OPMETA = {
    "tensor_scalar": (("out",), ("in0", "scalar1", "scalar2"), False),
    "scalar_tensor_tensor": (("out",), ("in0", "scalar", "in1"), False),
    "tensor_tensor": (("out",), ("in0", "in1"), False),
    "tensor_tensor_scan": (("out",), ("data0", "data1", "initial"), False),
    "tensor_copy": (("out",), ("in_",), False),
    "reduce_sum": (("out",), ("in_",), False),
    "tensor_reduce": (("out",), ("in_",), False),
    "reciprocal": (("out",), ("in_",), False),
    "activation": (("out", "accum_out"), ("in_", "bias", "scale"), False),
    "copy": (("out",), ("in_",), False),
    "square": (("out",), ("in_",), False),
    "sign": (("out",), ("in_",), False),
    "mul": (("out",), ("in_",), False),
    "add": (("out",), ("in_",), False),
    "sqrt": (("out",), ("in_",), False),
    "matmul": (("out",), ("lhsT", "rhs"), True),
    "transpose": (("out",), ("in_", "identity"), True),
}
_BLK = 512


def _regions(ap):
    try:
        name = ap.tensor.name
        off = ap.offset
        span = max(1, ap.free_size())
    except AttributeError:
        return ()
    return tuple((name, b) for b in range(off // _BLK, (off + span - 1) // _BLK + 1))


class Tracker:
    def __init__(self):
        self.w = {}
        self.r = {}

    def read_edges(self, ap):
        out = []
        for rg in _regions(ap):
            rec = self.w.get(rg)
            if rec is not None:
                out.append((rec[0], rec[1]))
        return out

    def write_edges(self, ap, is_mm):
        out = []
        for rg in _regions(ap):
            rec = self.w.get(rg)
            if rec is not None and not (is_mm and rec[2]):
                out.append((rec[0], rec[1]))
            for e in self.r.get(rg, ()):
                out.append(e)
        return out

    def note_read(self, ap, sem, ticket):
        for rg in _regions(ap):
            self.r.setdefault(rg, []).append((sem, ticket))

    def note_write(self, ap, sem, ticket, is_mm):
        for rg in _regions(ap):
            self.w[rg] = (sem, ticket, is_mm)
            self.r[rg] = []

    def note_dma_write(self, ap, sem, ticket):
        self.note_write(ap, sem, ticket, False)

    def note_dma_read(self, ap, sem, ticket):
        self.note_read(ap, sem, ticket)


class Wrap:

    def __init__(self, eng, sem, tracker, wm):
        self._e = eng
        self._s = sem
        self._t = tracker
        self._wm = wm

    def wait_ge(self, semh, val):
        k = id(semh)
        if self._wm.get(k, -1) >= val:
            return
        self._e.wait_ge(semh, val)
        self._wm[k] = val

    def _edge(self, sem, t):
        self.wait_ge(sem.h, t)

    def __getattr__(self, m):
        if m not in _OPMETA:
            return getattr(self._e, m)
        outs, ins, is_mm = _OPMETA[m]
        raw = getattr(self._e, m)
        tr = self._t

        def emit(**kw):
            for k in ins:
                v = kw.get(k)
                if v is not None and not isinstance(v, (int, float)):
                    for sem, t in tr.read_edges(v):
                        self._edge(sem, t)
            for k in outs:
                v = kw.get(k)
                if v is not None and not isinstance(v, (int, float)):
                    for sem, t in tr.write_edges(v, is_mm):
                        self._edge(sem, t)
            instr = raw(**kw)
            instr.then_inc(self._s.h, 1)
            self._s.n += 1
            tick = self._s.n
            for k in ins:
                v = kw.get(k)
                if v is not None and not isinstance(v, (int, float)):
                    tr.note_read(v, self._s, tick)
            for k in outs:
                v = kw.get(k)
                if v is not None and not isinstance(v, (int, float)):
                    tr.note_write(v, self._s, tick, is_mm)
            return instr
        return emit

    def memset(self, ap, const):
        for sem, t in self._t.write_edges(ap, False):
            self._edge(sem, t)
        instr = self._e.memset(ap, const)
        instr.then_inc(self._s.h, 1)
        self._s.n += 1
        self._t.note_write(ap, self._s, self._s.n, False)
        return instr


def build_program():
    nc = bass.Bass("TRN2", target_bir_lowering=False, debug=False)
    ap = lambda *a, **k: nc.dram_tensor(*a, **k).ap()

    xT = ap("xT", [DM, TOK], F32, kind="ExternalInput")
    xLB = ap("xLB", [DM, 3], F32, kind="ExternalInput")
    winshard = ap("winshard", [128, 4096], F32, kind="ExternalInput")
    winbc = ap("winbc", [DM, 1], F32, kind="ExternalInput")
    convw = ap("convw", [E, 4], F32, kind="ExternalInput")
    convb = ap("convb", [E], F32, kind="ExternalInput")
    wxw = ap("wxw", [E, 192], F32, kind="ExternalInput")
    wdt0 = ap("wdt0", [R, 1], F32, kind="ExternalInput")
    bdt0c = ap("bdt0c", [128, 1], F32, kind="ExternalInput")
    a0row = ap("a0row", [1, 128], F32, kind="ExternalInput")
    dvec = ap("dvec", [E], F32, kind="ExternalInput")
    lngv = ap("lngv", [E], F32, kind="ExternalInput")
    lnbv = ap("lnbv", [E], F32, kind="ExternalInput")
    woshard = ap("woshard", [128, 2048], F32, kind="ExternalInput")
    rsc128 = ap("rsc128", [128, 1], F32, kind="ExternalInput")
    fcshard = ap("fcshard", [128, HF], F32, kind="ExternalInput")
    lnw3v = ap("lnw3v", [HF], F32, kind="ExternalInput")
    lnb3v = ap("lnb3v", [HF], F32, kind="ExternalInput")
    pjshard = ap("pjshard", [128, 4096], F32, kind="ExternalInput")
    eye = ap("eye", [128, 128], F32, kind="ExternalInput")
    yT = ap("yT", [DM, TOK], F32, kind="ExternalOutput")

    st_win = ap("st_win", [ND * 128, 4096], BF16, kind="Internal")
    st_wout = ap("st_wout", [ND * 128, 2048], BF16, kind="Internal")
    st_fc = ap("st_fc", [ND * 128, HF], BF16, kind="Internal")
    st_proj = ap("st_proj", [ND * 128, 4096], BF16, kind="Internal")
    ccw = ap("ccw", [128, 4096], BF16, kind="Internal")
    ccf = ap("ccf", [128, 4096], BF16, kind="Internal")
    ccp = ap("ccp", [128, 4096], BF16, kind="Internal")
    cco = ap("cco", [128, 2048], BF16, kind="Internal")
    grin = ap("grin", [128, 2], F32, kind="Internal")
    grout = ap("grout", [128, 2], F32, kind="Internal")
    d_uc = ap("d_uc", [128, 16], BF16, kind="Internal")

    ctx = contextlib.ExitStack()
    EN = ctx.enter_context
    with ctx:
        x_fm = EN(nc.sbuf_tensor("x_fm", [128, ND * T], F32))
        x_fm2 = EN(nc.sbuf_tensor("x_fm2", [128, ND * T], F32))
        xln = EN(nc.sbuf_tensor("xln", [128, ND * T], BF16))
        xp_s = EN(nc.sbuf_tensor("xp_s", [128, 2 * (T + 3)], BF16))
        cxp = EN(nc.sbuf_tensor("cxp", [128, NE * 3], BF16))
        mixA = EN(nc.sbuf_tensor("mixA", [128, 16 * T], BF16))
        mixB = EN(nc.sbuf_tensor("mixB", [128, 16 * T], BF16))
        ygb = EN(nc.sbuf_tensor("ygb", [128, 4 * T], BF16))
        wgtA = EN(nc.sbuf_tensor("wgtA", [128, 16384], BF16))
        wgtB = EN(nc.sbuf_tensor("wgtB", [128, 16384], BF16))
        pjb = EN(nc.sbuf_tensor("pjb", [128, 4 * DM], BF16))
        out_s = EN(nc.sbuf_tensor("out_s", [128, ND * T], F32))
        tf1 = EN(nc.sbuf_tensor("tf1", [128, T], F32))
        tf2 = EN(nc.sbuf_tensor("tf2", [128, T], F32))
        tb1 = EN(nc.sbuf_tensor("tb1", [128, 2 * T], BF16))
        tb2 = EN(nc.sbuf_tensor("tb2", [128, 2 * T], BF16))
        carry = EN(nc.sbuf_tensor("carry", [128, 1], F32))
        dgb = EN(nc.sbuf_tensor("dgb", [128, 2 * 4 * 128], BF16))
        bca = EN(nc.sbuf_tensor("bca", [128, T], F32))
        bcb = EN(nc.sbuf_tensor("bcb", [128, T], F32))
        colf = EN(nc.sbuf_tensor("colf", [128, 96], F32))
        colb = EN(nc.sbuf_tensor("colb", [128, 128], BF16))
        colf2 = EN(nc.sbuf_tensor("colf2", [128, 8], F32))
        cwf = EN(nc.sbuf_tensor("cwf", [128, 64], F32))
        wxbf = EN(nc.sbuf_tensor("wxbf", [128, 16 * 192], BF16))
        lnb3c = EN(nc.sbuf_tensor("lnb3c", [128, 32], F32))
        rwf = EN(nc.sbuf_tensor("rwf", [1, 3 * T], F32))
        rwr = EN(nc.sbuf_tensor("rwr", [1, 2 * T], F32R))
        rsm = EN(nc.sbuf_tensor("rsm", [1, 320], F32))
        rsr = EN(nc.sbuf_tensor("rsr", [1, 288], F32R))
        rbf = EN(nc.sbuf_tensor("rbf", [1, 1024], BF16))
        ucrow = EN(nc.sbuf_tensor("ucrow", [1, 2048], BF16))
        gac = EN(nc.sbuf_tensor("gac", [128, 40], F32))
        gacr = EN(nc.sbuf_tensor("gacr", [128, 32], F32R))
        ones128r = EN(nc.sbuf_tensor("ones128r", [128, 128], F32R))
        eyef = EN(nc.sbuf_tensor("eyef", [128, 128], F32))
        xlbf = EN(nc.sbuf_tensor("xlbf", [128, ND * 3], F32))
        xlbb = EN(nc.sbuf_tensor("xlbb", [128, ND * 3], BF16))
        xlnp = EN(nc.sbuf_tensor("xlnp", [128, ND * 3], BF16))
        uccol = EN(nc.sbuf_tensor("uccol", [128, 16], BF16))

        ps = [EN(nc.psum_tensor("psb%d" % i, [128, T], F32)) for i in range(8)]

        L1 = Sem(EN(nc.semaphore("L1")))
        L2 = [Sem(EN(nc.semaphore("L2%d" % i))) for i in range(2)]
        LF = [Sem(EN(nc.semaphore("LF%d" % i))) for i in range(4)]
        LP = [Sem(EN(nc.semaphore("LP%d" % i))) for i in range(4)]
        Lwo = [Sem(EN(nc.semaphore("Lwo%d" % i))) for i in range(2)]
        Sx = Sem(EN(nc.semaphore("Sx")))
        Sw = Sem(EN(nc.semaphore("Sw")))
        Swo = Sem(EN(nc.semaphore("Swo")))
        Sfc = Sem(EN(nc.semaphore("Sfc")))
        SfA = Sem(EN(nc.semaphore("SfA")))
        Spj = [Sem(EN(nc.semaphore("Spj%d" % i))) for i in range(2)]
        Tw = [Sem(EN(nc.semaphore("Tw%d" % i))) for i in range(2)]
        Two = [Sem(EN(nc.semaphore("Two%d" % i))) for i in range(2)]
        TqF = [Sem(EN(nc.semaphore("TqF%d" % i))) for i in range(2)]
        TqP = [Sem(EN(nc.semaphore("TqP%d" % i))) for i in range(2)]
        Tout = Sem(EN(nc.semaphore("Tout")))
        Sub = [Sem(EN(nc.semaphore("Sub%d" % i))) for i in range(2)]
        Lsh = [Sem(EN(nc.semaphore("Lsh%d" % i))) for i in range(4)]
        TSG = EN(nc.semaphore("TSG"))
        CC = EN(nc.semaphore("CC"))
        LPQ = [Sem(EN(nc.semaphore("LPQ%d" % i))) for i in range(8)]
        PEs = Sem(EN(nc.semaphore("PEs")))
        ACs = Sem(EN(nc.semaphore("ACs")))
        DVs = Sem(EN(nc.semaphore("DVs")))
        GPs = Sem(EN(nc.semaphore("GPs")))

        block = EN(nc.Block())

        TRK = Tracker()
        WM = {"PE": {}, "AC": {}, "DV": {}, "GP": {}, "SP": {}}

        def BT(f):
            block.tensor(lambda e, f=f: f(Wrap(e, PEs, TRK, WM["PE"])))

        def BA(f):
            block.scalar(lambda e, f=f: f(Wrap(e, ACs, TRK, WM["AC"])))

        def BV(f):
            block.vector(lambda e, f=f: f(Wrap(e, DVs, TRK, WM["DV"])))

        def BG(f):
            block.gpsimd(lambda e, f=f: f(Wrap(e, GPs, TRK, WM["GP"])))

        def BS(f):
            block.sync(lambda e, f=f: f(Wrap(e, GPs, TRK, WM["SP"])))

        P0Q = []

        def drain(n):
            for _ in range(n):
                if P0Q:
                    P0Q.pop(0)()

        def drain_all():
            drain(len(P0Q))

        D_c = lambda c: colf[:, c:c + 1]
        G_c = lambda c: colf[:, 16 + c:16 + c + 1]
        Bb_c = lambda c: colf[:, 32 + c:32 + c + 1]
        CB_c = lambda c: colf[:, 48 + c:48 + c + 1]
        LNW_c = lambda k: colf[:, 64 + k:64 + k + 1]
        Dbf_c = lambda c: colb[:, c:c + 1]
        D2bf_c = lambda c: colb[:, 16 + c:16 + c + 1]
        LNWB = lambda k: colb[:, 32 + 2 * k:32 + 2 * k + 2]
        WDT0B = colb[0:R, 96:97]
        WINBB = lambda k: colb[:, 97 + k:97 + k + 1]
        ONESC = colb[:, 105:106]
        THRF, NEGF, THRP, NEGP, RS_C, BD0_C, EPS_C = (colf2[:, i:i + 1] for i in range(7))
        WDT0F = gac[0:R, 32:33]
        ONES_R = rsr[0:1, 0:128]
        AEXP_R = rsr[0:1, 128:256]
        RSTDP_R = rsr[0:1, 256:260]
        M2P_R = rsr[0:1, 260:264]
        A0_S = rsm[0:1, 128:256]
        MUP = rsm[0:1, 256:260]
        E2P = rsm[0:1, 260:264]
        ONEST_BF = rbf[0:1, 0:T]
        M3_BF = rbf[0:1, T:2 * T]
        U_ROW = lambda oc: ucrow[0:1, oc * 128:(oc + 1) * 128]
        C_ROW = lambda oc: ucrow[0:1, 1024 + oc * 128:1024 + (oc + 1) * 128]

        RF = lambda i: rwf[0:1, i * T:(i + 1) * T] if i < 3 else \
            (bca if i == 3 else bcb)[0:1, :]
        RRR = lambda i: rwr[0:1, i * T:(i + 1) * T]

        xs_c = lambda c: mixA[:, c * T:(c + 1) * T]
        sz_c = lambda c: mixB[:, c * T:(c + 1) * T]
        hg_c = lambda j: (mixA if j < 16 else mixB)[:, (j % 16) * T:((j % 16) + 1) * T]
        yg_c = lambda c: ygb[:, (c % 4) * T:(c % 4 + 1) * T]
        xfm_c = lambda k: x_fm[:, k * T:(k + 1) * T]
        xln_c = lambda k: xln[:, k * T:(k + 1) * T]
        outs_c = lambda k: out_s[:, k * T:(k + 1) * T]
        xpb = lambda p: xp_s[:, p * (T + 3):(p + 1) * (T + 3)]
        dg_t = lambda c, k: dgb[:, ((c % 2) * 4 + k) * 128:((c % 2) * 4 + k + 1) * 128]
        win_sl = lambda k: wgtA[:, k * 2048:(k + 1) * 2048]
        wout_sl = lambda k: wgtB[:, k * DM:(k + 1) * DM]
        fc_lhs = lambda k, oc: (wgtA if k < 4 else wgtB)[:, (k % 4) * 4096 + oc * 128:
                                                         (k % 4) * 4096 + (oc + 1) * 128]
        pj_sl = lambda k: pjb[:, (k % 4) * DM:((k % 4) + 1) * DM]
        pj_pair = lambda k: pjb[:, (k % 4) * DM:((k % 4) + 2) * DM]
        stgf = out_s[:, 0:4096]
        stgf_h = lambda p: out_s[:, p * 2048:(p + 1) * 2048]
        stg_bf = wgtB[:, 0:4096]
        stgb_h = lambda p: wgtB[:, p * 2048:(p + 1) * 2048]
        stgb_q = lambda p: wgtB[:, p * 1024:(p + 1) * 1024]
        sq_b = lambda p: tb1[:, p * T:(p + 1) * T]
        Cbf = tb1[:, 0:T]
        drbf = tb1[:, T:2 * T]
        cb_b = lambda p: tb2[:, p * T:(p + 1) * T]
        sq4 = lambda j: (tb1 if j < 2 else tb2)[:, (j % 2) * T:(j % 2 + 1) * T]
        cb4 = lambda j: mixB[:, j * T:(j + 1) * T]

        tk = {}

        def tw(key, default=0):
            return tk.get(key, default)

        def _p0v(sync):
            last = None
            for dst, src, slow in (
                (eyef[:], eye, False),
                (colf[:, 0:16], dvec.rearrange("(c p) -> p c", p=128), True),
                (colf[:, 16:32], lngv.rearrange("(c p) -> p c", p=128), True),
                (colf[:, 32:48], lnbv.rearrange("(c p) -> p c", p=128), True),
                (colf[:, 48:64], convb.rearrange("(c p) -> p c", p=128), True),
                (colf[:, 64:96], lnw3v.rearrange("(c p) -> p c", p=128), True),
                (lnb3c[:], lnb3v.rearrange("(c p) -> p c", p=128), True),
                (WDT0F, wdt0, True),
                (BD0_C, bdt0c, True),
                (RS_C, rsc128, True),
                (rsm[0:1, 128:256], a0row, False),
                (gac[:, 24:32].rearrange("p (k one) -> p k one", one=1),
                 winbc.rearrange("(k p) one -> p k one", p=128), True),
                (x_fm[:, 0:64].rearrange("p (c k) -> p c k", k=4),
                 convw.rearrange("(c p) k -> p c k", p=128), True),
                (x_fm[:, 64:64 + 16 * 192].rearrange("p (c r) -> p c r", r=192),
                 wxw.rearrange("(c p) r -> p c r", p=128), True),
                (xlbf[:].rearrange("p (k c) -> p k c", c=3),
                 xLB.rearrange("(k p) c -> p k c", p=128), True),
            ):
                L1.inc(sync.dma_start(out=dst, in_=src, allow_slow_non_contiguous=slow), 16)
            tk["p0v"] = L1.n
        BS(_p0v)

        def _p0c_dv(dve):
            dve.wait_ge(L1.h, tk["p0v"])
            dve.memset(EPS_C, 1e-5)
            dve.memset(ONEST_BF, 1.0)
            dve.memset(ONESC, 1.0)
            dve.memset(carry[:], 0.0)
            dve.memset(cxp[:], 0.0)
            dve.tensor_scalar(out=ONES_R, in0=eyef[0:1, 0:128], scalar1=0.0, scalar2=1.0,
                              op0=ALU.mult, op1=ALU.add)
            dve.tensor_scalar(out=ones128r[:], in0=eyef[:], scalar1=0.0, scalar2=1.0,
                              op0=ALU.mult, op1=ALU.add)
            dve.tensor_scalar(out=rsr[0:1, 256:264], in0=eyef[0:1, 0:8], scalar1=0.0,
                              scalar2=None, op0=ALU.mult)
            dve.tensor_copy(out=cwf[:], in_=x_fm[:, 0:64])
            dve.tensor_copy(out=wxbf[:], in_=x_fm[:, 64:64 + 16 * 192])
            dve.tensor_copy(out=colb[:, 0:16], in_=colf[:, 0:16])
            dve.tensor_copy(out=WDT0B, in_=WDT0F)
            dve.tensor_copy(out=colb[:, 97:105], in_=gac[:, 24:32])
            dve.tensor_copy(out=colb[:, 32:96:2], in_=colf[:, 64:96])
            i = dve.tensor_copy(out=colb[:, 33:96:2], in_=lnb3c[:])
            tk["p0cdv"] = DVs.n
        BV(_p0c_dv)

        def _p0c_ac(act):
            act.wait_ge(DVs.h, tk["p0cdv"])
            act.square(out=colb[:, 16:32], in_=colf[:, 0:16])
            i = act.activation(out=AEXP_R, in_=A0_S, func=AF.Exp)
            i = act.copy(out=xlbb[:], in_=xlbf[:])
            tk["p0cac"] = ACs.n
        BA(_p0c_ac)

        xlnF = xln[:].bitcast(F32)
        pnP_ = mixA[:, 0:4096]
        pnN_ = mixA[:, 4096:8192]
        RG = [list(range(NCORES))]

        def _shld(sync):
            sync.wait_ge(DVs.h, tk["p0cdv"])
            Lsh[0].inc(sync.dma_start(out=x_fm[:, 0:4096], in_=winshard), 16)
            Lsh[1].inc(sync.dma_start(out=out_s[:, 0:4096], in_=fcshard), 16)
            Lsh[2].inc(sync.dma_start(out=x_fm2[:, 0:4096], in_=pjshard), 16)
            Lsh[3].inc(sync.dma_start(out=xlnF[:, 0:2048], in_=woshard), 16)
        BS(_shld)

        def _wcast(act):
            act.wait_ge(Lsh[0].h, 16)
            i = act.copy(out=wgtB[:, 0:4096], in_=x_fm[:, 0:4096])
            tk["wincast"] = ACs.n
        BA(_wcast)

        def _x0(sync):
            sync.wait_ge(ACs.h, tk["wincast"])
            Sx.inc(sync.dma_start(
                out=x_fm[:].rearrange("p (k t) -> p k t", t=T),
                in_=xT.rearrange("(k p) t -> p k t", p=128)[:, :, 0:T]), 16)
            tk["xload_t0"] = Sx.n
        BS(_x0)

        def _gpart(dve):
            dve.wait_ge(Lsh[1].h, 16)
            dve.reduce_sum(out=gac[:, 0:1], in_=out_s[:, 0:4096], axis=AX.X,
                           apply_absolute_value=True)
            dve.wait_ge(Lsh[2].h, 16)
            i = dve.reduce_sum(out=gac[:, 1:2], in_=x_fm2[:, 0:4096], axis=AX.X,
                               apply_absolute_value=True)
            tk["gpart"] = DVs.n
        BV(_gpart)

        def _cc1(gp):
            gp.wait_ge(ACs.h, tk["wincast"])
            gp.dma_start(out=ccw, in_=wgtB[:, 0:4096]).then_inc(TSG, 16)
            gp.wait_ge(TSG, 16)
            gp.collective_compute("AllGather", ALU.bypass, replica_groups=RG,
                                  ins=[ccw.opt()], outs=[st_win.opt()]).then_inc(CC, 1)
            gp.wait_ge(DVs.h, tk["gpart"])
            gp.dma_start(out=grin, in_=gac[:, 0:2]).then_inc(TSG, 16)
            gp.wait_ge(TSG, 32)
            gp.collective_compute("AllReduce", ALU.add, replica_groups=RG,
                                  ins=[grin.opt()], outs=[grout.opt()]).then_inc(CC, 1)
            gp.wait_ge(CC, 2)
            gp.dma_start(out=gac[:, 4:6], in_=grout).then_inc(TSG, 16)
        BG(_cc1)

        def _gcp(dve):
            dve.wait_ge(TSG, 48)
            i = dve.tensor_scalar(out=gacr[:, 0:2], in0=gac[:, 4:6], scalar1=1.0,
                                  scalar2=None, op0=ALU.mult)
            tk["gcp"] = DVs.n
        BV(_gcp)

        def _gmm(pe):
            pe.wait_ge(DVs.h, tk["gcp"])
            i = pe.matmul(skip_group_check=True, out=ps[6][:, 0:2], lhsT=ones128r[:],
                          rhs=gacr[:, 0:2], start=True, stop=True)
            tk["gmm"] = PEs.n
        BT(_gmm)

        def _thr(dve):
            dve.wait_ge(PEs.h, tk["gmm"])
            dve.tensor_scalar(out=THRF, in0=ps[6][:, 0:1], scalar1=0.5 / 4194304.0,
                              scalar2=5e-7, op0=ALU.mult, op1=ALU.add)
            dve.tensor_scalar(out=NEGF, in0=ps[6][:, 0:1], scalar1=-0.5 / 4194304.0,
                              scalar2=-5e-7, op0=ALU.mult, op1=ALU.add)
            dve.tensor_scalar(out=THRP, in0=ps[6][:, 1:2], scalar1=0.5 / 4194304.0,
                              scalar2=5e-7, op0=ALU.mult, op1=ALU.add)
            i = dve.tensor_scalar(out=NEGP, in0=ps[6][:, 1:2], scalar1=-0.5 / 4194304.0,
                                  scalar2=-5e-7, op0=ALU.mult, op1=ALU.add)
            tk["thrF"] = DVs.n
            tk["thrP"] = DVs.n
        BV(_thr)

        def _qF(dve):
            dve.tensor_scalar(out=pnP_, in0=out_s[:, 0:4096], scalar1=THRF,
                              scalar2=None, op0=ALU.is_gt)
            dve.tensor_scalar(out=pnN_, in0=out_s[:, 0:4096], scalar1=NEGF,
                              scalar2=None, op0=ALU.is_lt)
            i = dve.tensor_tensor(out=pjb[:, 0:4096], in0=pnP_, in1=pnN_,
                                  op=ALU.subtract)
            tk["qFsub"] = DVs.n
        BV(_qF)

        def _qP(dve):
            dve.tensor_scalar(out=pnP_, in0=x_fm2[:, 0:4096], scalar1=THRP,
                              scalar2=None, op0=ALU.is_gt)
            dve.tensor_scalar(out=pnN_, in0=x_fm2[:, 0:4096], scalar1=NEGP,
                              scalar2=None, op0=ALU.is_lt)
            i = dve.tensor_tensor(out=mixB[:, 0:4096], in0=pnP_, in1=pnN_,
                                  op=ALU.subtract)
            tk["qPsub"] = DVs.n
        BV(_qP)

        def _wocast(act):
            act.wait_ge(Lsh[3].h, 16)
            act.wait_ge(L1.h, tk["p0v"])
            i = act.activation(out=mixB[:, 4096:6144], in_=xlnF[:, 0:2048],
                               func=AF.Identity, scale=RS_C)
            tk["wocast"] = ACs.n
        BA(_wocast)

        def _cc2(gp):
            gp.wait_ge(DVs.h, tk["qFsub"])
            gp.dma_start(out=ccf, in_=pjb[:, 0:4096]).then_inc(TSG, 16)
            gp.wait_ge(TSG, 64)
            gp.collective_compute("AllGather", ALU.bypass, replica_groups=RG,
                                  ins=[ccf.opt()], outs=[st_fc.opt()]).then_inc(CC, 1)
            gp.wait_ge(DVs.h, tk["qPsub"])
            gp.dma_start(out=ccp, in_=mixB[:, 0:4096]).then_inc(TSG, 16)
            gp.wait_ge(TSG, 80)
            gp.collective_compute("AllGather", ALU.bypass, replica_groups=RG,
                                  ins=[ccp.opt()], outs=[st_proj.opt()]).then_inc(CC, 1)
            gp.wait_ge(ACs.h, tk["wocast"])
            gp.dma_start(out=cco, in_=mixB[:, 4096:6144]).then_inc(TSG, 16)
            gp.wait_ge(TSG, 96)
            gp.collective_compute("AllGather", ALU.bypass, replica_groups=RG,
                                  ins=[cco.opt()], outs=[st_wout.opt()]).then_inc(CC, 1)
        BG(_cc2)

        def _w0(sync):
            sync.wait_ge(CC, 1)
            Sw.inc(sync.dma_start(
                out=wgtA[:].rearrange("p (k f) -> p k f", f=2048),
                in_=st_win.rearrange("(k p) f -> p k f", p=128)[:, :, 0:2048]), 16)
            tk["winxp_t0"] = Sw.n
        BS(_w0)

        tk["p0_quant_done"] = tk["qPsub"]
        tk["pnP%d" % (NH - 1)] = tk["qPsub"]
        tk["qF13"] = tk["qPsub"]
        tk["woc%d" % (NE - 1)] = tk["wocast"]

        def qp_drain(n):
            pass

        def qp_flush_gamma():
            pass

        def qp_finish():
            pass

        def _pro_dv(dve):
            dve.wait_ge(L1.h, tk["p0v"])
            i = dve.tensor_tensor(out=tb1[:, 0:ND * 3], in0=xlbf[:], in1=xlbf[:],
                                  op=ALU.mult)
            tk["pro_sq"] = DVs.n
        BV(_pro_dv)

        def _pro_pe(pe):
            pe.wait_ge(DVs.h, tk["pro_sq"])
            for k in range(ND):
                pe.matmul(skip_group_check=True, out=ps[4][0:1, 0:3], lhsT=ONESC, rhs=xlbb[:, k * 3:(k + 1) * 3],
                          start=(k == 0), stop=(k == ND - 1))
            last = None
            for k in range(ND):
                last = pe.matmul(skip_group_check=True, out=ps[5][0:1, 0:3], lhsT=ONESC,
                                 rhs=tb1[:, k * 3:(k + 1) * 3],
                                 start=(k == 0), stop=(k == ND - 1))
            tk["pro_mm"] = PEs.n
        BT(_pro_pe)

        def _pro_rows_dv(dve):
            dve.wait_ge(PEs.h, tk["pro_mm"])
            dve.tensor_scalar(out=MUP[0:1, 0:3], in0=ps[4][0:1, 0:3], scalar1=INV_D,
                              scalar2=None, op0=ALU.mult)
            dve.tensor_scalar(out=E2P[0:1, 0:3], in0=ps[5][0:1, 0:3], scalar1=INV_D,
                              scalar2=None, op0=ALU.mult)
            dve.tensor_tensor(out=rsm[0:1, 264:267], in0=MUP[0:1, 0:3], in1=MUP[0:1, 0:3],
                              op=ALU.mult)
            i = dve.tensor_tensor(out=E2P[0:1, 0:3], in0=E2P[0:1, 0:3],
                                  in1=rsm[0:1, 264:267], op=ALU.subtract)
            tk["pro_r1"] = DVs.n
        BV(_pro_rows_dv)

        def _pro_rows_ac(act):
            act.wait_ge(DVs.h, tk["pro_r1"])
            act.activation(out=rsm[0:1, 264:267], in_=E2P[0:1, 0:3], func=AF.Ln,
                           bias=EPS_C[0:1, :])
            i = act.activation(out=RSTDP_R[0:1, 0:3], in_=rsm[0:1, 264:267], func=AF.Exp,
                               scale=-0.5)
            tk["pro_r2"] = ACs.n
        BA(_pro_rows_ac)

        def _pro_rows_dv2(dve):
            dve.wait_ge(ACs.h, tk["pro_r2"])
            i = dve.tensor_scalar(out=rsm[0:1, 268:271], in0=MUP[0:1, 0:3], scalar1=-1.0,
                                  scalar2=None, op0=ALU.mult)
            i = dve.tensor_tensor(out=M2P_R[0:1, 0:3], in0=rsm[0:1, 268:271],
                                  in1=RSTDP_R[0:1, 0:3], op=ALU.mult)
            tk["pro_r3"] = DVs.n
        BV(_pro_rows_dv2)

        def _pro_bc(pe):
            pe.wait_ge(DVs.h, tk["pro_r3"])
            pe.matmul(skip_group_check=True, out=ps[2][:, 0:4], lhsT=ONES_R, rhs=RSTDP_R, start=True, stop=True)
            i = pe.matmul(skip_group_check=True, out=ps[3][:, 0:4], lhsT=ONES_R, rhs=M2P_R, start=True, stop=True)
            tk["pro_bc"] = PEs.n
        BT(_pro_bc)

        def _pro_bce(act):
            act.wait_ge(PEs.h, tk["pro_bc"])
            act.copy(out=bca[:, 0:4], in_=ps[2][:, 0:4])
            i = act.copy(out=bcb[:, 0:4], in_=ps[3][:, 0:4])
            tk["pro_bce"] = ACs.n
        BA(_pro_bce)

        def _pro_ap(dve):
            dve.wait_ge(DVs.h, tk["pro_bce"])
            last = None
            for k in range(ND):
                dve.tensor_tensor(out=tf1[:, 0:3], in0=xlbf[:, k * 3:(k + 1) * 3],
                                  in1=bca[:, 0:3], op=ALU.mult)
                last = dve.tensor_tensor(out=xlnp[:, k * 3:(k + 1) * 3], in0=tf1[:, 0:3],
                                         in1=bcb[:, 0:3], op=ALU.add)
            tk["pro_ap"] = DVs.n
        BV(_pro_ap)

        for t in range(NT):
            p = "t%d_" % t
            q = "t%d_" % (t - 1)
            xfm_c = lambda k, _xb=(x_fm if t % 2 == 0 else x_fm2): _xb[:, k * T:(k + 1) * T]

            if t > 0:
                def _w(sync, t=t):
                    sync.wait_ge(PEs.h, tk[q + "fcmm"])
                    Sw.inc(sync.dma_start(
                        out=wgtA[:].rearrange("p (k f) -> p k f", f=2048),
                        in_=st_win.rearrange("(k p) f -> p k f", p=128)[:, :, 0:2048]), 16)
                    tk[p + "winxp"] = Sw.n
                BS(_w)
            else:
                tk[p + "xload"] = tk["xload_t0"]
                tk[p + "winxp"] = tk["winxp_t0"]

            for g in range(2):
                def _cst(dve, t=t, g=g):
                    dve.wait_ge(Sx.h, tk[p + "xload"])
                    if g == 0 and t > 0:
                        dve.wait_ge(GPs.h, tw(q + "yg%d" % (NE - 1)))
                    if g == 0 and t == 0:
                        dve.wait_ge(DVs.h, tk["pro_ap"])
                        dve.wait_ge(PEs.h, tk["pro_mm"])
                    dve.wait_ge(PEs.h, tw(p + "s1mm%d" % (g - 1)))
                    for k in range(4 * g, 4 * g + 4):
                        i = dve.tensor_copy(out=sq4(k % 4), in_=xfm_c(k))
                    tk[p + "s1c%d" % g] = DVs.n
                BV(_cst)

                def _csq(act, t=t, g=g):
                    act.wait_ge(Sx.h, tk[p + "xload"])
                    if g == 0 and t == 0:
                        act.wait_ge(PEs.h, tk["pro_mm"])
                        act.wait_ge(TSG, 80)
                    act.wait_ge(PEs.h, tw(p + "s1mm%d" % (g - 1)))
                    if g == 0:
                        act.wait_ge(PEs.h, tw(q + "projmm"))
                    for k in range(4 * g, 4 * g + 4):
                        i = act.square(out=cb4(k % 4), in_=xfm_c(k))
                    tk[p + "s1a%d" % g] = ACs.n
                BA(_csq)

                def _mm(pe, t=t, g=g):
                    pe.wait_ge(DVs.h, tk[p + "s1c%d" % g])
                    pe.wait_ge(ACs.h, tk[p + "s1a%d" % g])
                    if g == 0:
                        pe.wait_ge(DVs.h, tw(q + "stt"))
                        if t == 0:
                            pe.wait_ge(DVs.h, tk["pro_r1"])
                    for k in range(4 * g, 4 * g + 4):
                        pe.matmul(skip_group_check=True, out=ps[4][0:1, :], lhsT=ONESC,
                                  rhs=sq4(k % 4), start=(k == 0), stop=(k == ND - 1))
                        i = pe.matmul(skip_group_check=True, out=ps[5][0:1, :], lhsT=ONESC,
                                      rhs=cb4(k % 4), start=(k == 0), stop=(k == ND - 1))
                    tk[p + "s1mm%d" % g] = PEs.n
                BT(_mm)
                if t == 0:
                    qp_drain(4)

            def _r1dv(dve, t=t):
                dve.wait_ge(PEs.h, tk[p + "s1mm%d" % 1])
                dve.tensor_scalar(out=RF(0), in0=ps[4][0:1, :], scalar1=INV_D,
                                  scalar2=None, op0=ALU.mult)
                dve.tensor_scalar(out=RF(1), in0=ps[5][0:1, :], scalar1=INV_D,
                                  scalar2=None, op0=ALU.mult)
                dve.tensor_tensor(out=RF(2), in0=RF(0), in1=RF(0), op=ALU.mult)
                i = dve.tensor_tensor(out=RF(1), in0=RF(1), in1=RF(2), op=ALU.subtract)
                tk[p + "r1dv"] = DVs.n
            BV(_r1dv)

            def _r1ac(act, t=t):
                act.wait_ge(DVs.h, tk[p + "r1dv"])
                act.activation(out=RF(2), in_=RF(1), func=AF.Ln, bias=EPS_C[0:1, :])
                i = act.activation(out=RRR(0), in_=RF(2), func=AF.Exp, scale=-0.5)
                tk[p + "r1ac"] = ACs.n
            BA(_r1ac)

            def _r1dv2(dve, t=t):
                dve.wait_ge(ACs.h, tk[p + "r1ac"])
                dve.tensor_scalar(out=RF(2), in0=RF(0), scalar1=-1.0, scalar2=None,
                                  op0=ALU.mult)
                i = dve.tensor_tensor(out=RRR(1), in0=RF(2), in1=RRR(0), op=ALU.mult)
                tk[p + "r1dv2"] = DVs.n
            BV(_r1dv2)

            def _bc1(pe, t=t):
                pe.wait_ge(DVs.h, tk[p + "r1dv2"])
                pe.wait_ge(DVs.h, tw(q + "bc3e"))
                pe.matmul(skip_group_check=True, out=ps[2][:], lhsT=ONES_R, rhs=RRR(0), start=True, stop=True)
                i = pe.matmul(skip_group_check=True, out=ps[3][:], lhsT=ONES_R, rhs=RRR(1), start=True, stop=True)
                tk[p + "bc1"] = PEs.n
            BT(_bc1)

            def _bc1e(dve, t=t):
                dve.wait_ge(PEs.h, tk[p + "bc1"])
                dve.tensor_copy(out=bca[:], in_=ps[2][:])
                i = dve.tensor_copy(out=bcb[:], in_=ps[3][:])
                tk[p + "bc1e"] = DVs.n
            BV(_bc1e)

            def _ap1(dve, t=t):
                dve.wait_ge(DVs.h, tk[p + "bc1e"])
                dve.wait_ge(PEs.h, tw(q + "fcmm"))
                dve.wait_ge(PEs.h, tw(q + "hs%d" % (NH // 4 - 1)))
                if t == 0:
                    dve.wait_ge(ACs.h, tk["woc%d" % (NE - 1)])
                for k in range(ND):
                    dve.tensor_tensor(out=tf1[:], in0=xfm_c(k), in1=bca[:], op=ALU.mult)
                    i = dve.tensor_tensor(out=xln_c(k), in0=tf1[:], in1=bcb[:],
                                          op=ALU.add)
                    if k == 3:
                        tk[p + "ap1h"] = DVs.n
                tk[p + "ap1"] = DVs.n
            BV(_ap1)

            for g in range(NE // 2 + 1):
                cA, cB = 2 * g, 2 * g + 1

                def _pe(pe, t=t, g=g, cA=cA, cB=cB):
                    if cA < NE:
                        pe.wait_ge(Sw.h, tk[p + "winxp"])
                        pe.wait_ge(DVs.h, tk[p + "ap1h"])
                        pe.wait_ge(DVs.h, tw(p + "xpe%d" % (g - 1)))
                        for c in (cA, cB):
                            if t == 0:
                                pe.wait_ge(DVs.h, tk["pro_ap"])
                                if c < 2:
                                    pe.wait_ge(DVs.h, tk[p + "r1dv"])
                                for k in range(ND):
                                    pe.matmul(skip_group_check=True, out=ps[4 + c % 2][:, 0:3],
                                              lhsT=win_sl(k)[:, c * 128:(c + 1) * 128],
                                              rhs=xlnp[:, k * 3:(k + 1) * 3],
                                              start=(k == 0), stop=(k == ND - 1))
                            for k in range(ND):
                                if k == 4:
                                    pe.wait_ge(DVs.h, tk[p + "ap1"])
                                pe.matmul(skip_group_check=True, out=ps[c % 2][:],
                                          lhsT=win_sl(k)[:, c * 128:(c + 1) * 128],
                                          rhs=xln_c(k), start=(k == 0), stop=(k == ND - 1))
                        tk[p + "xp%d" % g] = PEs.n
                    if g > 0:
                        for cc in (2 * g - 2, 2 * g - 1):
                            pe.wait_ge(DVs.h, tk[p + "xpe%d" % (g - 1)])
                            pe.wait_ge(DVs.h, tk[p + "dg%d" % (g - 1)])
                            if cc < 2:
                                pe.wait_ge(DVs.h, tk[p + "bc1e"])
                            last = None
                            for kk in range(4):
                                last = pe.matmul(skip_group_check=True, out=ps[2 + cc % 2][:],
                                                 lhsT=dg_t(cc, kk),
                                                 rhs=xpb(cc % 2)[:, kk:kk + T],
                                                 start=(kk == 0), stop=(kk == 3))
                        tk[p + "cv%d" % (g - 1)] = PEs.n
                BT(_pe)

                def _dve_e(dve, t=t, g=g, cA=cA, cB=cB):
                    if cA < NE:
                        dve.wait_ge(PEs.h, tk[p + "xp%d" % g])
                        dve.wait_ge(PEs.h, tw(p + "cv%d" % (g - 1)))
                        for c in (cA, cB):
                            if t == 0:
                                dve.tensor_copy(out=cxp[:, c * 3:(c + 1) * 3],
                                                in_=ps[4 + c % 2][:, 0:3])
                            dve.tensor_copy(out=xpb(c % 2)[:, 0:3], in_=cxp[:, c * 3:(c + 1) * 3])
                            dve.tensor_copy(out=xpb(c % 2)[:, 3:T + 3], in_=ps[c % 2][:])
                            i = dve.tensor_copy(out=cxp[:, c * 3:(c + 1) * 3],
                                                in_=xpb(c % 2)[:, T:T + 3])
                        tk[p + "xpe%d" % g] = DVs.n
                        for cc in (cA, cB):
                            for kk in range(4):
                                i = dve.tensor_scalar(out=dg_t(cc, kk), in0=eyef[:],
                                                      scalar1=cwf[:, cc * 4 + kk:cc * 4 + kk + 1],
                                                      scalar2=None, op0=ALU.mult)
                        tk[p + "dg%d" % g] = DVs.n
                BV(_dve_e)

                def _ac(act, t=t, g=g):
                    if g > 0:
                        act.wait_ge(PEs.h, tk[p + "cv%d" % (g - 1)])
                        act.wait_ge(PEs.h, tw(q + "projmm"))
                        if t == 0 and g == 1:
                            act.wait_ge(DVs.h, tk["qF13"])
                        for cc in (2 * g - 2, 2 * g - 1):
                            i = act.activation(out=xs_c(cc), in_=ps[2 + cc % 2][:], func=AF.Silu,
                                               bias=CB_c(cc))
                            tk[p + "silu%d" % cc] = ACs.n
                BA(_ac)
                if t == 0:
                    qp_drain(5)
            tk[p + "xpmm_all"] = tk[p + "xp%d" % (NE // 2 - 1)]

            def _wz(sync, t=t):
                sync.wait_ge(PEs.h, tk[p + "xpmm_all"])
                Sw.inc(sync.dma_start(
                    out=wgtA[:].rearrange("p (k f) -> p k f", f=2048),
                    in_=st_win.rearrange("(k p) f -> p k f", p=128)[:, :, 2048:4096]), 16)
                tk[p + "winz"] = Sw.n
            BS(_wz)

            if t + 1 < NT:
                def _xn(sync, t=t):
                    tn = t + 1
                    if tn == 1:
                        sync.wait_ge(DVs.h, tk["p0_quant_done"])
                    else:
                        sync.wait_ge(DVs.h, tk["t%d_stt" % (tn - 2)])
                    Sx.inc(sync.dma_start(
                        out=(x_fm if tn % 2 == 0 else x_fm2)[:].rearrange(
                            "p (k t) -> p k t", t=T),
                        in_=xT.rearrange("(k p) t -> p k t", p=128)[:, :, tn * T:(tn + 1) * T]), 16)
                    tk["t%d_xload" % tn] = Sx.n
                BS(_xn)

            for g in range(NE // 2):
                cA, cB = 2 * g, 2 * g + 1

                def _pe(pe, t=t, g=g, cA=cA, cB=cB):
                    pe.wait_ge(Sw.h, tk[p + "winz"])
                    if g >= 1:
                        pe.wait_ge(ACs.h, tw(p + "sze%d" % (g - 1)))
                    else:
                        pe.wait_ge(DVs.h, tk[p + "xpe%d" % (NE // 2 - 1)])
                    for c in (cA, cB):
                        for k in range(ND):
                            i = pe.matmul(skip_group_check=True, out=ps[c % 2][:],
                                          lhsT=win_sl(k)[:, c * 128:(c + 1) * 128],
                                          rhs=xln_c(k), start=(k == 0), stop=(k == ND - 1))
                    tk[p + "z%d" % g] = PEs.n
                BT(_pe)

                def _ac(act, t=t, g=g, cA=cA, cB=cB):
                    act.wait_ge(PEs.h, tk[p + "z%d" % g])
                    if g == 0:
                        act.wait_ge(PEs.h, tw(q + "projmm"))
                        if t == 0:
                            act.wait_ge(TSG, 96)
                    for c in (cA, cB):
                        i = act.activation(out=sz_c(c), in_=ps[c % 2][:], func=AF.Sigmoid)
                    tk[p + "sze%d" % g] = ACs.n
                BA(_ac)
                if t == 0:
                    qp_drain(5)

            def _bp0(pe, t=t):
                pe.wait_ge(DVs.h, tk[p + "r1dv"])
                last = None
                for k in range(ND):
                    last = pe.matmul(skip_group_check=True, out=ps[5][0:1, :], lhsT=WINBB(k),
                                     rhs=xln_c(k), start=(k == 0), stop=(k == ND - 1))
                tk[p + "bp0"] = PEs.n
            BT(_bp0)

            if t == 0:
                qp_flush_gamma()

            def _wx(pe, t=t):
                pe.wait_ge(ACs.h, tk[p + "silu%d" % (NE - 1)])
                pe.wait_ge(DVs.h, tw(q + "rows3"))
                if t == 0:
                    pe.wait_ge(DVs.h, tk["thrF"])
                    pe.wait_ge(DVs.h, tk["thrP"])
                if t == 0:
                    pe.wait_ge(DVs.h, tw(p + "xpe0"))
                for c in range(NE):
                    pe.matmul(skip_group_check=True, out=ps[6][0:R, :], lhsT=wxbf[:, c * 192:c * 192 + R],
                              rhs=xs_c(c), start=(c == 0), stop=(c == NE - 1))
                last = None
                for c in range(NE):
                    last = pe.matmul(skip_group_check=True, out=ps[7][:], lhsT=wxbf[:, c * 192 + R:(c + 1) * 192],
                                     rhs=xs_c(c), start=(c == 0), stop=(c == NE - 1))
                tk[p + "wx"] = PEs.n
            BT(_wx)

            def _wxe(dve, t=t):
                dve.wait_ge(PEs.h, tk[p + "wx"])
                dve.tensor_copy(out=drbf[0:R, :], in_=ps[6][0:R, :])
                i = dve.tensor_copy(out=Cbf[:], in_=ps[7][:])
                tk[p + "wxe"] = DVs.n
            BV(_wxe)

            sq13 = lambda c: tb2[:, (c % 2) * T:(c % 2 + 1) * T]

            def emit_s13sq(g13):
                def _sq(dve, t=t, g13=g13):
                    if g13 == 0:
                        dve.wait_ge(PEs.h, tk[p + "s1mm1"])
                    else:
                        dve.wait_ge(PEs.h, tk[p + "sv%d" % (g13 - 1)])
                    for c in (2 * g13, 2 * g13 + 1):
                        i = dve.tensor_tensor(out=sq13(c), in0=xs_c(c), in1=xs_c(c),
                                              op=ALU.mult)
                    tk[p + "vsq%d" % g13] = DVs.n
                BV(_sq)

            def emit_s13sv(g13):
                def _mm(pe, t=t, g13=g13):
                    pe.wait_ge(DVs.h, tk[p + "vsq%d" % g13])
                    if g13 == 0:
                        pe.wait_ge(DVs.h, tk[p + "r11c"])
                    for c in (2 * g13, 2 * g13 + 1):
                        pe.matmul(skip_group_check=True, out=ps[4][0:1, :], lhsT=Dbf_c(c),
                                  rhs=xs_c(c), start=(c == 0), stop=(c == NE - 1))
                        i = pe.matmul(skip_group_check=True, out=ps[5][0:1, :], lhsT=D2bf_c(c),
                                      rhs=sq13(c), start=(c == 0), stop=(c == NE - 1))
                    tk[p + "sv%d" % g13] = PEs.n
                BT(_mm)
            emit_s13sq(0)

            def _d0(pe, t=t):
                pe.wait_ge(DVs.h, tk[p + "wxe"])
                pe.wait_ge(DVs.h, tw(p + "r1dv"))
                i = pe.matmul(skip_group_check=True, out=ps[4][0:1, :], lhsT=WDT0B, rhs=drbf[0:R, :],
                              start=True, stop=True)
                tk[p + "d0"] = PEs.n
            BT(_d0)

            def _r11a(dve, t=t):
                dve.wait_ge(PEs.h, tk[p + "d0"])
                dve.wait_ge(PEs.h, tk[p + "bp0"])
                i = dve.tensor_scalar(out=RF(0), in0=ps[4][0:1, :], scalar1=BD0_C[0:1, :],
                                      scalar2=None, op0=ALU.add)
                tk[p + "r11a"] = DVs.n
            BV(_r11a)

            def _r11b(act, t=t):
                act.wait_ge(DVs.h, tk[p + "r11a"])
                act.activation(out=RF(3), in_=RF(0), func=AF.Abs)
                act.activation(out=RF(2), in_=RF(3), func=AF.Exp, scale=-1.0)
                act.activation(out=RF(3), in_=RF(2), func=AF.Ln, bias=1.0)
                act.activation(out=RF(4), in_=ps[5][0:1, :], func=AF.Abs)
                act.activation(out=RF(2), in_=RF(4), func=AF.Exp, scale=-1.0)
                i = act.activation(out=RF(4), in_=RF(2), func=AF.Ln, bias=1.0)
                tk[p + "r11b"] = ACs.n
            BA(_r11b)

            def _r11c(dve, t=t):
                dve.wait_ge(ACs.h, tk[p + "r11b"])
                dve.tensor_scalar(out=RF(2), in0=RF(0), scalar1=0.0, scalar2=None,
                                  op0=ALU.max)
                dve.tensor_tensor(out=RF(0), in0=RF(2), in1=RF(3), op=ALU.add)
                dve.tensor_scalar(out=RF(0), in0=RF(0), scalar1=1e-6, scalar2=100.0,
                                  op0=ALU.max, op1=ALU.min)
                dve.tensor_scalar(out=RRR(0), in0=RF(0), scalar1=-1.0, scalar2=None,
                                  op0=ALU.mult)
                dve.tensor_scalar(out=RF(2), in0=ps[5][0:1, :], scalar1=0.0, scalar2=None,
                                  op0=ALU.max)
                dve.tensor_tensor(out=RF(2), in0=RF(2), in1=RF(4), op=ALU.add)
                dve.tensor_tensor(out=RF(3), in0=RF(0), in1=RF(2), op=ALU.mult)
                i = dve.tensor_tensor(out=RRR(1), in0=RF(3), in1=mixA[0:1, 0:T],
                                      op=ALU.mult)
                tk[p + "r11c"] = DVs.n
            BV(_r11c)

            emit_s13sv(0)
            for g13 in range(1, 8):
                emit_s13sq(g13)
                emit_s13sv(g13)
                if t == 0:
                    qp_drain(2)

            def _abd(pe, t=t):
                pe.wait_ge(DVs.h, tk[p + "r11c"])
                pe.wait_ge(ACs.h, tk[p + "silu%d" % (NE - 1)])
                pe.matmul(skip_group_check=True, out=ps[2][:], lhsT=AEXP_R, rhs=RRR(0), start=True, stop=True)
                i = pe.matmul(skip_group_check=True, out=ps[3][:], lhsT=ONES_R, rhs=RRR(1), start=True, stop=True)
                tk[p + "abd"] = PEs.n
            BT(_abd)

            def _aexp(act, t=t):
                act.wait_ge(PEs.h, tk[p + "abd"])
                i = act.activation(out=tf1[:], in_=ps[2][:], func=AF.Exp)
                tk[p + "aexp"] = ACs.n
            BA(_aexp)

            def _scan(dve, t=t):
                dve.wait_ge(ACs.h, tk[p + "aexp"])
                dve.tensor_tensor_scan(out=tf2[:], data0=tf1[:], data1=ps[3][:],
                                       initial=carry[:], op0=ALU.mult, op1=ALU.add)
                dve.tensor_copy(out=carry[:], in_=tf2[:, T - 1:T])
                i = dve.tensor_tensor(out=Cbf[:], in0=Cbf[:], in1=tf2[:], op=ALU.mult)
                tk[p + "scan"] = DVs.n
            BV(_scan)

            def _ysc(pe, t=t):
                pe.wait_ge(DVs.h, tk[p + "scan"])
                i = pe.matmul(skip_group_check=True, out=ps[6][0:1, :], lhsT=ONESC, rhs=Cbf[:],
                              start=True, stop=True)
                tk[p + "ysc"] = PEs.n
            BT(_ysc)


            def _r13(dve, t=t):
                dve.wait_ge(PEs.h, tk[p + "sv%d" % 7])
                dve.wait_ge(PEs.h, tk[p + "ysc"])
                dve.tensor_scalar(out=RF(0), in0=ps[4][0:1, :], scalar1=INV_E,
                                  scalar2=None, op0=ALU.mult)
                dve.tensor_scalar(out=RF(1), in0=ps[5][0:1, :], scalar1=INV_E,
                                  scalar2=None, op0=ALU.mult)
                dve.tensor_scalar(out=RF(3), in0=ps[6][0:1, :], scalar1=1.0,
                                  scalar2=None, op0=ALU.mult)
                dve.tensor_tensor(out=RF(4), in0=RF(0), in1=RF(3), op=ALU.add)
                dve.tensor_tensor(out=RF(2), in0=RF(0), in1=RF(0), op=ALU.mult)
                i = dve.tensor_tensor(out=RF(1), in0=RF(1), in1=RF(2), op=ALU.subtract)
                tk[p + "r13"] = DVs.n
            BV(_r13)

            def _r13b(act, t=t):
                act.wait_ge(DVs.h, tk[p + "r13"])
                act.activation(out=RF(2), in_=RF(1), func=AF.Ln, bias=EPS_C[0:1, :])
                i = act.activation(out=RRR(0), in_=RF(2), func=AF.Exp, scale=-0.5)
                tk[p + "r13b"] = ACs.n
            BA(_r13b)

            def _r13c(dve, t=t):
                dve.wait_ge(ACs.h, tk[p + "r13b"])
                dve.tensor_tensor(out=RF(2), in0=RF(3), in1=RF(4), op=ALU.subtract)
                i = dve.tensor_scalar(out=RRR(1), in0=RF(2), scalar1=1.0, scalar2=None,
                                      op0=ALU.mult)
                tk[p + "r13c"] = DVs.n
            BV(_r13c)

            def _bc2(pe, t=t):
                pe.wait_ge(DVs.h, tk[p + "r13c"])
                pe.wait_ge(DVs.h, tk[p + "scan"])
                pe.matmul(skip_group_check=True, out=ps[2][:], lhsT=ONES_R, rhs=RRR(0), start=True, stop=True)
                i = pe.matmul(skip_group_check=True, out=ps[3][:], lhsT=ONES_R, rhs=RRR(1), start=True, stop=True)
                tk[p + "bc2"] = PEs.n
            BT(_bc2)

            def _bc2e(dve, t=t):
                dve.wait_ge(PEs.h, tk[p + "bc2"])
                dve.tensor_copy(out=bca[:], in_=ps[2][:])
                i = dve.tensor_copy(out=bcb[:], in_=ps[3][:])
                tk[p + "bc2e"] = DVs.n
            BV(_bc2e)

            def _wol(sync, t=t):
                sync.wait_ge(PEs.h, tw(q + "fcmm"))
                sync.wait_ge(CC, 5)
                if t == 0:
                    sync.wait_ge(TSG, 16)
                Swo.inc(sync.dma_start(
                    out=wgtB[:].rearrange("p (c two f) -> p c two f", two=2, f=DM),
                    in_=st_wout.rearrange("(c p) (two f) -> p c two f", p=128, two=2)), 16)
                tk[p + "wol"] = Swo.n
            BS(_wol)

            def _fcla(sync, t=t):
                sync.wait_ge(PEs.h, tk[p + "z%d" % (NE // 2 - 1)])
                sync.wait_ge(CC, 3)
                SfA.inc(sync.dma_start(
                    out=wgtA[:].rearrange("p (k f) -> p k f", f=4096),
                    in_=st_fc.rearrange("(k p) f -> p k f", p=128)[:, 0:4, :]), 16)
                tk[p + "fcla"] = SfA.n
            BS(_fcla)

            if t == 0:
                qp_finish()

            for c in range(NE):
                def _yg(dve, t=t, c=c):
                    if c == 0:
                        dve.wait_ge(DVs.h, tk[p + "bc2e"])
                        dve.wait_ge(ACs.h, tk[p + "sze%d" % (NE // 2 - 1)])
                        if t == 0:
                            dve.wait_ge(DVs.h, tk["pnP%d" % (NH - 1)])
                        else:
                            dve.wait_ge(PEs.h, tk[q + "woutmm"])
                    dve.scalar_tensor_tensor(out=tf1[:], in0=xs_c(c), scalar=D_c(c),
                                             in1=bcb[:], op0=ALU.mult, op1=ALU.add)
                    dve.tensor_tensor(out=tf2[:], in0=tf1[:], in1=bca[:], op=ALU.mult)
                    i = dve.tensor_scalar(out=(tb1 if c % 2 == 0 else tb2)[:].bitcast(F32),
                                          in0=tf2[:], scalar1=G_c(c), scalar2=Bb_c(c),
                                          op0=ALU.mult, op1=ALU.add)
                    tk[p + "ygn%d" % c] = DVs.n
                BV(_yg)

                def _ygp(gp, t=t, c=c):
                    gp.wait_ge(DVs.h, tk[p + "ygn%d" % c])
                    i = gp.tensor_tensor(out=yg_c(c),
                                         in0=(tb1 if c % 2 == 0 else tb2)[:].bitcast(F32),
                                         in1=sz_c(c), op=ALU.mult)
                    tk[p + "yg%d" % c] = GPs.n
                BG(_ygp)

                def _mm(pe, t=t, c=c):
                    pe.wait_ge(Swo.h, tk[p + "wol"])
                    pe.wait_ge(GPs.h, tk[p + "yg%d" % c])
                    if c == 0:
                        pe.wait_ge(ACs.h, tk[p + "sze%d" % (NE // 2 - 1)])
                        pe.wait_ge(DVs.h, tk[p + "r13"])
                        pe.wait_ge(DVs.h, tk[p + "scan"])
                    for oc in range(ND):
                        i = pe.matmul(skip_group_check=True, out=ps[oc][:],
                                      lhsT=wout_sl(c)[:, oc * 128:(oc + 1) * 128],
                                      rhs=yg_c(c), start=(c == 0), stop=(c == NE - 1))
                    tk[p + "mixk%d" % c] = PEs.n
                BT(_mm)
            tk[p + "yg"] = tk[p + "yg%d" % (NE - 1)]

            def _x2(dve, t=t):
                dve.wait_ge(PEs.h, tk[p + "mixk%d" % (NE - 1)])
                for oc in range(ND):
                    i = dve.tensor_tensor(out=xfm_c(oc), in0=xfm_c(oc), in1=ps[oc][:],
                                          op=ALU.add)
                    tk[p + "x2_%d" % oc] = DVs.n
            BV(_x2)
            tk[p + "woutmm"] = tk[p + "mixk%d" % (NE - 1)]

            for g in range(2):
                def _cst(dve, t=t, g=g):
                    for k in range(4 * g, 4 * g + 4):
                        dve.wait_ge(DVs.h, tk[p + "x2_%d" % k])
                    dve.wait_ge(PEs.h, tw(p + "s2mm%d" % (g - 1)))
                    for k in range(4 * g, 4 * g + 4):
                        i = dve.tensor_copy(out=sq4(k % 4), in_=xfm_c(k))
                    tk[p + "s2c%d" % g] = DVs.n
                BV(_cst)

                def _csq(act, t=t, g=g):
                    for k in range(4 * g, 4 * g + 4):
                        act.wait_ge(DVs.h, tk[p + "x2_%d" % k])
                    act.wait_ge(PEs.h, tw(p + "s2mm%d" % (g - 1)))
                    for k in range(4 * g, 4 * g + 4):
                        i = act.square(out=cb4(k % 4), in_=xfm_c(k))
                    tk[p + "s2a%d" % g] = ACs.n
                BA(_csq)

                def _mm(pe, t=t, g=g):
                    pe.wait_ge(DVs.h, tk[p + "s2c%d" % g])
                    pe.wait_ge(ACs.h, tk[p + "s2a%d" % g])
                    if g == 0:
                        pe.wait_ge(DVs.h, tk[p + "x2_%d" % (ND - 1)])
                        pe.wait_ge(DVs.h, tk[p + "x2_%d" % (ND - 2)])
                    for k in range(4 * g, 4 * g + 4):
                        pe.matmul(skip_group_check=True, out=ps[4][0:1, :], lhsT=ONESC,
                                  rhs=sq4(k % 4), start=(k == 0), stop=(k == ND - 1))
                        i = pe.matmul(skip_group_check=True, out=ps[5][0:1, :], lhsT=ONESC,
                                      rhs=cb4(k % 4), start=(k == 0), stop=(k == ND - 1))
                    tk[p + "s2mm%d" % g] = PEs.n
                BT(_mm)

            def _r2dv(dve, t=t):
                dve.wait_ge(PEs.h, tk[p + "s2mm%d" % 1])
                dve.tensor_scalar(out=RF(0), in0=ps[4][0:1, :], scalar1=INV_D,
                                  scalar2=None, op0=ALU.mult)
                dve.tensor_scalar(out=RF(1), in0=ps[5][0:1, :], scalar1=INV_D,
                                  scalar2=None, op0=ALU.mult)
                dve.tensor_tensor(out=RF(2), in0=RF(0), in1=RF(0), op=ALU.mult)
                i = dve.tensor_tensor(out=RF(1), in0=RF(1), in1=RF(2), op=ALU.subtract)
                tk[p + "r2dv"] = DVs.n
            BV(_r2dv)

            def _r2ac(act, t=t):
                act.wait_ge(DVs.h, tk[p + "r2dv"])
                act.activation(out=RF(2), in_=RF(1), func=AF.Ln, bias=EPS_C[0:1, :])
                i = act.activation(out=RRR(0), in_=RF(2), func=AF.Exp, scale=-0.5)
                tk[p + "r2ac"] = ACs.n
            BA(_r2ac)

            def _r2dv2(dve, t=t):
                dve.wait_ge(ACs.h, tk[p + "r2ac"])
                dve.tensor_scalar(out=RF(2), in0=RF(0), scalar1=-1.0, scalar2=None,
                                  op0=ALU.mult)
                i = dve.tensor_tensor(out=RRR(1), in0=RF(2), in1=RRR(0), op=ALU.mult)
                tk[p + "r2dv2"] = DVs.n
            BV(_r2dv2)

            def _bc2b(pe, t=t):
                pe.wait_ge(DVs.h, tk[p + "r2dv2"])
                pe.wait_ge(DVs.h, tk[p + "yg"])
                pe.matmul(skip_group_check=True, out=ps[2][:], lhsT=ONES_R, rhs=RRR(0), start=True, stop=True)
                i = pe.matmul(skip_group_check=True, out=ps[3][:], lhsT=ONES_R, rhs=RRR(1), start=True, stop=True)
                tk[p + "bcL2"] = PEs.n
            BT(_bc2b)

            def _bc2be(dve, t=t):
                dve.wait_ge(PEs.h, tk[p + "bcL2"])
                dve.tensor_copy(out=bca[:], in_=ps[2][:])
                i = dve.tensor_copy(out=bcb[:], in_=ps[3][:])
                tk[p + "bcL2e"] = DVs.n
            BV(_bc2be)

            def _ap2(dve, t=t):
                dve.wait_ge(DVs.h, tk[p + "bcL2e"])
                dve.wait_ge(PEs.h, tk[p + "bp0"])
                for k in range(ND):
                    dve.tensor_tensor(out=tf1[:], in0=xfm_c(k), in1=bca[:], op=ALU.mult)
                    i = dve.tensor_tensor(out=xln_c(k), in0=tf1[:], in1=bcb[:],
                                          op=ALU.add)
                    if k == 3:
                        tk[p + "ap2h"] = DVs.n
                tk[p + "ap2"] = DVs.n
            BV(_ap2)


            def _fclb(sync, t=t):
                sync.wait_ge(PEs.h, tk[p + "woutmm"])
                if t == 0:
                    sync.wait_ge(CC, 3)
                Sfc.inc(sync.dma_start(
                    out=wgtB[:].rearrange("p (k f) -> p k f", f=4096),
                    in_=st_fc.rearrange("(k p) f -> p k f", p=128)[:, 4:8, :]), 16)
                tk[p + "fclb"] = Sfc.n
            BS(_fclb)

            sq17 = lambda j: (tb1 if j % 8 < 2 else tb2 if j % 8 < 4 else
                              tf1[:].bitcast(BF16) if j % 8 < 6 else
                              tf2[:].bitcast(BF16))[:, (j % 2) * T:(j % 2 + 1) * T]
            for g in range(NH // 4 + 2):
                ocs = [oc for oc in range(4 * g, 4 * g + 4) if oc < NH]

                def _pe(pe, t=t, g=g, ocs=tuple(ocs)):
                    for oc in ocs:
                        pe.wait_ge(SfA.h, tk[p + "fcla"])
                        pe.wait_ge(DVs.h, tk[p + "ap2h"])
                        pe.wait_ge(ACs.h, tw(p + "gel%d" % (g - 1)))
                        if oc == 0:
                            pe.wait_ge(DVs.h, tk[p + "bcL2e"])
                            pe.wait_ge(DVs.h, tk[p + "yg"])
                        for k in range(ND):
                            if k == ND // 2:
                                pe.wait_ge(Sfc.h, tk[p + "fclb"])
                                pe.wait_ge(DVs.h, tk[p + "ap2"])
                            i = pe.matmul(skip_group_check=True, out=ps[oc % 4][:],
                                          lhsT=fc_lhs(k, oc), rhs=xln_c(k),
                                          start=(k == 0), stop=(k == ND - 1))
                    if ocs:
                        tk[p + "h%d" % g] = PEs.n
                    if g > 1:
                        for j in range(4 * (g - 2), 4 * (g - 1)):
                            pe.wait_ge(DVs.h, tk[p + "hsq%d" % (g - 2)])
                            if j == 0:
                                pe.wait_ge(DVs.h, tk[p + "r13c"])
                                pe.wait_ge(DVs.h, tk[p + "scan"])
                            pe.matmul(skip_group_check=True, out=ps[6][0:1, :], lhsT=ONESC,
                                      rhs=hg_c(j), start=(j == 0), stop=(j == NH - 1))
                            i = pe.matmul(skip_group_check=True, out=ps[7][0:1, :], lhsT=ONESC,
                                          rhs=sq17(j), start=(j == 0), stop=(j == NH - 1))
                        tk[p + "hs%d" % (g - 2)] = PEs.n
                BT(_pe)

                def _ac(act, t=t, g=g, ocs=tuple(ocs)):
                    if ocs:
                        act.wait_ge(PEs.h, tk[p + "h%d" % g])
                        for oc in ocs:
                            i = act.activation(out=hg_c(oc), in_=ps[oc % 4][:], func=AF.Gelu)
                        tk[p + "gel%d" % g] = ACs.n
                BA(_ac)

                def _sqd(dve, t=t, g=g, ocs=tuple(ocs)):
                    if ocs:
                        dve.wait_ge(ACs.h, tk[p + "gel%d" % g])
                        dve.wait_ge(PEs.h, tw(p + "hs%d" % (g - 2)))
                        for oc in ocs:
                            i = dve.tensor_tensor(out=sq17(oc), in0=hg_c(oc),
                                                  in1=hg_c(oc), op=ALU.mult)
                        tk[p + "hsq%d" % g] = DVs.n
                BV(_sqd)
            tk[p + "fcmm"] = tk[p + "h%d" % (NH // 4 - 1)]

            def _r3dv(dve, t=t):
                dve.wait_ge(PEs.h, tk[p + "hs%d" % (NH // 4 - 1)])
                dve.tensor_scalar(out=RF(0), in0=ps[6][0:1, :], scalar1=INV_H,
                                  scalar2=None, op0=ALU.mult)
                dve.tensor_scalar(out=RF(1), in0=ps[7][0:1, :], scalar1=INV_H,
                                  scalar2=None, op0=ALU.mult)
                dve.tensor_tensor(out=RF(2), in0=RF(0), in1=RF(0), op=ALU.mult)
                i = dve.tensor_tensor(out=RF(1), in0=RF(1), in1=RF(2), op=ALU.subtract)
                tk[p + "rows3"] = DVs.n
            BV(_r3dv)

            def _r3ac(act, t=t):
                act.wait_ge(DVs.h, tk[p + "rows3"])
                act.activation(out=RF(2), in_=RF(1), func=AF.Ln, bias=EPS_C[0:1, :])
                i = act.activation(out=RRR(0), in_=RF(2), func=AF.Exp, scale=-0.5)
                tk[p + "r3ac"] = ACs.n
            BA(_r3ac)

            def _r3dv2(dve, t=t):
                dve.wait_ge(ACs.h, tk[p + "r3ac"])
                dve.tensor_scalar(out=RF(2), in0=RF(0), scalar1=-1.0, scalar2=None,
                                  op0=ALU.mult)
                i = dve.tensor_tensor(out=M3_BF, in0=RF(2), in1=RRR(0), op=ALU.mult)
                tk[p + "r3dv2"] = DVs.n
            BV(_r3dv2)

            def _bc3(pe, t=t):
                pe.wait_ge(DVs.h, tk[p + "r3dv2"])
                pe.wait_ge(DVs.h, tk[p + "ap2"])
                i = pe.matmul(skip_group_check=True, out=ps[2][:], lhsT=ONES_R, rhs=RRR(0), start=True, stop=True)
                tk[p + "bc3"] = PEs.n
            BT(_bc3)

            def _bc3e(dve, t=t):
                dve.wait_ge(PEs.h, tk[p + "bc3"])
                i = dve.tensor_copy(out=bca[:], in_=ps[2][:])
                tk[p + "bc3e"] = DVs.n
            BV(_bc3e)

            for gh in range(8):
                def _hs(dve, t=t, gh=gh):
                    if gh == 0:
                        dve.wait_ge(DVs.h, tk[p + "bc3e"])
                    for j in range(4 * gh, 4 * gh + 4):
                        i = dve.scalar_tensor_tensor(out=hg_c(j), in0=hg_c(j),
                                                     scalar=LNW_c(j), in1=bca[:],
                                                     op0=ALU.mult, op1=ALU.mult)
                    tk[p + "hsg%d" % gh] = DVs.n
                BV(_hs)
            tk[p + "hs"] = tk[p + "hsg7"]

            if t == 0:
                for k in range(NH):
                    if k % 2 == 0:
                        def _ld(sync, k=k):
                            sync.wait_ge(PEs.h, tw("ucmm%d" % (k - 3)))
                            if k < 4:
                                sync.wait_ge(CC, 4)
                                sync.wait_ge(TSG, 64)
                            i = sync.dma_start(out=pj_pair(k).rearrange("p (two f) -> p two f", f=DM),
                                               in_=st_proj[(k // 4) * 128:(k // 4 + 1) * 128,
                                                           (k % 4) * DM:(k % 4 + 2) * DM]
                                               .rearrange("p (two f) -> p two f", f=DM))
                            tk["ucl%d" % k] = Spj[(k // 2) % 2].inc(i, 16)
                        BS(_ld)

                    def _mm(pe, k=k):
                        pe.wait_ge(Spj[(k // 2) % 2].h, tk["ucl%d" % (k - k % 2)])
                        if k == 0:
                            pe.wait_ge(ACs.h, tk[p + "gel%d" % (NH // 4 - 1)])
                            pe.wait_ge(DVs.h, tk[p + "bc3e"])
                            pe.wait_ge(DVs.h, tk[p + "bcL2e"])
                            pe.wait_ge(DVs.h, tk[p + "r2dv"])
                            pe.wait_ge(DVs.h, tk[p + "rows3"])
                        last = None
                        for oc in range(ND):
                            last = pe.matmul(skip_group_check=True, out=ps[oc][:, 0:2],
                                             lhsT=pj_sl(k)[:, oc * 128:(oc + 1) * 128],
                                             rhs=LNWB(k), start=(k == 0), stop=(k == NH - 1))
                        tk["ucmm%d" % k] = PEs.n
                    BT(_mm)

                def _uce(dve):
                    dve.wait_ge(PEs.h, tk["ucmm%d" % (NH - 1)])
                    for oc in range(ND):
                        i = dve.tensor_copy(out=uccol[:, oc * 2:oc * 2 + 2],
                                            in_=ps[oc][:, 0:2])
                    tk["uce"] = DVs.n
                BV(_uce)

                def _ucw(gp):
                    gp.wait_ge(DVs.h, tk["uce"])
                    i = gp.dma_start(out=d_uc, in_=uccol[:])
                    tk["ucw"] = Tout.inc(i, 16)
                BG(_ucw)

                def _ucr(gp):
                    gp.wait_ge(Tout.h, tk["ucw"])
                    Sub[0].inc(gp.dma_start(out=ucrow[0:1, 0:1024].rearrange("one (oc j) -> one oc j", j=128),
                               in_=d_uc.rearrange("j (oc two) -> two oc j", two=2)[0:1],
                               allow_slow_non_contiguous=True), 16)
                    Sub[1].inc(gp.dma_start(out=ucrow[0:1, 1024:2048].rearrange("one (oc j) -> one oc j", j=128),
                               in_=d_uc.rearrange("j (oc two) -> two oc j", two=2)[1:2],
                               allow_slow_non_contiguous=True), 16)
                BG(_ucr)

            for k in range(NH):
                if k % 2 == 0:
                    def _ld(sync, t=t, k=k):
                        if k >= 4:
                            sync.wait_ge(PEs.h, tk[p + "pj%d" % (k - 3)])
                        elif t == 0:
                            sync.wait_ge(PEs.h, tk["ucmm%d" % (NH - 1)])
                        else:
                            sync.wait_ge(PEs.h, tk[q + "pj%d" % (NH - 1)])
                        i = sync.dma_start(out=pj_pair(k).rearrange("p (two f) -> p two f", f=DM),
                                           in_=st_proj[(k // 4) * 128:(k // 4 + 1) * 128,
                                                       (k % 4) * DM:(k % 4 + 2) * DM]
                                           .rearrange("p (two f) -> p two f", f=DM))
                        tk[p + "pjl%d" % k] = Spj[(k // 2) % 2].inc(i, 16)
                    BS(_ld)

                def _mm(pe, t=t, k=k):
                    pe.wait_ge(Spj[(k // 2) % 2].h, tk[p + "pjl%d" % (k - k % 2)])
                    pe.wait_ge(DVs.h, tk[p + "hsg%d" % (k // 4)])
                    if k == 0:
                        pe.wait_ge(ACs.h, tk[p + "gel%d" % (NH // 4 - 1)])
                        pe.wait_ge(DVs.h, tk[p + "bc3e"])
                        pe.wait_ge(DVs.h, tk[p + "r2dv"])
                        pe.wait_ge(DVs.h, tk[p + "rows3"])
                        if t == 0:
                            pe.wait_ge(DVs.h, tk["uce"])
                    last = None
                    for oc in range(ND):
                        last = pe.matmul(skip_group_check=True, out=ps[oc][:],
                                         lhsT=pj_sl(k)[:, oc * 128:(oc + 1) * 128],
                                         rhs=hg_c(k), start=(k == 0), stop=False)
                    tk[p + "pj%d" % k] = PEs.n
                BT(_mm)

            def _rk1(pe, t=t):
                pe.wait_ge(DVs.h, tk[p + "r3dv2"])
                if t == 0:
                    pe.wait_ge(Sub[0].h, 16)
                    pe.wait_ge(Sub[1].h, 16)
                last = None
                for oc in range(ND):
                    pe.matmul(skip_group_check=True, out=ps[oc][:], lhsT=U_ROW(oc), rhs=M3_BF,
                              start=False, stop=False)
                    last = pe.matmul(skip_group_check=True, out=ps[oc][:], lhsT=C_ROW(oc), rhs=ONEST_BF,
                                     start=False, stop=True)
                tk[p + "rk1"] = PEs.n
            BT(_rk1)

            def _stt(dve, t=t):
                dve.wait_ge(PEs.h, tk[p + "rk1"])
                dve.wait_ge(Tout.h, tw(q + "out"))
                if t == 0:
                    dve.wait_ge(DVs.h, tk["pnP%d" % (NH - 1)])
                last = None
                for oc in range(ND):
                    last = dve.scalar_tensor_tensor(out=outs_c(oc), in0=ps[oc][:],
                                                    scalar=1.0, in1=xfm_c(oc),
                                                    op0=ALU.mult, op1=ALU.add)
                tk[p + "stt"] = DVs.n
            BV(_stt)

            def _out(gp, t=t):
                gp.wait_ge(DVs.h, tk[p + "stt"])
                i = gp.dma_start(
                    out=yT.rearrange("(oc p) t -> p oc t", p=128)[:, :, t * T:(t + 1) * T],
                    in_=out_s[:].rearrange("p (oc t) -> p oc t", t=T))
                Tout.inc(i, 16)
                tk[p + "out"] = Tout.n
            BG(_out)

        def _fin(gp):
            gp.wait_ge(Tout.h, Tout.n)
        BG(_fin)

    return nc


_NC_CACHE = {}


def _get_program():
    if "nc" not in _NC_CACHE:
        _NC_CACHE["nc"] = build_program()
    return _NC_CACHE["nc"]


def kernel(L0_attn_x, L0_attn_Win, L0_attn_conv_w, L0_attn_conv_b, L0_attn_Wx,
           L0_attn_Wdt, L0_attn_bdt, L0_attn_A_log, L0_attn_D, L0_attn_ln_g,
           L0_attn_ln_b, L0_attn_Wout, L0_attn_res_scale, L0_ffn_W_fc, L0_ffn_ln_w,
           L0_ffn_ln_b, L0_ffn_W_proj):
    f32 = np.float32
    x = np.asarray(L0_attn_x, f32)[0]
    xTf = np.ascontiguousarray(x.T)
    win = np.asarray(L0_attn_Win, f32)
    winx = np.ascontiguousarray(win[:, 0:4096])
    winbc = np.ascontiguousarray(win[:, 4096:4097])
    convw = np.ascontiguousarray(np.asarray(L0_attn_conv_w, f32))
    convb = np.ascontiguousarray(np.asarray(L0_attn_conv_b, f32))
    wxw = np.ascontiguousarray(np.asarray(L0_attn_Wx, f32))
    wdt0 = np.ascontiguousarray(np.asarray(L0_attn_Wdt, f32)[:, 0:1])
    bdt0c = np.ascontiguousarray(np.broadcast_to(np.asarray(L0_attn_bdt, f32)[0:1],
                                                 (128,))).reshape(128, 1).copy()
    a0row = np.ascontiguousarray(np.asarray(L0_attn_A_log, f32)[0:1, :])
    dvec = np.ascontiguousarray(np.asarray(L0_attn_D, f32))
    lngv = np.ascontiguousarray(np.asarray(L0_attn_ln_g, f32))
    lnbv = np.ascontiguousarray(np.asarray(L0_attn_ln_b, f32))
    woutw = np.ascontiguousarray(np.asarray(L0_attn_Wout, f32))
    rs = np.asarray(L0_attn_res_scale, f32).reshape(-1)[0:1]
    rsc128 = np.ascontiguousarray(np.broadcast_to(rs, (128,))).reshape(128, 1).copy()
    fcT = np.ascontiguousarray(np.asarray(L0_ffn_W_fc, f32).T)
    lnw3v = np.ascontiguousarray(np.asarray(L0_ffn_ln_w, f32))
    lnb3v = np.ascontiguousarray(np.asarray(L0_ffn_ln_b, f32))
    projT = np.ascontiguousarray(np.asarray(L0_ffn_W_proj, f32).T)
    eyem = np.eye(128, dtype=f32)

    in_maps = []
    for c in range(NCORES):
        lo = c * TOK
        xs = np.ascontiguousarray(xTf[:, lo:lo + TOK])
        if c == 0:
            xlb = np.zeros((DM, 3), f32)
        else:
            xlb = np.ascontiguousarray(xTf[:, lo - 3:lo])
        in_maps.append(dict(
            xT=xs, xLB=xlb, winbc=winbc, convw=convw, convb=convb,
            wxw=wxw, wdt0=wdt0, bdt0c=bdt0c, a0row=a0row, dvec=dvec, lngv=lngv,
            lnbv=lnbv, rsc128=rsc128, lnw3v=lnw3v, lnb3v=lnb3v, eye=eyem,
            winshard=np.ascontiguousarray(winx[c * 128:(c + 1) * 128, :]),
            fcshard=np.ascontiguousarray(fcT[c * 128:(c + 1) * 128, :]),
            pjshard=np.ascontiguousarray(
                np.concatenate([projT[(4 * c + j) * 128:(4 * c + j + 1) * 128, :]
                                for j in range(4)], axis=1)),
            woshard=np.ascontiguousarray(
                np.concatenate([woutw[(2 * c + j) * 128:(2 * c + j + 1) * 128, :]
                                for j in range(2)], axis=1)),
        ))

    nc = _get_program()
    res = run_bass_kernel_spmd(nc, in_maps, core_ids=list(range(NCORES)))
    out = np.empty((DM, NCORES * TOK), f32)
    for c in range(NCORES):
        out[:, c * TOK:(c + 1) * TOK] = res.results[c]["yT"]
    return np.ascontiguousarray(out.T).reshape(1, NCORES * TOK, DM)


# version 38
# speedup vs baseline: 1.0090x; 1.0090x over previous
import contextlib
import numpy as np
import ml_dtypes
import concourse.bass as bass
import concourse.mybir as mybir
from concourse.bass_isa import ReduceOp
from concourse.bass_utils import run_bass_kernel_spmd

F32 = mybir.dt.float32
F32R = mybir.dt.float32r
BF16 = mybir.dt.bfloat16
AF = mybir.ActivationFunctionType
ALU = mybir.AluOpType
AX = mybir.AxisListType

NCORES = 8
TOK = 2048
T = 512
NT = TOK // T
DM = 1024
ND = DM // 128
E = 2048
NE = E // 128
HF = 4096
NH = HF // 128
R = 64
INV_D = 1.0 / DM
INV_E = 1.0 / E
INV_H = 1.0 / HF


class Sem:
    def __init__(self, h):
        self.h = h
        self.n = 0

    def inc(self, instr, amt=1):
        instr.then_inc(self.h, amt)
        self.n += amt
        return self.n




_OPMETA = {
    "tensor_scalar": (("out",), ("in0", "scalar1", "scalar2"), False),
    "scalar_tensor_tensor": (("out",), ("in0", "scalar", "in1"), False),
    "tensor_tensor": (("out",), ("in0", "in1"), False),
    "tensor_tensor_scan": (("out",), ("data0", "data1", "initial"), False),
    "tensor_copy": (("out",), ("in_",), False),
    "reduce_sum": (("out",), ("in_",), False),
    "tensor_reduce": (("out",), ("in_",), False),
    "reciprocal": (("out",), ("in_",), False),
    "activation": (("out", "accum_out"), ("in_", "bias", "scale"), False),
    "copy": (("out",), ("in_",), False),
    "square": (("out",), ("in_",), False),
    "sign": (("out",), ("in_",), False),
    "mul": (("out",), ("in_",), False),
    "add": (("out",), ("in_",), False),
    "sqrt": (("out",), ("in_",), False),
    "matmul": (("out",), ("lhsT", "rhs"), True),
    "transpose": (("out",), ("in_", "identity"), True),
}
_BLK = 512


def _regions(ap):
    try:
        name = ap.tensor.name
        off = ap.offset
        span = max(1, ap.free_size())
    except AttributeError:
        return ()
    return tuple((name, b) for b in range(off // _BLK, (off + span - 1) // _BLK + 1))


class Tracker:
    def __init__(self):
        self.w = {}
        self.r = {}

    def read_edges(self, ap):
        out = []
        for rg in _regions(ap):
            rec = self.w.get(rg)
            if rec is not None:
                out.append((rec[0], rec[1]))
        return out

    def write_edges(self, ap, is_mm):
        out = []
        for rg in _regions(ap):
            rec = self.w.get(rg)
            if rec is not None and not (is_mm and rec[2]):
                out.append((rec[0], rec[1]))
            for e in self.r.get(rg, ()):
                out.append(e)
        return out

    def note_read(self, ap, sem, ticket):
        for rg in _regions(ap):
            self.r.setdefault(rg, []).append((sem, ticket))

    def note_write(self, ap, sem, ticket, is_mm):
        for rg in _regions(ap):
            self.w[rg] = (sem, ticket, is_mm)
            self.r[rg] = []

    def note_dma_write(self, ap, sem, ticket):
        self.note_write(ap, sem, ticket, False)

    def note_dma_read(self, ap, sem, ticket):
        self.note_read(ap, sem, ticket)


class Wrap:

    def __init__(self, eng, sem, tracker, wm):
        self._e = eng
        self._s = sem
        self._t = tracker
        self._wm = wm

    def wait_ge(self, semh, val):
        k = id(semh)
        if self._wm.get(k, -1) >= val:
            return
        self._e.wait_ge(semh, val)
        self._wm[k] = val

    def _edge(self, sem, t):
        self.wait_ge(sem.h, t)

    def __getattr__(self, m):
        if m not in _OPMETA:
            return getattr(self._e, m)
        outs, ins, is_mm = _OPMETA[m]
        raw = getattr(self._e, m)
        tr = self._t

        def emit(**kw):
            for k in ins:
                v = kw.get(k)
                if v is not None and not isinstance(v, (int, float)):
                    for sem, t in tr.read_edges(v):
                        self._edge(sem, t)
            for k in outs:
                v = kw.get(k)
                if v is not None and not isinstance(v, (int, float)):
                    for sem, t in tr.write_edges(v, is_mm):
                        self._edge(sem, t)
            instr = raw(**kw)
            instr.then_inc(self._s.h, 1)
            self._s.n += 1
            tick = self._s.n
            for k in ins:
                v = kw.get(k)
                if v is not None and not isinstance(v, (int, float)):
                    tr.note_read(v, self._s, tick)
            for k in outs:
                v = kw.get(k)
                if v is not None and not isinstance(v, (int, float)):
                    tr.note_write(v, self._s, tick, is_mm)
            return instr
        return emit

    def memset(self, ap, const):
        for sem, t in self._t.write_edges(ap, False):
            self._edge(sem, t)
        instr = self._e.memset(ap, const)
        instr.then_inc(self._s.h, 1)
        self._s.n += 1
        self._t.note_write(ap, self._s, self._s.n, False)
        return instr


def build_program():
    nc = bass.Bass("TRN2", target_bir_lowering=False, debug=False)
    ap = lambda *a, **k: nc.dram_tensor(*a, **k).ap()

    xT = ap("xT", [DM, TOK], F32, kind="ExternalInput")
    xLB = ap("xLB", [DM, 3], F32, kind="ExternalInput")
    winshard = ap("winshard", [128, 4096], F32, kind="ExternalInput")
    winbc = ap("winbc", [DM, 1], F32, kind="ExternalInput")
    convw = ap("convw", [E, 4], F32, kind="ExternalInput")
    convb = ap("convb", [E], F32, kind="ExternalInput")
    wxw = ap("wxw", [E, 192], F32, kind="ExternalInput")
    wdt0 = ap("wdt0", [R, 1], F32, kind="ExternalInput")
    bdt0c = ap("bdt0c", [128, 1], F32, kind="ExternalInput")
    a0row = ap("a0row", [1, 128], F32, kind="ExternalInput")
    dvec = ap("dvec", [E], F32, kind="ExternalInput")
    lngv = ap("lngv", [E], F32, kind="ExternalInput")
    lnbv = ap("lnbv", [E], F32, kind="ExternalInput")
    woshard = ap("woshard", [128, 2048], F32, kind="ExternalInput")
    rsc128 = ap("rsc128", [128, 1], F32, kind="ExternalInput")
    fcshard = ap("fcshard", [128, HF], F32, kind="ExternalInput")
    lnw3v = ap("lnw3v", [HF], F32, kind="ExternalInput")
    lnb3v = ap("lnb3v", [HF], F32, kind="ExternalInput")
    pjshard = ap("pjshard", [128, 4096], F32, kind="ExternalInput")
    eye = ap("eye", [128, 128], F32, kind="ExternalInput")
    yT = ap("yT", [DM, TOK], F32, kind="ExternalOutput")

    st_win = ap("st_win", [ND * 128, 4096], BF16, kind="Internal", addr_space="Shared")
    st_wout = ap("st_wout", [ND * 128, 2048], BF16, kind="Internal", addr_space="Shared")
    st_fc = ap("st_fc", [ND * 128, HF], BF16, kind="Internal", addr_space="Shared")
    st_proj = ap("st_proj", [ND * 128, 4096], BF16, kind="Internal", addr_space="Shared")
    ccw = ap("ccw", [128, 4096], BF16, kind="Internal")
    ccf = ap("ccf", [128, 4096], BF16, kind="Internal")
    ccp = ap("ccp", [128, 4096], BF16, kind="Internal")
    cco = ap("cco", [128, 2048], BF16, kind="Internal")
    grin = ap("grin", [128, 2], F32, kind="Internal")
    grout = ap("grout", [128, 2], F32, kind="Internal", addr_space="Shared")
    d_uc = ap("d_uc", [128, 16], BF16, kind="Internal")

    ctx = contextlib.ExitStack()
    EN = ctx.enter_context
    with ctx:
        x_fm = EN(nc.sbuf_tensor("x_fm", [128, ND * T], F32))
        x_fm2 = EN(nc.sbuf_tensor("x_fm2", [128, ND * T], F32))
        xln = EN(nc.sbuf_tensor("xln", [128, ND * T], BF16))
        xp_s = EN(nc.sbuf_tensor("xp_s", [128, 2 * (T + 3)], BF16))
        cxp = EN(nc.sbuf_tensor("cxp", [128, NE * 3], BF16))
        mixA = EN(nc.sbuf_tensor("mixA", [128, 16 * T], BF16))
        mixB = EN(nc.sbuf_tensor("mixB", [128, 16 * T], BF16))
        ygb = EN(nc.sbuf_tensor("ygb", [128, 4 * T], BF16))
        wgtA = EN(nc.sbuf_tensor("wgtA", [128, 16384], BF16))
        wgtB = EN(nc.sbuf_tensor("wgtB", [128, 16384], BF16))
        pjb = EN(nc.sbuf_tensor("pjb", [128, 4 * DM], BF16))
        out_s = EN(nc.sbuf_tensor("out_s", [128, ND * T], F32))
        tf1 = EN(nc.sbuf_tensor("tf1", [128, T], F32))
        tf2 = EN(nc.sbuf_tensor("tf2", [128, T], F32))
        tb1 = EN(nc.sbuf_tensor("tb1", [128, 2 * T], BF16))
        tb2 = EN(nc.sbuf_tensor("tb2", [128, 2 * T], BF16))
        carry = EN(nc.sbuf_tensor("carry", [128, 1], F32))
        dgb = EN(nc.sbuf_tensor("dgb", [128, 2 * 4 * 128], BF16))
        bca = EN(nc.sbuf_tensor("bca", [128, T], F32))
        bcb = EN(nc.sbuf_tensor("bcb", [128, T], F32))
        colf = EN(nc.sbuf_tensor("colf", [128, 96], F32))
        colb = EN(nc.sbuf_tensor("colb", [128, 128], BF16))
        colf2 = EN(nc.sbuf_tensor("colf2", [128, 8], F32))
        cwf = EN(nc.sbuf_tensor("cwf", [128, 64], F32))
        wxbf = EN(nc.sbuf_tensor("wxbf", [128, 16 * 192], BF16))
        lnb3c = EN(nc.sbuf_tensor("lnb3c", [128, 32], F32))
        rwf = EN(nc.sbuf_tensor("rwf", [1, 3 * T], F32))
        rwr = EN(nc.sbuf_tensor("rwr", [1, 2 * T], F32R))
        rsm = EN(nc.sbuf_tensor("rsm", [1, 320], F32))
        rsr = EN(nc.sbuf_tensor("rsr", [1, 288], F32R))
        rbf = EN(nc.sbuf_tensor("rbf", [1, 1024], BF16))
        ucrow = EN(nc.sbuf_tensor("ucrow", [1, 2048], BF16))
        gac = EN(nc.sbuf_tensor("gac", [128, 40], F32))
        gacr = EN(nc.sbuf_tensor("gacr", [128, 32], F32R))
        ones128r = EN(nc.sbuf_tensor("ones128r", [128, 128], F32R))
        eyef = EN(nc.sbuf_tensor("eyef", [128, 128], F32))
        xlbf = EN(nc.sbuf_tensor("xlbf", [128, ND * 3], F32))
        xlbb = EN(nc.sbuf_tensor("xlbb", [128, ND * 3], BF16))
        xlnp = EN(nc.sbuf_tensor("xlnp", [128, ND * 3], BF16))
        uccol = EN(nc.sbuf_tensor("uccol", [128, 16], BF16))

        ps = [EN(nc.psum_tensor("psb%d" % i, [128, T], F32)) for i in range(8)]

        L1 = Sem(EN(nc.semaphore("L1")))
        L2 = [Sem(EN(nc.semaphore("L2%d" % i))) for i in range(2)]
        LF = [Sem(EN(nc.semaphore("LF%d" % i))) for i in range(4)]
        LP = [Sem(EN(nc.semaphore("LP%d" % i))) for i in range(4)]
        Lwo = [Sem(EN(nc.semaphore("Lwo%d" % i))) for i in range(2)]
        Sx = Sem(EN(nc.semaphore("Sx")))
        Sw = Sem(EN(nc.semaphore("Sw")))
        Swo = Sem(EN(nc.semaphore("Swo")))
        Sfc = Sem(EN(nc.semaphore("Sfc")))
        SfA = Sem(EN(nc.semaphore("SfA")))
        Spj = [Sem(EN(nc.semaphore("Spj%d" % i))) for i in range(2)]
        Tw = [Sem(EN(nc.semaphore("Tw%d" % i))) for i in range(2)]
        Two = [Sem(EN(nc.semaphore("Two%d" % i))) for i in range(2)]
        TqF = [Sem(EN(nc.semaphore("TqF%d" % i))) for i in range(2)]
        TqP = [Sem(EN(nc.semaphore("TqP%d" % i))) for i in range(2)]
        Tout = Sem(EN(nc.semaphore("Tout")))
        Sub = [Sem(EN(nc.semaphore("Sub%d" % i))) for i in range(2)]
        Lsh = [Sem(EN(nc.semaphore("Lsh%d" % i))) for i in range(4)]
        TSG = EN(nc.semaphore("TSG"))
        CC = EN(nc.semaphore("CC"))
        LPQ = [Sem(EN(nc.semaphore("LPQ%d" % i))) for i in range(8)]
        PEs = Sem(EN(nc.semaphore("PEs")))
        ACs = Sem(EN(nc.semaphore("ACs")))
        DVs = Sem(EN(nc.semaphore("DVs")))
        GPs = Sem(EN(nc.semaphore("GPs")))

        block = EN(nc.Block())

        TRK = Tracker()
        WM = {"PE": {}, "AC": {}, "DV": {}, "GP": {}, "SP": {}}

        def BT(f):
            block.tensor(lambda e, f=f: f(Wrap(e, PEs, TRK, WM["PE"])))

        def BA(f):
            block.scalar(lambda e, f=f: f(Wrap(e, ACs, TRK, WM["AC"])))

        def BV(f):
            block.vector(lambda e, f=f: f(Wrap(e, DVs, TRK, WM["DV"])))

        def BG(f):
            block.gpsimd(lambda e, f=f: f(Wrap(e, GPs, TRK, WM["GP"])))

        def BS(f):
            block.sync(lambda e, f=f: f(Wrap(e, GPs, TRK, WM["SP"])))

        P0Q = []

        def drain(n):
            for _ in range(n):
                if P0Q:
                    P0Q.pop(0)()

        def drain_all():
            drain(len(P0Q))

        D_c = lambda c: colf[:, c:c + 1]
        G_c = lambda c: colf[:, 16 + c:16 + c + 1]
        Bb_c = lambda c: colf[:, 32 + c:32 + c + 1]
        CB_c = lambda c: colf[:, 48 + c:48 + c + 1]
        LNW_c = lambda k: colf[:, 64 + k:64 + k + 1]
        Dbf_c = lambda c: colb[:, c:c + 1]
        D2bf_c = lambda c: colb[:, 16 + c:16 + c + 1]
        LNWB = lambda k: colb[:, 32 + 2 * k:32 + 2 * k + 2]
        WDT0B = colb[0:R, 96:97]
        WINBB = lambda k: colb[:, 97 + k:97 + k + 1]
        ONESC = colb[:, 105:106]
        THRF, NEGF, THRP, NEGP, RS_C, BD0_C, EPS_C = (colf2[:, i:i + 1] for i in range(7))
        WDT0F = gac[0:R, 32:33]
        ONES_R = rsr[0:1, 0:128]
        AEXP_R = rsr[0:1, 128:256]
        RSTDP_R = rsr[0:1, 256:260]
        M2P_R = rsr[0:1, 260:264]
        A0_S = rsm[0:1, 128:256]
        MUP = rsm[0:1, 256:260]
        E2P = rsm[0:1, 260:264]
        ONEST_BF = rbf[0:1, 0:T]
        M3_BF = rbf[0:1, T:2 * T]
        U_ROW = lambda oc: ucrow[0:1, oc * 128:(oc + 1) * 128]
        C_ROW = lambda oc: ucrow[0:1, 1024 + oc * 128:1024 + (oc + 1) * 128]

        RF = lambda i: rwf[0:1, i * T:(i + 1) * T] if i < 3 else \
            (bca if i == 3 else bcb)[0:1, :]
        RRR = lambda i: rwr[0:1, i * T:(i + 1) * T]

        xs_c = lambda c: mixA[:, c * T:(c + 1) * T]
        sz_c = lambda c: mixB[:, c * T:(c + 1) * T]
        hg_c = lambda j: (mixA if j < 16 else mixB)[:, (j % 16) * T:((j % 16) + 1) * T]
        yg_c = lambda c: ygb[:, (c % 4) * T:(c % 4 + 1) * T]
        xfm_c = lambda k: x_fm[:, k * T:(k + 1) * T]
        xln_c = lambda k: xln[:, k * T:(k + 1) * T]
        outs_c = lambda k: out_s[:, k * T:(k + 1) * T]
        xpb = lambda p: xp_s[:, p * (T + 3):(p + 1) * (T + 3)]
        dg_t = lambda c, k: dgb[:, ((c % 2) * 4 + k) * 128:((c % 2) * 4 + k + 1) * 128]
        win_sl = lambda k: wgtA[:, k * 2048:(k + 1) * 2048]
        wout_sl = lambda k: wgtB[:, k * DM:(k + 1) * DM]
        fc_lhs = lambda k, oc: (wgtA if k < 4 else wgtB)[:, (k % 4) * 4096 + oc * 128:
                                                         (k % 4) * 4096 + (oc + 1) * 128]
        pj_sl = lambda k: pjb[:, (k % 4) * DM:((k % 4) + 1) * DM]
        pj_pair = lambda k: pjb[:, (k % 4) * DM:((k % 4) + 2) * DM]
        stgf = out_s[:, 0:4096]
        stgf_h = lambda p: out_s[:, p * 2048:(p + 1) * 2048]
        stg_bf = wgtB[:, 0:4096]
        stgb_h = lambda p: wgtB[:, p * 2048:(p + 1) * 2048]
        stgb_q = lambda p: wgtB[:, p * 1024:(p + 1) * 1024]
        sq_b = lambda p: tb1[:, p * T:(p + 1) * T]
        Cbf = tb1[:, 0:T]
        drbf = tb1[:, T:2 * T]
        cb_b = lambda p: tb2[:, p * T:(p + 1) * T]
        sq4 = lambda j: (tb1 if j < 2 else tb2)[:, (j % 2) * T:(j % 2 + 1) * T]
        cb4 = lambda j: mixB[:, j * T:(j + 1) * T]

        tk = {}

        def tw(key, default=0):
            return tk.get(key, default)

        def _p0v(sync):
            last = None
            for dst, src, slow in (
                (eyef[:], eye, False),
                (colf[:, 0:16], dvec.rearrange("(c p) -> p c", p=128), True),
                (colf[:, 16:32], lngv.rearrange("(c p) -> p c", p=128), True),
                (colf[:, 32:48], lnbv.rearrange("(c p) -> p c", p=128), True),
                (colf[:, 48:64], convb.rearrange("(c p) -> p c", p=128), True),
                (colf[:, 64:96], lnw3v.rearrange("(c p) -> p c", p=128), True),
                (lnb3c[:], lnb3v.rearrange("(c p) -> p c", p=128), True),
                (WDT0F, wdt0, True),
                (BD0_C, bdt0c, True),
                (RS_C, rsc128, True),
                (rsm[0:1, 128:256], a0row, False),
                (gac[:, 24:32].rearrange("p (k one) -> p k one", one=1),
                 winbc.rearrange("(k p) one -> p k one", p=128), True),
                (x_fm[:, 0:64].rearrange("p (c k) -> p c k", k=4),
                 convw.rearrange("(c p) k -> p c k", p=128), True),
                (x_fm[:, 64:64 + 16 * 192].rearrange("p (c r) -> p c r", r=192),
                 wxw.rearrange("(c p) r -> p c r", p=128), True),
                (xlbf[:].rearrange("p (k c) -> p k c", c=3),
                 xLB.rearrange("(k p) c -> p k c", p=128), True),
            ):
                L1.inc(sync.dma_start(out=dst, in_=src, allow_slow_non_contiguous=slow), 16)
            tk["p0v"] = L1.n
        BS(_p0v)

        def _p0c_dv(dve):
            dve.wait_ge(L1.h, tk["p0v"])
            dve.memset(EPS_C, 1e-5)
            dve.memset(ONEST_BF, 1.0)
            dve.memset(ONESC, 1.0)
            dve.memset(carry[:], 0.0)
            dve.memset(cxp[:], 0.0)
            dve.tensor_scalar(out=ONES_R, in0=eyef[0:1, 0:128], scalar1=0.0, scalar2=1.0,
                              op0=ALU.mult, op1=ALU.add)
            dve.tensor_scalar(out=ones128r[:], in0=eyef[:], scalar1=0.0, scalar2=1.0,
                              op0=ALU.mult, op1=ALU.add)
            dve.tensor_scalar(out=rsr[0:1, 256:264], in0=eyef[0:1, 0:8], scalar1=0.0,
                              scalar2=None, op0=ALU.mult)
            dve.tensor_copy(out=cwf[:], in_=x_fm[:, 0:64])
            dve.tensor_copy(out=wxbf[:], in_=x_fm[:, 64:64 + 16 * 192])
            dve.tensor_copy(out=colb[:, 0:16], in_=colf[:, 0:16])
            dve.tensor_copy(out=WDT0B, in_=WDT0F)
            dve.tensor_copy(out=colb[:, 97:105], in_=gac[:, 24:32])
            dve.tensor_copy(out=colb[:, 32:96:2], in_=colf[:, 64:96])
            i = dve.tensor_copy(out=colb[:, 33:96:2], in_=lnb3c[:])
            tk["p0cdv"] = DVs.n
        BV(_p0c_dv)

        def _p0c_ac(act):
            act.wait_ge(DVs.h, tk["p0cdv"])
            act.square(out=colb[:, 16:32], in_=colf[:, 0:16])
            i = act.activation(out=AEXP_R, in_=A0_S, func=AF.Exp)
            i = act.copy(out=xlbb[:], in_=xlbf[:])
            tk["p0cac"] = ACs.n
        BA(_p0c_ac)

        xlnF = xln[:].bitcast(F32)
        pnP_ = mixA[:, 0:4096]
        pnN_ = mixA[:, 4096:8192]
        RG = [list(range(NCORES))]

        def _shld(sync):
            sync.wait_ge(DVs.h, tk["p0cdv"])
            Lsh[0].inc(sync.dma_start(out=x_fm[:, 0:4096], in_=winshard), 16)
            Lsh[1].inc(sync.dma_start(out=out_s[:, 0:4096], in_=fcshard), 16)
            Lsh[2].inc(sync.dma_start(out=x_fm2[:, 0:4096], in_=pjshard), 16)
            Lsh[3].inc(sync.dma_start(out=xlnF[:, 0:2048], in_=woshard), 16)
        BS(_shld)

        def _wcast(act):
            act.wait_ge(Lsh[0].h, 16)
            i = act.copy(out=wgtB[:, 0:4096], in_=x_fm[:, 0:4096])
            tk["wincast"] = ACs.n
        BA(_wcast)

        def _x0(sync):
            sync.wait_ge(ACs.h, tk["wincast"])
            Sx.inc(sync.dma_start(
                out=x_fm[:].rearrange("p (k t) -> p k t", t=T),
                in_=xT.rearrange("(k p) t -> p k t", p=128)[:, :, 0:T]), 16)
            tk["xload_t0"] = Sx.n
        BS(_x0)

        def _gpart(dve):
            dve.wait_ge(Lsh[1].h, 16)
            dve.reduce_sum(out=gac[:, 0:1], in_=out_s[:, 0:4096], axis=AX.X,
                           apply_absolute_value=True)
            dve.wait_ge(Lsh[2].h, 16)
            i = dve.reduce_sum(out=gac[:, 1:2], in_=x_fm2[:, 0:4096], axis=AX.X,
                               apply_absolute_value=True)
            tk["gpart"] = DVs.n
        BV(_gpart)

        def _cc1(gp):
            gp.wait_ge(ACs.h, tk["wincast"])
            gp.dma_start(out=ccw, in_=wgtB[:, 0:4096]).then_inc(TSG, 16)
            gp.wait_ge(TSG, 16)
            gp.collective_compute("AllGather", ALU.bypass, replica_groups=RG,
                                  ins=[ccw.opt()], outs=[st_win.opt()]).then_inc(CC, 1)
            gp.wait_ge(DVs.h, tk["gpart"])
            gp.dma_start(out=grin, in_=gac[:, 0:2]).then_inc(TSG, 16)
            gp.wait_ge(TSG, 32)
            gp.collective_compute("AllReduce", ALU.add, replica_groups=RG,
                                  ins=[grin.opt()], outs=[grout.opt()]).then_inc(CC, 1)
            gp.wait_ge(CC, 2)
            gp.dma_start(out=gac[:, 4:6], in_=grout).then_inc(TSG, 16)
        BG(_cc1)

        def _gcp(dve):
            dve.wait_ge(TSG, 48)
            i = dve.tensor_scalar(out=gacr[:, 0:2], in0=gac[:, 4:6], scalar1=1.0,
                                  scalar2=None, op0=ALU.mult)
            tk["gcp"] = DVs.n
        BV(_gcp)

        def _gmm(pe):
            pe.wait_ge(DVs.h, tk["gcp"])
            i = pe.matmul(skip_group_check=True, out=ps[6][:, 0:2], lhsT=ones128r[:],
                          rhs=gacr[:, 0:2], start=True, stop=True)
            tk["gmm"] = PEs.n
        BT(_gmm)

        def _thr(dve):
            dve.wait_ge(PEs.h, tk["gmm"])
            dve.tensor_scalar(out=THRF, in0=ps[6][:, 0:1], scalar1=0.5 / 4194304.0,
                              scalar2=5e-7, op0=ALU.mult, op1=ALU.add)
            dve.tensor_scalar(out=NEGF, in0=ps[6][:, 0:1], scalar1=-0.5 / 4194304.0,
                              scalar2=-5e-7, op0=ALU.mult, op1=ALU.add)
            dve.tensor_scalar(out=THRP, in0=ps[6][:, 1:2], scalar1=0.5 / 4194304.0,
                              scalar2=5e-7, op0=ALU.mult, op1=ALU.add)
            i = dve.tensor_scalar(out=NEGP, in0=ps[6][:, 1:2], scalar1=-0.5 / 4194304.0,
                                  scalar2=-5e-7, op0=ALU.mult, op1=ALU.add)
            tk["thrF"] = DVs.n
            tk["thrP"] = DVs.n
        BV(_thr)

        def _qF(dve):
            dve.tensor_scalar(out=pnP_, in0=out_s[:, 0:4096], scalar1=THRF,
                              scalar2=None, op0=ALU.is_gt)
            dve.tensor_scalar(out=pnN_, in0=out_s[:, 0:4096], scalar1=NEGF,
                              scalar2=None, op0=ALU.is_lt)
            i = dve.tensor_tensor(out=pjb[:, 0:4096], in0=pnP_, in1=pnN_,
                                  op=ALU.subtract)
            tk["qFsub"] = DVs.n
        BV(_qF)

        def _qP(dve):
            dve.tensor_scalar(out=pnP_, in0=x_fm2[:, 0:4096], scalar1=THRP,
                              scalar2=None, op0=ALU.is_gt)
            dve.tensor_scalar(out=pnN_, in0=x_fm2[:, 0:4096], scalar1=NEGP,
                              scalar2=None, op0=ALU.is_lt)
            i = dve.tensor_tensor(out=mixB[:, 0:4096], in0=pnP_, in1=pnN_,
                                  op=ALU.subtract)
            tk["qPsub"] = DVs.n
        BV(_qP)

        def _wocast(act):
            act.wait_ge(Lsh[3].h, 16)
            act.wait_ge(L1.h, tk["p0v"])
            i = act.activation(out=mixB[:, 4096:6144], in_=xlnF[:, 0:2048],
                               func=AF.Identity, scale=RS_C)
            tk["wocast"] = ACs.n
        BA(_wocast)

        def _cc2(gp):
            gp.wait_ge(DVs.h, tk["qFsub"])
            gp.dma_start(out=ccf, in_=pjb[:, 0:4096]).then_inc(TSG, 16)
            gp.wait_ge(TSG, 64)
            gp.collective_compute("AllGather", ALU.bypass, replica_groups=RG,
                                  ins=[ccf.opt()], outs=[st_fc.opt()]).then_inc(CC, 1)
            gp.wait_ge(DVs.h, tk["qPsub"])
            gp.dma_start(out=ccp, in_=mixB[:, 0:4096]).then_inc(TSG, 16)
            gp.wait_ge(TSG, 80)
            gp.collective_compute("AllGather", ALU.bypass, replica_groups=RG,
                                  ins=[ccp.opt()], outs=[st_proj.opt()]).then_inc(CC, 1)
            gp.wait_ge(ACs.h, tk["wocast"])
            gp.dma_start(out=cco, in_=mixB[:, 4096:6144]).then_inc(TSG, 16)
            gp.wait_ge(TSG, 96)
            gp.collective_compute("AllGather", ALU.bypass, replica_groups=RG,
                                  ins=[cco.opt()], outs=[st_wout.opt()]).then_inc(CC, 1)
        BG(_cc2)

        def _w0(sync):
            sync.wait_ge(CC, 1)
            Sw.inc(sync.dma_start(
                out=wgtA[:].rearrange("p (k f) -> p k f", f=2048),
                in_=st_win.rearrange("(k p) f -> p k f", p=128)[:, :, 0:2048]), 16)
            tk["winxp_t0"] = Sw.n
        BS(_w0)

        tk["p0_quant_done"] = tk["qPsub"]
        tk["pnP%d" % (NH - 1)] = tk["qPsub"]
        tk["qF13"] = tk["qPsub"]
        tk["woc%d" % (NE - 1)] = tk["wocast"]

        def qp_drain(n):
            pass

        def qp_flush_gamma():
            pass

        def qp_finish():
            pass

        def _pro_dv(dve):
            dve.wait_ge(L1.h, tk["p0v"])
            i = dve.tensor_tensor(out=tb1[:, 0:ND * 3], in0=xlbf[:], in1=xlbf[:],
                                  op=ALU.mult)
            tk["pro_sq"] = DVs.n
        BV(_pro_dv)

        def _pro_pe(pe):
            pe.wait_ge(DVs.h, tk["pro_sq"])
            for k in range(ND):
                pe.matmul(skip_group_check=True, out=ps[4][0:1, 0:3], lhsT=ONESC, rhs=xlbb[:, k * 3:(k + 1) * 3],
                          start=(k == 0), stop=(k == ND - 1))
            last = None
            for k in range(ND):
                last = pe.matmul(skip_group_check=True, out=ps[5][0:1, 0:3], lhsT=ONESC,
                                 rhs=tb1[:, k * 3:(k + 1) * 3],
                                 start=(k == 0), stop=(k == ND - 1))
            tk["pro_mm"] = PEs.n
        BT(_pro_pe)

        def _pro_rows_dv(dve):
            dve.wait_ge(PEs.h, tk["pro_mm"])
            dve.tensor_scalar(out=MUP[0:1, 0:3], in0=ps[4][0:1, 0:3], scalar1=INV_D,
                              scalar2=None, op0=ALU.mult)
            dve.tensor_scalar(out=E2P[0:1, 0:3], in0=ps[5][0:1, 0:3], scalar1=INV_D,
                              scalar2=None, op0=ALU.mult)
            dve.tensor_tensor(out=rsm[0:1, 264:267], in0=MUP[0:1, 0:3], in1=MUP[0:1, 0:3],
                              op=ALU.mult)
            i = dve.tensor_tensor(out=E2P[0:1, 0:3], in0=E2P[0:1, 0:3],
                                  in1=rsm[0:1, 264:267], op=ALU.subtract)
            tk["pro_r1"] = DVs.n
        BV(_pro_rows_dv)

        def _pro_rows_ac(act):
            act.wait_ge(DVs.h, tk["pro_r1"])
            act.activation(out=rsm[0:1, 264:267], in_=E2P[0:1, 0:3], func=AF.Ln,
                           bias=EPS_C[0:1, :])
            i = act.activation(out=RSTDP_R[0:1, 0:3], in_=rsm[0:1, 264:267], func=AF.Exp,
                               scale=-0.5)
            tk["pro_r2"] = ACs.n
        BA(_pro_rows_ac)

        def _pro_rows_dv2(dve):
            dve.wait_ge(ACs.h, tk["pro_r2"])
            i = dve.tensor_scalar(out=rsm[0:1, 268:271], in0=MUP[0:1, 0:3], scalar1=-1.0,
                                  scalar2=None, op0=ALU.mult)
            i = dve.tensor_tensor(out=M2P_R[0:1, 0:3], in0=rsm[0:1, 268:271],
                                  in1=RSTDP_R[0:1, 0:3], op=ALU.mult)
            tk["pro_r3"] = DVs.n
        BV(_pro_rows_dv2)

        def _pro_bc(pe):
            pe.wait_ge(DVs.h, tk["pro_r3"])
            pe.matmul(skip_group_check=True, out=ps[2][:, 0:4], lhsT=ONES_R, rhs=RSTDP_R, start=True, stop=True)
            i = pe.matmul(skip_group_check=True, out=ps[3][:, 0:4], lhsT=ONES_R, rhs=M2P_R, start=True, stop=True)
            tk["pro_bc"] = PEs.n
        BT(_pro_bc)

        def _pro_bce(act):
            act.wait_ge(PEs.h, tk["pro_bc"])
            act.copy(out=bca[:, 0:4], in_=ps[2][:, 0:4])
            i = act.copy(out=bcb[:, 0:4], in_=ps[3][:, 0:4])
            tk["pro_bce"] = ACs.n
        BA(_pro_bce)

        def _pro_ap(dve):
            dve.wait_ge(DVs.h, tk["pro_bce"])
            last = None
            for k in range(ND):
                dve.tensor_tensor(out=tf1[:, 0:3], in0=xlbf[:, k * 3:(k + 1) * 3],
                                  in1=bca[:, 0:3], op=ALU.mult)
                last = dve.tensor_tensor(out=xlnp[:, k * 3:(k + 1) * 3], in0=tf1[:, 0:3],
                                         in1=bcb[:, 0:3], op=ALU.add)
            tk["pro_ap"] = DVs.n
        BV(_pro_ap)

        for t in range(NT):
            p = "t%d_" % t
            q = "t%d_" % (t - 1)
            xfm_c = lambda k, _xb=(x_fm if t % 2 == 0 else x_fm2): _xb[:, k * T:(k + 1) * T]

            if t > 0:
                def _w(sync, t=t):
                    sync.wait_ge(PEs.h, tk[q + "fcmm"])
                    Sw.inc(sync.dma_start(
                        out=wgtA[:].rearrange("p (k f) -> p k f", f=2048),
                        in_=st_win.rearrange("(k p) f -> p k f", p=128)[:, :, 0:2048]), 16)
                    tk[p + "winxp"] = Sw.n
                BS(_w)
            else:
                tk[p + "xload"] = tk["xload_t0"]
                tk[p + "winxp"] = tk["winxp_t0"]

            for g in range(2):
                def _cst(dve, t=t, g=g):
                    dve.wait_ge(Sx.h, tk[p + "xload"])
                    if g == 0 and t > 0:
                        dve.wait_ge(GPs.h, tw(q + "yg%d" % (NE - 1)))
                    if g == 0 and t == 0:
                        dve.wait_ge(DVs.h, tk["pro_ap"])
                        dve.wait_ge(PEs.h, tk["pro_mm"])
                    dve.wait_ge(PEs.h, tw(p + "s1mm%d" % (g - 1)))
                    for k in range(4 * g, 4 * g + 4):
                        i = dve.tensor_copy(out=sq4(k % 4), in_=xfm_c(k))
                    tk[p + "s1c%d" % g] = DVs.n
                BV(_cst)

                def _csq(act, t=t, g=g):
                    act.wait_ge(Sx.h, tk[p + "xload"])
                    if g == 0 and t == 0:
                        act.wait_ge(PEs.h, tk["pro_mm"])
                        act.wait_ge(TSG, 80)
                    act.wait_ge(PEs.h, tw(p + "s1mm%d" % (g - 1)))
                    if g == 0:
                        act.wait_ge(PEs.h, tw(q + "projmm"))
                    for k in range(4 * g, 4 * g + 4):
                        i = act.square(out=cb4(k % 4), in_=xfm_c(k))
                    tk[p + "s1a%d" % g] = ACs.n
                BA(_csq)

                def _mm(pe, t=t, g=g):
                    pe.wait_ge(DVs.h, tk[p + "s1c%d" % g])
                    pe.wait_ge(ACs.h, tk[p + "s1a%d" % g])
                    if g == 0:
                        pe.wait_ge(DVs.h, tw(q + "stt"))
                        if t == 0:
                            pe.wait_ge(DVs.h, tk["pro_r1"])
                    for k in range(4 * g, 4 * g + 4):
                        pe.matmul(skip_group_check=True, out=ps[4][0:1, :], lhsT=ONESC,
                                  rhs=sq4(k % 4), start=(k == 0), stop=(k == ND - 1))
                        i = pe.matmul(skip_group_check=True, out=ps[5][0:1, :], lhsT=ONESC,
                                      rhs=cb4(k % 4), start=(k == 0), stop=(k == ND - 1))
                    tk[p + "s1mm%d" % g] = PEs.n
                BT(_mm)
                if t == 0:
                    qp_drain(4)

            def _r1dv(dve, t=t):
                dve.wait_ge(PEs.h, tk[p + "s1mm%d" % 1])
                dve.tensor_scalar(out=RF(0), in0=ps[4][0:1, :], scalar1=INV_D,
                                  scalar2=None, op0=ALU.mult)
                dve.tensor_scalar(out=RF(1), in0=ps[5][0:1, :], scalar1=INV_D,
                                  scalar2=None, op0=ALU.mult)
                dve.tensor_tensor(out=RF(2), in0=RF(0), in1=RF(0), op=ALU.mult)
                i = dve.tensor_tensor(out=RF(1), in0=RF(1), in1=RF(2), op=ALU.subtract)
                tk[p + "r1dv"] = DVs.n
            BV(_r1dv)

            def _r1ac(act, t=t):
                act.wait_ge(DVs.h, tk[p + "r1dv"])
                act.activation(out=RF(2), in_=RF(1), func=AF.Ln, bias=EPS_C[0:1, :])
                i = act.activation(out=RRR(0), in_=RF(2), func=AF.Exp, scale=-0.5)
                tk[p + "r1ac"] = ACs.n
            BA(_r1ac)

            def _r1dv2(dve, t=t):
                dve.wait_ge(ACs.h, tk[p + "r1ac"])
                dve.tensor_scalar(out=RF(2), in0=RF(0), scalar1=-1.0, scalar2=None,
                                  op0=ALU.mult)
                i = dve.tensor_tensor(out=RRR(1), in0=RF(2), in1=RRR(0), op=ALU.mult)
                tk[p + "r1dv2"] = DVs.n
            BV(_r1dv2)

            def _bc1(pe, t=t):
                pe.wait_ge(DVs.h, tk[p + "r1dv2"])
                pe.wait_ge(DVs.h, tw(q + "bc3e"))
                pe.matmul(skip_group_check=True, out=ps[2][:], lhsT=ONES_R, rhs=RRR(0), start=True, stop=True)
                i = pe.matmul(skip_group_check=True, out=ps[3][:], lhsT=ONES_R, rhs=RRR(1), start=True, stop=True)
                tk[p + "bc1"] = PEs.n
            BT(_bc1)

            def _bc1e(dve, t=t):
                dve.wait_ge(PEs.h, tk[p + "bc1"])
                dve.tensor_copy(out=bca[:], in_=ps[2][:])
                i = dve.tensor_copy(out=bcb[:], in_=ps[3][:])
                tk[p + "bc1e"] = DVs.n
            BV(_bc1e)

            def _ap1(dve, t=t):
                dve.wait_ge(DVs.h, tk[p + "bc1e"])
                dve.wait_ge(PEs.h, tw(q + "fcmm"))
                dve.wait_ge(PEs.h, tw(q + "hs%d" % (NH // 4 - 1)))
                if t == 0:
                    dve.wait_ge(ACs.h, tk["woc%d" % (NE - 1)])
                for k in range(ND):
                    dve.tensor_tensor(out=tf1[:], in0=xfm_c(k), in1=bca[:], op=ALU.mult)
                    i = dve.tensor_tensor(out=xln_c(k), in0=tf1[:], in1=bcb[:],
                                          op=ALU.add)
                    if k == 3:
                        tk[p + "ap1h"] = DVs.n
                tk[p + "ap1"] = DVs.n
            BV(_ap1)

            for g in range(NE // 2 + 1):
                cA, cB = 2 * g, 2 * g + 1

                def _pe(pe, t=t, g=g, cA=cA, cB=cB):
                    if cA < NE:
                        pe.wait_ge(Sw.h, tk[p + "winxp"])
                        pe.wait_ge(DVs.h, tk[p + "ap1h"])
                        pe.wait_ge(DVs.h, tw(p + "xpe%d" % (g - 1)))
                        for c in (cA, cB):
                            if t == 0:
                                pe.wait_ge(DVs.h, tk["pro_ap"])
                                if c < 2:
                                    pe.wait_ge(DVs.h, tk[p + "r1dv"])
                                for k in range(ND):
                                    pe.matmul(skip_group_check=True, out=ps[4 + c % 2][:, 0:3],
                                              lhsT=win_sl(k)[:, c * 128:(c + 1) * 128],
                                              rhs=xlnp[:, k * 3:(k + 1) * 3],
                                              start=(k == 0), stop=(k == ND - 1))
                            for k in range(ND):
                                if k == 4:
                                    pe.wait_ge(DVs.h, tk[p + "ap1"])
                                pe.matmul(skip_group_check=True, out=ps[c % 2][:],
                                          lhsT=win_sl(k)[:, c * 128:(c + 1) * 128],
                                          rhs=xln_c(k), start=(k == 0), stop=(k == ND - 1))
                        tk[p + "xp%d" % g] = PEs.n
                    if g > 0:
                        for cc in (2 * g - 2, 2 * g - 1):
                            pe.wait_ge(DVs.h, tk[p + "xpe%d" % (g - 1)])
                            pe.wait_ge(DVs.h, tk[p + "dg%d" % (g - 1)])
                            if cc < 2:
                                pe.wait_ge(DVs.h, tk[p + "bc1e"])
                            last = None
                            for kk in range(4):
                                last = pe.matmul(skip_group_check=True, out=ps[2 + cc % 2][:],
                                                 lhsT=dg_t(cc, kk),
                                                 rhs=xpb(cc % 2)[:, kk:kk + T],
                                                 start=(kk == 0), stop=(kk == 3))
                        tk[p + "cv%d" % (g - 1)] = PEs.n
                BT(_pe)

                def _dve_e(dve, t=t, g=g, cA=cA, cB=cB):
                    if cA < NE:
                        dve.wait_ge(PEs.h, tk[p + "xp%d" % g])
                        dve.wait_ge(PEs.h, tw(p + "cv%d" % (g - 1)))
                        for c in (cA, cB):
                            if t == 0:
                                dve.tensor_copy(out=cxp[:, c * 3:(c + 1) * 3],
                                                in_=ps[4 + c % 2][:, 0:3])
                            dve.tensor_copy(out=xpb(c % 2)[:, 0:3], in_=cxp[:, c * 3:(c + 1) * 3])
                            dve.tensor_copy(out=xpb(c % 2)[:, 3:T + 3], in_=ps[c % 2][:])
                            i = dve.tensor_copy(out=cxp[:, c * 3:(c + 1) * 3],
                                                in_=xpb(c % 2)[:, T:T + 3])
                        tk[p + "xpe%d" % g] = DVs.n
                        for cc in (cA, cB):
                            for kk in range(4):
                                i = dve.tensor_scalar(out=dg_t(cc, kk), in0=eyef[:],
                                                      scalar1=cwf[:, cc * 4 + kk:cc * 4 + kk + 1],
                                                      scalar2=None, op0=ALU.mult)
                        tk[p + "dg%d" % g] = DVs.n
                BV(_dve_e)

                def _ac(act, t=t, g=g):
                    if g > 0:
                        act.wait_ge(PEs.h, tk[p + "cv%d" % (g - 1)])
                        act.wait_ge(PEs.h, tw(q + "projmm"))
                        if t == 0 and g == 1:
                            act.wait_ge(DVs.h, tk["qF13"])
                        for cc in (2 * g - 2, 2 * g - 1):
                            i = act.activation(out=xs_c(cc), in_=ps[2 + cc % 2][:], func=AF.Silu,
                                               bias=CB_c(cc))
                            tk[p + "silu%d" % cc] = ACs.n
                BA(_ac)
                if t == 0:
                    qp_drain(5)
            tk[p + "xpmm_all"] = tk[p + "xp%d" % (NE // 2 - 1)]

            def _wz(sync, t=t):
                sync.wait_ge(PEs.h, tk[p + "xpmm_all"])
                Sw.inc(sync.dma_start(
                    out=wgtA[:].rearrange("p (k f) -> p k f", f=2048),
                    in_=st_win.rearrange("(k p) f -> p k f", p=128)[:, :, 2048:4096]), 16)
                tk[p + "winz"] = Sw.n
            BS(_wz)

            if t + 1 < NT:
                def _xn(sync, t=t):
                    tn = t + 1
                    if tn == 1:
                        sync.wait_ge(DVs.h, tk["p0_quant_done"])
                    else:
                        sync.wait_ge(DVs.h, tk["t%d_stt" % (tn - 2)])
                    Sx.inc(sync.dma_start(
                        out=(x_fm if tn % 2 == 0 else x_fm2)[:].rearrange(
                            "p (k t) -> p k t", t=T),
                        in_=xT.rearrange("(k p) t -> p k t", p=128)[:, :, tn * T:(tn + 1) * T]), 16)
                    tk["t%d_xload" % tn] = Sx.n
                BS(_xn)

            for g in range(NE // 2):
                cA, cB = 2 * g, 2 * g + 1

                def _pe(pe, t=t, g=g, cA=cA, cB=cB):
                    pe.wait_ge(Sw.h, tk[p + "winz"])
                    if g >= 1:
                        pe.wait_ge(ACs.h, tw(p + "sze%d" % (g - 1)))
                    else:
                        pe.wait_ge(DVs.h, tk[p + "xpe%d" % (NE // 2 - 1)])
                    for c in (cA, cB):
                        for k in range(ND):
                            i = pe.matmul(skip_group_check=True, out=ps[c % 2][:],
                                          lhsT=win_sl(k)[:, c * 128:(c + 1) * 128],
                                          rhs=xln_c(k), start=(k == 0), stop=(k == ND - 1))
                    tk[p + "z%d" % g] = PEs.n
                BT(_pe)

                def _ac(act, t=t, g=g, cA=cA, cB=cB):
                    act.wait_ge(PEs.h, tk[p + "z%d" % g])
                    if g == 0:
                        act.wait_ge(PEs.h, tw(q + "projmm"))
                        if t == 0:
                            act.wait_ge(TSG, 96)
                    for c in (cA, cB):
                        i = act.activation(out=sz_c(c), in_=ps[c % 2][:], func=AF.Sigmoid)
                    tk[p + "sze%d" % g] = ACs.n
                BA(_ac)
                if t == 0:
                    qp_drain(5)

            def _bp0(pe, t=t):
                pe.wait_ge(DVs.h, tk[p + "r1dv"])
                last = None
                for k in range(ND):
                    last = pe.matmul(skip_group_check=True, out=ps[5][0:1, :], lhsT=WINBB(k),
                                     rhs=xln_c(k), start=(k == 0), stop=(k == ND - 1))
                tk[p + "bp0"] = PEs.n
            BT(_bp0)

            if t == 0:
                qp_flush_gamma()

            def _wx(pe, t=t):
                pe.wait_ge(ACs.h, tk[p + "silu%d" % (NE - 1)])
                pe.wait_ge(DVs.h, tw(q + "rows3"))
                if t == 0:
                    pe.wait_ge(DVs.h, tk["thrF"])
                    pe.wait_ge(DVs.h, tk["thrP"])
                if t == 0:
                    pe.wait_ge(DVs.h, tw(p + "xpe0"))
                for c in range(NE):
                    pe.matmul(skip_group_check=True, out=ps[6][0:R, :], lhsT=wxbf[:, c * 192:c * 192 + R],
                              rhs=xs_c(c), start=(c == 0), stop=(c == NE - 1))
                last = None
                for c in range(NE):
                    last = pe.matmul(skip_group_check=True, out=ps[7][:], lhsT=wxbf[:, c * 192 + R:(c + 1) * 192],
                                     rhs=xs_c(c), start=(c == 0), stop=(c == NE - 1))
                tk[p + "wx"] = PEs.n
            BT(_wx)

            def _wxe(dve, t=t):
                dve.wait_ge(PEs.h, tk[p + "wx"])
                dve.tensor_copy(out=drbf[0:R, :], in_=ps[6][0:R, :])
                i = dve.tensor_copy(out=Cbf[:], in_=ps[7][:])
                tk[p + "wxe"] = DVs.n
            BV(_wxe)

            sq13 = lambda c: tb2[:, (c % 2) * T:(c % 2 + 1) * T]

            def emit_s13sq(g13):
                def _sq(dve, t=t, g13=g13):
                    if g13 == 0:
                        dve.wait_ge(PEs.h, tk[p + "s1mm1"])
                    else:
                        dve.wait_ge(PEs.h, tk[p + "sv%d" % (g13 - 1)])
                    for c in (2 * g13, 2 * g13 + 1):
                        i = dve.tensor_tensor(out=sq13(c), in0=xs_c(c), in1=xs_c(c),
                                              op=ALU.mult)
                    tk[p + "vsq%d" % g13] = DVs.n
                BV(_sq)

            def emit_s13sv(g13):
                def _mm(pe, t=t, g13=g13):
                    pe.wait_ge(DVs.h, tk[p + "vsq%d" % g13])
                    if g13 == 0:
                        pe.wait_ge(DVs.h, tk[p + "r11c"])
                    for c in (2 * g13, 2 * g13 + 1):
                        pe.matmul(skip_group_check=True, out=ps[4][0:1, :], lhsT=Dbf_c(c),
                                  rhs=xs_c(c), start=(c == 0), stop=(c == NE - 1))
                        i = pe.matmul(skip_group_check=True, out=ps[5][0:1, :], lhsT=D2bf_c(c),
                                      rhs=sq13(c), start=(c == 0), stop=(c == NE - 1))
                    tk[p + "sv%d" % g13] = PEs.n
                BT(_mm)
            emit_s13sq(0)

            def _d0(pe, t=t):
                pe.wait_ge(DVs.h, tk[p + "wxe"])
                pe.wait_ge(DVs.h, tw(p + "r1dv"))
                i = pe.matmul(skip_group_check=True, out=ps[4][0:1, :], lhsT=WDT0B, rhs=drbf[0:R, :],
                              start=True, stop=True)
                tk[p + "d0"] = PEs.n
            BT(_d0)

            def _r11a(dve, t=t):
                dve.wait_ge(PEs.h, tk[p + "d0"])
                dve.wait_ge(PEs.h, tk[p + "bp0"])
                i = dve.tensor_scalar(out=RF(0), in0=ps[4][0:1, :], scalar1=BD0_C[0:1, :],
                                      scalar2=None, op0=ALU.add)
                tk[p + "r11a"] = DVs.n
            BV(_r11a)

            def _r11b(act, t=t):
                act.wait_ge(DVs.h, tk[p + "r11a"])
                act.activation(out=RF(3), in_=RF(0), func=AF.Abs)
                act.activation(out=RF(2), in_=RF(3), func=AF.Exp, scale=-1.0)
                act.activation(out=RF(3), in_=RF(2), func=AF.Ln, bias=1.0)
                act.activation(out=RF(4), in_=ps[5][0:1, :], func=AF.Abs)
                act.activation(out=RF(2), in_=RF(4), func=AF.Exp, scale=-1.0)
                i = act.activation(out=RF(4), in_=RF(2), func=AF.Ln, bias=1.0)
                tk[p + "r11b"] = ACs.n
            BA(_r11b)

            def _r11c(dve, t=t):
                dve.wait_ge(ACs.h, tk[p + "r11b"])
                dve.tensor_scalar(out=RF(2), in0=RF(0), scalar1=0.0, scalar2=None,
                                  op0=ALU.max)
                dve.tensor_tensor(out=RF(0), in0=RF(2), in1=RF(3), op=ALU.add)
                dve.tensor_scalar(out=RF(0), in0=RF(0), scalar1=1e-6, scalar2=100.0,
                                  op0=ALU.max, op1=ALU.min)
                dve.tensor_scalar(out=RRR(0), in0=RF(0), scalar1=-1.0, scalar2=None,
                                  op0=ALU.mult)
                dve.tensor_scalar(out=RF(2), in0=ps[5][0:1, :], scalar1=0.0, scalar2=None,
                                  op0=ALU.max)
                dve.tensor_tensor(out=RF(2), in0=RF(2), in1=RF(4), op=ALU.add)
                dve.tensor_tensor(out=RF(3), in0=RF(0), in1=RF(2), op=ALU.mult)
                i = dve.tensor_tensor(out=RRR(1), in0=RF(3), in1=mixA[0:1, 0:T],
                                      op=ALU.mult)
                tk[p + "r11c"] = DVs.n
            BV(_r11c)

            emit_s13sv(0)
            for g13 in range(1, 8):
                emit_s13sq(g13)
                emit_s13sv(g13)
                if t == 0:
                    qp_drain(2)

            def _abd(pe, t=t):
                pe.wait_ge(DVs.h, tk[p + "r11c"])
                pe.wait_ge(ACs.h, tk[p + "silu%d" % (NE - 1)])
                pe.matmul(skip_group_check=True, out=ps[2][:], lhsT=AEXP_R, rhs=RRR(0), start=True, stop=True)
                i = pe.matmul(skip_group_check=True, out=ps[3][:], lhsT=ONES_R, rhs=RRR(1), start=True, stop=True)
                tk[p + "abd"] = PEs.n
            BT(_abd)

            def _aexp(act, t=t):
                act.wait_ge(PEs.h, tk[p + "abd"])
                i = act.activation(out=tf1[:], in_=ps[2][:], func=AF.Exp)
                tk[p + "aexp"] = ACs.n
            BA(_aexp)

            def _scan(dve, t=t):
                dve.wait_ge(ACs.h, tk[p + "aexp"])
                dve.tensor_tensor_scan(out=tf2[:], data0=tf1[:], data1=ps[3][:],
                                       initial=carry[:], op0=ALU.mult, op1=ALU.add)
                dve.tensor_copy(out=carry[:], in_=tf2[:, T - 1:T])
                i = dve.tensor_tensor(out=Cbf[:], in0=Cbf[:], in1=tf2[:], op=ALU.mult)
                tk[p + "scan"] = DVs.n
            BV(_scan)

            def _ysc(pe, t=t):
                pe.wait_ge(DVs.h, tk[p + "scan"])
                i = pe.matmul(skip_group_check=True, out=ps[6][0:1, :], lhsT=ONESC, rhs=Cbf[:],
                              start=True, stop=True)
                tk[p + "ysc"] = PEs.n
            BT(_ysc)


            def _r13(dve, t=t):
                dve.wait_ge(PEs.h, tk[p + "sv%d" % 7])
                dve.wait_ge(PEs.h, tk[p + "ysc"])
                dve.tensor_scalar(out=RF(0), in0=ps[4][0:1, :], scalar1=INV_E,
                                  scalar2=None, op0=ALU.mult)
                dve.tensor_scalar(out=RF(1), in0=ps[5][0:1, :], scalar1=INV_E,
                                  scalar2=None, op0=ALU.mult)
                dve.tensor_scalar(out=RF(3), in0=ps[6][0:1, :], scalar1=1.0,
                                  scalar2=None, op0=ALU.mult)
                dve.tensor_tensor(out=RF(4), in0=RF(0), in1=RF(3), op=ALU.add)
                dve.tensor_tensor(out=RF(2), in0=RF(0), in1=RF(0), op=ALU.mult)
                i = dve.tensor_tensor(out=RF(1), in0=RF(1), in1=RF(2), op=ALU.subtract)
                tk[p + "r13"] = DVs.n
            BV(_r13)

            def _r13b(act, t=t):
                act.wait_ge(DVs.h, tk[p + "r13"])
                act.activation(out=RF(2), in_=RF(1), func=AF.Ln, bias=EPS_C[0:1, :])
                i = act.activation(out=RRR(0), in_=RF(2), func=AF.Exp, scale=-0.5)
                tk[p + "r13b"] = ACs.n
            BA(_r13b)

            def _r13c(dve, t=t):
                dve.wait_ge(ACs.h, tk[p + "r13b"])
                dve.tensor_tensor(out=RF(2), in0=RF(3), in1=RF(4), op=ALU.subtract)
                i = dve.tensor_scalar(out=RRR(1), in0=RF(2), scalar1=1.0, scalar2=None,
                                      op0=ALU.mult)
                tk[p + "r13c"] = DVs.n
            BV(_r13c)

            def _bc2(pe, t=t):
                pe.wait_ge(DVs.h, tk[p + "r13c"])
                pe.wait_ge(DVs.h, tk[p + "scan"])
                pe.matmul(skip_group_check=True, out=ps[2][:], lhsT=ONES_R, rhs=RRR(0), start=True, stop=True)
                i = pe.matmul(skip_group_check=True, out=ps[3][:], lhsT=ONES_R, rhs=RRR(1), start=True, stop=True)
                tk[p + "bc2"] = PEs.n
            BT(_bc2)

            def _bc2e(dve, t=t):
                dve.wait_ge(PEs.h, tk[p + "bc2"])
                dve.tensor_copy(out=bca[:], in_=ps[2][:])
                i = dve.tensor_copy(out=bcb[:], in_=ps[3][:])
                tk[p + "bc2e"] = DVs.n
            BV(_bc2e)

            def _wol(sync, t=t):
                sync.wait_ge(PEs.h, tw(q + "fcmm"))
                sync.wait_ge(CC, 5)
                if t == 0:
                    sync.wait_ge(TSG, 16)
                Swo.inc(sync.dma_start(
                    out=wgtB[:].rearrange("p (c two f) -> p c two f", two=2, f=DM),
                    in_=st_wout.rearrange("(c p) (two f) -> p c two f", p=128, two=2)), 16)
                tk[p + "wol"] = Swo.n
            BS(_wol)

            def _fcla(sync, t=t):
                sync.wait_ge(PEs.h, tk[p + "z%d" % (NE // 2 - 1)])
                sync.wait_ge(CC, 3)
                SfA.inc(sync.dma_start(
                    out=wgtA[:].rearrange("p (k f) -> p k f", f=4096),
                    in_=st_fc.rearrange("(k p) f -> p k f", p=128)[:, 0:4, :]), 16)
                tk[p + "fcla"] = SfA.n
            BS(_fcla)

            if t == 0:
                qp_finish()

            for c in range(NE):
                def _yg(dve, t=t, c=c):
                    if c == 0:
                        dve.wait_ge(DVs.h, tk[p + "bc2e"])
                        dve.wait_ge(ACs.h, tk[p + "sze%d" % (NE // 2 - 1)])
                        if t == 0:
                            dve.wait_ge(DVs.h, tk["pnP%d" % (NH - 1)])
                        else:
                            dve.wait_ge(PEs.h, tk[q + "woutmm"])
                    dve.scalar_tensor_tensor(out=tf1[:], in0=xs_c(c), scalar=D_c(c),
                                             in1=bcb[:], op0=ALU.mult, op1=ALU.add)
                    dve.tensor_tensor(out=tf2[:], in0=tf1[:], in1=bca[:], op=ALU.mult)
                    i = dve.tensor_scalar(out=(tb1 if c % 2 == 0 else tb2)[:].bitcast(F32),
                                          in0=tf2[:], scalar1=G_c(c), scalar2=Bb_c(c),
                                          op0=ALU.mult, op1=ALU.add)
                    tk[p + "ygn%d" % c] = DVs.n
                BV(_yg)

                def _ygp(gp, t=t, c=c):
                    gp.wait_ge(DVs.h, tk[p + "ygn%d" % c])
                    i = gp.tensor_tensor(out=yg_c(c),
                                         in0=(tb1 if c % 2 == 0 else tb2)[:].bitcast(F32),
                                         in1=sz_c(c), op=ALU.mult)
                    tk[p + "yg%d" % c] = GPs.n
                BG(_ygp)

                def _mm(pe, t=t, c=c):
                    pe.wait_ge(Swo.h, tk[p + "wol"])
                    pe.wait_ge(GPs.h, tk[p + "yg%d" % c])
                    if c == 0:
                        pe.wait_ge(ACs.h, tk[p + "sze%d" % (NE // 2 - 1)])
                        pe.wait_ge(DVs.h, tk[p + "r13"])
                        pe.wait_ge(DVs.h, tk[p + "scan"])
                    for oc in range(ND):
                        i = pe.matmul(skip_group_check=True, out=ps[oc][:],
                                      lhsT=wout_sl(c)[:, oc * 128:(oc + 1) * 128],
                                      rhs=yg_c(c), start=(c == 0), stop=(c == NE - 1))
                    tk[p + "mixk%d" % c] = PEs.n
                BT(_mm)
            tk[p + "yg"] = tk[p + "yg%d" % (NE - 1)]

            def _x2(dve, t=t):
                dve.wait_ge(PEs.h, tk[p + "mixk%d" % (NE - 1)])
                for oc in range(ND):
                    i = dve.tensor_tensor(out=xfm_c(oc), in0=xfm_c(oc), in1=ps[oc][:],
                                          op=ALU.add)
                    tk[p + "x2_%d" % oc] = DVs.n
            BV(_x2)
            tk[p + "woutmm"] = tk[p + "mixk%d" % (NE - 1)]

            for g in range(2):
                def _cst(dve, t=t, g=g):
                    for k in range(4 * g, 4 * g + 4):
                        dve.wait_ge(DVs.h, tk[p + "x2_%d" % k])
                    dve.wait_ge(PEs.h, tw(p + "s2mm%d" % (g - 1)))
                    for k in range(4 * g, 4 * g + 4):
                        i = dve.tensor_copy(out=sq4(k % 4), in_=xfm_c(k))
                    tk[p + "s2c%d" % g] = DVs.n
                BV(_cst)

                def _csq(act, t=t, g=g):
                    for k in range(4 * g, 4 * g + 4):
                        act.wait_ge(DVs.h, tk[p + "x2_%d" % k])
                    act.wait_ge(PEs.h, tw(p + "s2mm%d" % (g - 1)))
                    for k in range(4 * g, 4 * g + 4):
                        i = act.square(out=cb4(k % 4), in_=xfm_c(k))
                    tk[p + "s2a%d" % g] = ACs.n
                BA(_csq)

                def _mm(pe, t=t, g=g):
                    pe.wait_ge(DVs.h, tk[p + "s2c%d" % g])
                    pe.wait_ge(ACs.h, tk[p + "s2a%d" % g])
                    if g == 0:
                        pe.wait_ge(DVs.h, tk[p + "x2_%d" % (ND - 1)])
                        pe.wait_ge(DVs.h, tk[p + "x2_%d" % (ND - 2)])
                    for k in range(4 * g, 4 * g + 4):
                        pe.matmul(skip_group_check=True, out=ps[4][0:1, :], lhsT=ONESC,
                                  rhs=sq4(k % 4), start=(k == 0), stop=(k == ND - 1))
                        i = pe.matmul(skip_group_check=True, out=ps[5][0:1, :], lhsT=ONESC,
                                      rhs=cb4(k % 4), start=(k == 0), stop=(k == ND - 1))
                    tk[p + "s2mm%d" % g] = PEs.n
                BT(_mm)

            def _r2dv(dve, t=t):
                dve.wait_ge(PEs.h, tk[p + "s2mm%d" % 1])
                dve.tensor_scalar(out=RF(0), in0=ps[4][0:1, :], scalar1=INV_D,
                                  scalar2=None, op0=ALU.mult)
                dve.tensor_scalar(out=RF(1), in0=ps[5][0:1, :], scalar1=INV_D,
                                  scalar2=None, op0=ALU.mult)
                dve.tensor_tensor(out=RF(2), in0=RF(0), in1=RF(0), op=ALU.mult)
                i = dve.tensor_tensor(out=RF(1), in0=RF(1), in1=RF(2), op=ALU.subtract)
                tk[p + "r2dv"] = DVs.n
            BV(_r2dv)

            def _r2ac(act, t=t):
                act.wait_ge(DVs.h, tk[p + "r2dv"])
                act.activation(out=RF(2), in_=RF(1), func=AF.Ln, bias=EPS_C[0:1, :])
                i = act.activation(out=RRR(0), in_=RF(2), func=AF.Exp, scale=-0.5)
                tk[p + "r2ac"] = ACs.n
            BA(_r2ac)

            def _r2dv2(dve, t=t):
                dve.wait_ge(ACs.h, tk[p + "r2ac"])
                dve.tensor_scalar(out=RF(2), in0=RF(0), scalar1=-1.0, scalar2=None,
                                  op0=ALU.mult)
                i = dve.tensor_tensor(out=RRR(1), in0=RF(2), in1=RRR(0), op=ALU.mult)
                tk[p + "r2dv2"] = DVs.n
            BV(_r2dv2)

            def _bc2b(pe, t=t):
                pe.wait_ge(DVs.h, tk[p + "r2dv2"])
                pe.wait_ge(DVs.h, tk[p + "yg"])
                pe.matmul(skip_group_check=True, out=ps[2][:], lhsT=ONES_R, rhs=RRR(0), start=True, stop=True)
                i = pe.matmul(skip_group_check=True, out=ps[3][:], lhsT=ONES_R, rhs=RRR(1), start=True, stop=True)
                tk[p + "bcL2"] = PEs.n
            BT(_bc2b)

            def _bc2be(dve, t=t):
                dve.wait_ge(PEs.h, tk[p + "bcL2"])
                dve.tensor_copy(out=bca[:], in_=ps[2][:])
                i = dve.tensor_copy(out=bcb[:], in_=ps[3][:])
                tk[p + "bcL2e"] = DVs.n
            BV(_bc2be)

            def _ap2(dve, t=t):
                dve.wait_ge(DVs.h, tk[p + "bcL2e"])
                dve.wait_ge(PEs.h, tk[p + "bp0"])
                for k in range(ND):
                    dve.tensor_tensor(out=tf1[:], in0=xfm_c(k), in1=bca[:], op=ALU.mult)
                    i = dve.tensor_tensor(out=xln_c(k), in0=tf1[:], in1=bcb[:],
                                          op=ALU.add)
                    if k == 3:
                        tk[p + "ap2h"] = DVs.n
                tk[p + "ap2"] = DVs.n
            BV(_ap2)


            def _fclb(sync, t=t):
                sync.wait_ge(PEs.h, tk[p + "woutmm"])
                if t == 0:
                    sync.wait_ge(CC, 3)
                Sfc.inc(sync.dma_start(
                    out=wgtB[:].rearrange("p (k f) -> p k f", f=4096),
                    in_=st_fc.rearrange("(k p) f -> p k f", p=128)[:, 4:8, :]), 16)
                tk[p + "fclb"] = Sfc.n
            BS(_fclb)

            sq17 = lambda j: (tb1 if j % 8 < 2 else tb2 if j % 8 < 4 else
                              tf1[:].bitcast(BF16) if j % 8 < 6 else
                              tf2[:].bitcast(BF16))[:, (j % 2) * T:(j % 2 + 1) * T]
            for g in range(NH // 4 + 2):
                ocs = [oc for oc in range(4 * g, 4 * g + 4) if oc < NH]

                def _pe(pe, t=t, g=g, ocs=tuple(ocs)):
                    for oc in ocs:
                        pe.wait_ge(SfA.h, tk[p + "fcla"])
                        pe.wait_ge(DVs.h, tk[p + "ap2h"])
                        pe.wait_ge(ACs.h, tw(p + "gel%d" % (g - 1)))
                        if oc == 0:
                            pe.wait_ge(DVs.h, tk[p + "bcL2e"])
                            pe.wait_ge(DVs.h, tk[p + "yg"])
                        for k in range(ND):
                            if k == ND // 2:
                                pe.wait_ge(Sfc.h, tk[p + "fclb"])
                                pe.wait_ge(DVs.h, tk[p + "ap2"])
                            i = pe.matmul(skip_group_check=True, out=ps[oc % 4][:],
                                          lhsT=fc_lhs(k, oc), rhs=xln_c(k),
                                          start=(k == 0), stop=(k == ND - 1))
                    if ocs:
                        tk[p + "h%d" % g] = PEs.n
                    if g > 1:
                        for j in range(4 * (g - 2), 4 * (g - 1)):
                            pe.wait_ge(DVs.h, tk[p + "hsq%d" % (g - 2)])
                            if j == 0:
                                pe.wait_ge(DVs.h, tk[p + "r13c"])
                                pe.wait_ge(DVs.h, tk[p + "scan"])
                            pe.matmul(skip_group_check=True, out=ps[6][0:1, :], lhsT=ONESC,
                                      rhs=hg_c(j), start=(j == 0), stop=(j == NH - 1))
                            i = pe.matmul(skip_group_check=True, out=ps[7][0:1, :], lhsT=ONESC,
                                          rhs=sq17(j), start=(j == 0), stop=(j == NH - 1))
                        tk[p + "hs%d" % (g - 2)] = PEs.n
                BT(_pe)

                def _ac(act, t=t, g=g, ocs=tuple(ocs)):
                    if ocs:
                        act.wait_ge(PEs.h, tk[p + "h%d" % g])
                        for oc in ocs:
                            i = act.activation(out=hg_c(oc), in_=ps[oc % 4][:], func=AF.Gelu)
                        tk[p + "gel%d" % g] = ACs.n
                BA(_ac)

                def _sqd(dve, t=t, g=g, ocs=tuple(ocs)):
                    if ocs:
                        dve.wait_ge(ACs.h, tk[p + "gel%d" % g])
                        dve.wait_ge(PEs.h, tw(p + "hs%d" % (g - 2)))
                        for oc in ocs:
                            i = dve.tensor_tensor(out=sq17(oc), in0=hg_c(oc),
                                                  in1=hg_c(oc), op=ALU.mult)
                        tk[p + "hsq%d" % g] = DVs.n
                BV(_sqd)
            tk[p + "fcmm"] = tk[p + "h%d" % (NH // 4 - 1)]

            def _r3dv(dve, t=t):
                dve.wait_ge(PEs.h, tk[p + "hs%d" % (NH // 4 - 1)])
                dve.tensor_scalar(out=RF(0), in0=ps[6][0:1, :], scalar1=INV_H,
                                  scalar2=None, op0=ALU.mult)
                dve.tensor_scalar(out=RF(1), in0=ps[7][0:1, :], scalar1=INV_H,
                                  scalar2=None, op0=ALU.mult)
                dve.tensor_tensor(out=RF(2), in0=RF(0), in1=RF(0), op=ALU.mult)
                i = dve.tensor_tensor(out=RF(1), in0=RF(1), in1=RF(2), op=ALU.subtract)
                tk[p + "rows3"] = DVs.n
            BV(_r3dv)

            def _r3ac(act, t=t):
                act.wait_ge(DVs.h, tk[p + "rows3"])
                act.activation(out=RF(2), in_=RF(1), func=AF.Ln, bias=EPS_C[0:1, :])
                i = act.activation(out=RRR(0), in_=RF(2), func=AF.Exp, scale=-0.5)
                tk[p + "r3ac"] = ACs.n
            BA(_r3ac)

            def _r3dv2(dve, t=t):
                dve.wait_ge(ACs.h, tk[p + "r3ac"])
                dve.tensor_scalar(out=RF(2), in0=RF(0), scalar1=-1.0, scalar2=None,
                                  op0=ALU.mult)
                i = dve.tensor_tensor(out=M3_BF, in0=RF(2), in1=RRR(0), op=ALU.mult)
                tk[p + "r3dv2"] = DVs.n
            BV(_r3dv2)

            def _bc3(pe, t=t):
                pe.wait_ge(DVs.h, tk[p + "r3dv2"])
                pe.wait_ge(DVs.h, tk[p + "ap2"])
                i = pe.matmul(skip_group_check=True, out=ps[2][:], lhsT=ONES_R, rhs=RRR(0), start=True, stop=True)
                tk[p + "bc3"] = PEs.n
            BT(_bc3)

            def _bc3e(dve, t=t):
                dve.wait_ge(PEs.h, tk[p + "bc3"])
                i = dve.tensor_copy(out=bca[:], in_=ps[2][:])
                tk[p + "bc3e"] = DVs.n
            BV(_bc3e)

            for gh in range(8):
                def _hs(dve, t=t, gh=gh):
                    if gh == 0:
                        dve.wait_ge(DVs.h, tk[p + "bc3e"])
                    for j in range(4 * gh, 4 * gh + 4):
                        i = dve.scalar_tensor_tensor(out=hg_c(j), in0=hg_c(j),
                                                     scalar=LNW_c(j), in1=bca[:],
                                                     op0=ALU.mult, op1=ALU.mult)
                    tk[p + "hsg%d" % gh] = DVs.n
                BV(_hs)
            tk[p + "hs"] = tk[p + "hsg7"]

            if t == 0:
                for k in range(NH):
                    if k % 2 == 0:
                        def _ld(sync, k=k):
                            sync.wait_ge(PEs.h, tw("ucmm%d" % (k - 3)))
                            if k < 4:
                                sync.wait_ge(CC, 4)
                                sync.wait_ge(TSG, 64)
                            i = sync.dma_start(out=pj_pair(k).rearrange("p (two f) -> p two f", f=DM),
                                               in_=st_proj[(k // 4) * 128:(k // 4 + 1) * 128,
                                                           (k % 4) * DM:(k % 4 + 2) * DM]
                                               .rearrange("p (two f) -> p two f", f=DM))
                            tk["ucl%d" % k] = Spj[(k // 2) % 2].inc(i, 16)
                        BS(_ld)

                    def _mm(pe, k=k):
                        pe.wait_ge(Spj[(k // 2) % 2].h, tk["ucl%d" % (k - k % 2)])
                        if k == 0:
                            pe.wait_ge(ACs.h, tk[p + "gel%d" % (NH // 4 - 1)])
                            pe.wait_ge(DVs.h, tk[p + "bc3e"])
                            pe.wait_ge(DVs.h, tk[p + "bcL2e"])
                            pe.wait_ge(DVs.h, tk[p + "r2dv"])
                            pe.wait_ge(DVs.h, tk[p + "rows3"])
                        last = None
                        for oc in range(ND):
                            last = pe.matmul(skip_group_check=True, out=ps[oc][:, 0:2],
                                             lhsT=pj_sl(k)[:, oc * 128:(oc + 1) * 128],
                                             rhs=LNWB(k), start=(k == 0), stop=(k == NH - 1))
                        tk["ucmm%d" % k] = PEs.n
                    BT(_mm)

                def _uce(dve):
                    dve.wait_ge(PEs.h, tk["ucmm%d" % (NH - 1)])
                    for oc in range(ND):
                        i = dve.tensor_copy(out=uccol[:, oc * 2:oc * 2 + 2],
                                            in_=ps[oc][:, 0:2])
                    tk["uce"] = DVs.n
                BV(_uce)

                def _ucw(gp):
                    gp.wait_ge(DVs.h, tk["uce"])
                    i = gp.dma_start(out=d_uc, in_=uccol[:])
                    tk["ucw"] = Tout.inc(i, 16)
                BG(_ucw)

                def _ucr(gp):
                    gp.wait_ge(Tout.h, tk["ucw"])
                    Sub[0].inc(gp.dma_start(out=ucrow[0:1, 0:1024].rearrange("one (oc j) -> one oc j", j=128),
                               in_=d_uc.rearrange("j (oc two) -> two oc j", two=2)[0:1],
                               allow_slow_non_contiguous=True), 16)
                    Sub[1].inc(gp.dma_start(out=ucrow[0:1, 1024:2048].rearrange("one (oc j) -> one oc j", j=128),
                               in_=d_uc.rearrange("j (oc two) -> two oc j", two=2)[1:2],
                               allow_slow_non_contiguous=True), 16)
                BG(_ucr)

            for k in range(NH):
                if k % 2 == 0:
                    def _ld(sync, t=t, k=k):
                        if k >= 4:
                            sync.wait_ge(PEs.h, tk[p + "pj%d" % (k - 3)])
                        elif t == 0:
                            sync.wait_ge(PEs.h, tk["ucmm%d" % (NH - 1)])
                        else:
                            sync.wait_ge(PEs.h, tk[q + "pj%d" % (NH - 1)])
                        i = sync.dma_start(out=pj_pair(k).rearrange("p (two f) -> p two f", f=DM),
                                           in_=st_proj[(k // 4) * 128:(k // 4 + 1) * 128,
                                                       (k % 4) * DM:(k % 4 + 2) * DM]
                                           .rearrange("p (two f) -> p two f", f=DM))
                        tk[p + "pjl%d" % k] = Spj[(k // 2) % 2].inc(i, 16)
                    BS(_ld)

                def _mm(pe, t=t, k=k):
                    pe.wait_ge(Spj[(k // 2) % 2].h, tk[p + "pjl%d" % (k - k % 2)])
                    pe.wait_ge(DVs.h, tk[p + "hsg%d" % (k // 4)])
                    if k == 0:
                        pe.wait_ge(ACs.h, tk[p + "gel%d" % (NH // 4 - 1)])
                        pe.wait_ge(DVs.h, tk[p + "bc3e"])
                        pe.wait_ge(DVs.h, tk[p + "r2dv"])
                        pe.wait_ge(DVs.h, tk[p + "rows3"])
                        if t == 0:
                            pe.wait_ge(DVs.h, tk["uce"])
                    last = None
                    for oc in range(ND):
                        last = pe.matmul(skip_group_check=True, out=ps[oc][:],
                                         lhsT=pj_sl(k)[:, oc * 128:(oc + 1) * 128],
                                         rhs=hg_c(k), start=(k == 0), stop=False)
                    tk[p + "pj%d" % k] = PEs.n
                BT(_mm)

            def _rk1(pe, t=t):
                pe.wait_ge(DVs.h, tk[p + "r3dv2"])
                if t == 0:
                    pe.wait_ge(Sub[0].h, 16)
                    pe.wait_ge(Sub[1].h, 16)
                last = None
                for oc in range(ND):
                    pe.matmul(skip_group_check=True, out=ps[oc][:], lhsT=U_ROW(oc), rhs=M3_BF,
                              start=False, stop=False)
                    last = pe.matmul(skip_group_check=True, out=ps[oc][:], lhsT=C_ROW(oc), rhs=ONEST_BF,
                                     start=False, stop=True)
                tk[p + "rk1"] = PEs.n
            BT(_rk1)

            def _stt(dve, t=t):
                dve.wait_ge(PEs.h, tk[p + "rk1"])
                dve.wait_ge(Tout.h, tw(q + "out"))
                if t == 0:
                    dve.wait_ge(DVs.h, tk["pnP%d" % (NH - 1)])
                last = None
                for oc in range(ND):
                    last = dve.scalar_tensor_tensor(out=outs_c(oc), in0=ps[oc][:],
                                                    scalar=1.0, in1=xfm_c(oc),
                                                    op0=ALU.mult, op1=ALU.add)
                tk[p + "stt"] = DVs.n
            BV(_stt)

            def _out(gp, t=t):
                gp.wait_ge(DVs.h, tk[p + "stt"])
                i = gp.dma_start(
                    out=yT.rearrange("(oc p) t -> p oc t", p=128)[:, :, t * T:(t + 1) * T],
                    in_=out_s[:].rearrange("p (oc t) -> p oc t", t=T))
                Tout.inc(i, 16)
                tk[p + "out"] = Tout.n
            BG(_out)

        def _fin(gp):
            gp.wait_ge(Tout.h, Tout.n)
        BG(_fin)

    return nc


_NC_CACHE = {}


def _get_program():
    if "nc" not in _NC_CACHE:
        _NC_CACHE["nc"] = build_program()
    return _NC_CACHE["nc"]


def kernel(L0_attn_x, L0_attn_Win, L0_attn_conv_w, L0_attn_conv_b, L0_attn_Wx,
           L0_attn_Wdt, L0_attn_bdt, L0_attn_A_log, L0_attn_D, L0_attn_ln_g,
           L0_attn_ln_b, L0_attn_Wout, L0_attn_res_scale, L0_ffn_W_fc, L0_ffn_ln_w,
           L0_ffn_ln_b, L0_ffn_W_proj):
    f32 = np.float32
    x = np.asarray(L0_attn_x, f32)[0]
    xTf = np.ascontiguousarray(x.T)
    win = np.asarray(L0_attn_Win, f32)
    winx = np.ascontiguousarray(win[:, 0:4096])
    winbc = np.ascontiguousarray(win[:, 4096:4097])
    convw = np.ascontiguousarray(np.asarray(L0_attn_conv_w, f32))
    convb = np.ascontiguousarray(np.asarray(L0_attn_conv_b, f32))
    wxw = np.ascontiguousarray(np.asarray(L0_attn_Wx, f32))
    wdt0 = np.ascontiguousarray(np.asarray(L0_attn_Wdt, f32)[:, 0:1])
    bdt0c = np.ascontiguousarray(np.broadcast_to(np.asarray(L0_attn_bdt, f32)[0:1],
                                                 (128,))).reshape(128, 1).copy()
    a0row = np.ascontiguousarray(np.asarray(L0_attn_A_log, f32)[0:1, :])
    dvec = np.ascontiguousarray(np.asarray(L0_attn_D, f32))
    lngv = np.ascontiguousarray(np.asarray(L0_attn_ln_g, f32))
    lnbv = np.ascontiguousarray(np.asarray(L0_attn_ln_b, f32))
    woutw = np.ascontiguousarray(np.asarray(L0_attn_Wout, f32))
    rs = np.asarray(L0_attn_res_scale, f32).reshape(-1)[0:1]
    rsc128 = np.ascontiguousarray(np.broadcast_to(rs, (128,))).reshape(128, 1).copy()
    fcT = np.ascontiguousarray(np.asarray(L0_ffn_W_fc, f32).T)
    lnw3v = np.ascontiguousarray(np.asarray(L0_ffn_ln_w, f32))
    lnb3v = np.ascontiguousarray(np.asarray(L0_ffn_ln_b, f32))
    projT = np.ascontiguousarray(np.asarray(L0_ffn_W_proj, f32).T)
    eyem = np.eye(128, dtype=f32)

    in_maps = []
    for c in range(NCORES):
        lo = c * TOK
        xs = np.ascontiguousarray(xTf[:, lo:lo + TOK])
        if c == 0:
            xlb = np.zeros((DM, 3), f32)
        else:
            xlb = np.ascontiguousarray(xTf[:, lo - 3:lo])
        in_maps.append(dict(
            xT=xs, xLB=xlb, winbc=winbc, convw=convw, convb=convb,
            wxw=wxw, wdt0=wdt0, bdt0c=bdt0c, a0row=a0row, dvec=dvec, lngv=lngv,
            lnbv=lnbv, rsc128=rsc128, lnw3v=lnw3v, lnb3v=lnb3v, eye=eyem,
            winshard=np.ascontiguousarray(winx[c * 128:(c + 1) * 128, :]),
            fcshard=np.ascontiguousarray(fcT[c * 128:(c + 1) * 128, :]),
            pjshard=np.ascontiguousarray(
                np.concatenate([projT[(4 * c + j) * 128:(4 * c + j + 1) * 128, :]
                                for j in range(4)], axis=1)),
            woshard=np.ascontiguousarray(
                np.concatenate([woutw[(2 * c + j) * 128:(2 * c + j + 1) * 128, :]
                                for j in range(2)], axis=1)),
        ))

    nc = _get_program()
    res = run_bass_kernel_spmd(nc, in_maps, core_ids=list(range(NCORES)))
    out = np.empty((DM, NCORES * TOK), f32)
    for c in range(NCORES):
        out[:, c * TOK:(c + 1) * TOK] = res.results[c]["yT"]
    return np.ascontiguousarray(out.T).reshape(1, NCORES * TOK, DM)


# version 39 (final)
# speedup vs baseline: 1.0618x; 1.0523x over previous
import contextlib
import numpy as np
import ml_dtypes
import concourse.bass as bass
import concourse.mybir as mybir
from concourse.bass_isa import ReduceOp
from concourse.bass_utils import run_bass_kernel_spmd

F32 = mybir.dt.float32
F32R = mybir.dt.float32r
BF16 = mybir.dt.bfloat16
AF = mybir.ActivationFunctionType
ALU = mybir.AluOpType
AX = mybir.AxisListType

NCORES = 8
TOK = 2048
T = 512
NT = TOK // T
DM = 1024
ND = DM // 128
E = 2048
NE = E // 128
HF = 4096
NH = HF // 128
R = 64
INV_D = 1.0 / DM
INV_E = 1.0 / E
INV_H = 1.0 / HF


class Sem:
    def __init__(self, h):
        self.h = h
        self.n = 0

    def inc(self, instr, amt=1):
        instr.then_inc(self.h, amt)
        self.n += amt
        return self.n




_OPMETA = {
    "tensor_scalar": (("out",), ("in0", "scalar1", "scalar2"), False),
    "scalar_tensor_tensor": (("out",), ("in0", "scalar", "in1"), False),
    "tensor_tensor": (("out",), ("in0", "in1"), False),
    "tensor_tensor_scan": (("out",), ("data0", "data1", "initial"), False),
    "tensor_copy": (("out",), ("in_",), False),
    "reduce_sum": (("out",), ("in_",), False),
    "tensor_reduce": (("out",), ("in_",), False),
    "reciprocal": (("out",), ("in_",), False),
    "activation": (("out", "accum_out"), ("in_", "bias", "scale"), False),
    "copy": (("out",), ("in_",), False),
    "square": (("out",), ("in_",), False),
    "sign": (("out",), ("in_",), False),
    "mul": (("out",), ("in_",), False),
    "add": (("out",), ("in_",), False),
    "sqrt": (("out",), ("in_",), False),
    "matmul": (("out",), ("lhsT", "rhs"), True),
    "transpose": (("out",), ("in_", "identity"), True),
}
_BLK = 512


def _regions(ap):
    try:
        name = ap.tensor.name
        off = ap.offset
        span = max(1, ap.free_size())
    except AttributeError:
        return ()
    return tuple((name, b) for b in range(off // _BLK, (off + span - 1) // _BLK + 1))


class Tracker:
    def __init__(self):
        self.w = {}
        self.r = {}

    def read_edges(self, ap):
        out = []
        for rg in _regions(ap):
            rec = self.w.get(rg)
            if rec is not None:
                out.append((rec[0], rec[1]))
        return out

    def write_edges(self, ap, is_mm):
        out = []
        for rg in _regions(ap):
            rec = self.w.get(rg)
            if rec is not None and not (is_mm and rec[2]):
                out.append((rec[0], rec[1]))
            for e in self.r.get(rg, ()):
                out.append(e)
        return out

    def note_read(self, ap, sem, ticket):
        for rg in _regions(ap):
            self.r.setdefault(rg, []).append((sem, ticket))

    def note_write(self, ap, sem, ticket, is_mm):
        for rg in _regions(ap):
            self.w[rg] = (sem, ticket, is_mm)
            self.r[rg] = []

    def note_dma_write(self, ap, sem, ticket):
        self.note_write(ap, sem, ticket, False)

    def note_dma_read(self, ap, sem, ticket):
        self.note_read(ap, sem, ticket)


class Wrap:

    def __init__(self, eng, sem, tracker, wm):
        self._e = eng
        self._s = sem
        self._t = tracker
        self._wm = wm

    def wait_ge(self, semh, val):
        k = id(semh)
        if self._wm.get(k, -1) >= val:
            return
        self._e.wait_ge(semh, val)
        self._wm[k] = val

    def _edge(self, sem, t):
        self.wait_ge(sem.h, t)

    def __getattr__(self, m):
        if m not in _OPMETA:
            return getattr(self._e, m)
        outs, ins, is_mm = _OPMETA[m]
        raw = getattr(self._e, m)
        tr = self._t

        def emit(**kw):
            for k in ins:
                v = kw.get(k)
                if v is not None and not isinstance(v, (int, float)):
                    for sem, t in tr.read_edges(v):
                        self._edge(sem, t)
            for k in outs:
                v = kw.get(k)
                if v is not None and not isinstance(v, (int, float)):
                    for sem, t in tr.write_edges(v, is_mm):
                        self._edge(sem, t)
            instr = raw(**kw)
            instr.then_inc(self._s.h, 1)
            self._s.n += 1
            tick = self._s.n
            for k in ins:
                v = kw.get(k)
                if v is not None and not isinstance(v, (int, float)):
                    tr.note_read(v, self._s, tick)
            for k in outs:
                v = kw.get(k)
                if v is not None and not isinstance(v, (int, float)):
                    tr.note_write(v, self._s, tick, is_mm)
            return instr
        return emit

    def memset(self, ap, const):
        for sem, t in self._t.write_edges(ap, False):
            self._edge(sem, t)
        instr = self._e.memset(ap, const)
        instr.then_inc(self._s.h, 1)
        self._s.n += 1
        self._t.note_write(ap, self._s, self._s.n, False)
        return instr


def build_program():
    nc = bass.Bass("TRN2", target_bir_lowering=False, debug=False)
    ap = lambda *a, **k: nc.dram_tensor(*a, **k).ap()

    xT = ap("xT", [DM, TOK], F32, kind="ExternalInput")
    xLB = ap("xLB", [DM, 3], F32, kind="ExternalInput")
    winshard = ap("winshard", [128, 4096], F32, kind="ExternalInput")
    winbc = ap("winbc", [DM, 1], F32, kind="ExternalInput")
    convw = ap("convw", [E, 4], F32, kind="ExternalInput")
    convb = ap("convb", [E], F32, kind="ExternalInput")
    wxw = ap("wxw", [E, 192], F32, kind="ExternalInput")
    wdt0 = ap("wdt0", [R, 1], F32, kind="ExternalInput")
    bdt0c = ap("bdt0c", [128, 1], F32, kind="ExternalInput")
    a0row = ap("a0row", [1, 128], F32, kind="ExternalInput")
    dvec = ap("dvec", [E], F32, kind="ExternalInput")
    lngv = ap("lngv", [E], F32, kind="ExternalInput")
    lnbv = ap("lnbv", [E], F32, kind="ExternalInput")
    woshard = ap("woshard", [128, 2048], F32, kind="ExternalInput")
    rsc128 = ap("rsc128", [128, 1], F32, kind="ExternalInput")
    fcshard = ap("fcshard", [128, HF], F32, kind="ExternalInput")
    lnw3v = ap("lnw3v", [HF], F32, kind="ExternalInput")
    lnb3v = ap("lnb3v", [HF], F32, kind="ExternalInput")
    pjshard = ap("pjshard", [128, 4096], F32, kind="ExternalInput")
    eye = ap("eye", [128, 128], F32, kind="ExternalInput")
    yT = ap("yT", [DM, TOK], F32, kind="ExternalOutput")

    st_win = ap("st_win", [ND * 128, 4096], BF16, kind="Internal", addr_space="Shared")
    st_wout = ap("st_wout", [ND * 128, 2048], BF16, kind="Internal", addr_space="Shared")
    st_fc = ap("st_fc", [ND * 128, HF], BF16, kind="Internal", addr_space="Shared")
    st_proj = ap("st_proj", [ND * 128, 4096], BF16, kind="Internal", addr_space="Shared")
    ccw = ap("ccw", [128, 4096], BF16, kind="Internal")
    ccf = ap("ccf", [128, 4096], BF16, kind="Internal")
    ccp = ap("ccp", [128, 4096], BF16, kind="Internal")
    cco = ap("cco", [128, 2048], BF16, kind="Internal")
    grin = ap("grin", [128, 2], F32, kind="Internal")
    grout = ap("grout", [128, 2], F32, kind="Internal", addr_space="Shared")
    d_uc = ap("d_uc", [128, 16], BF16, kind="Internal")

    ctx = contextlib.ExitStack()
    EN = ctx.enter_context
    with ctx:
        x_fm = EN(nc.sbuf_tensor("x_fm", [128, ND * T], F32))
        x_fm2 = EN(nc.sbuf_tensor("x_fm2", [128, ND * T], F32))
        xln = EN(nc.sbuf_tensor("xln", [128, ND * T], BF16))
        xp_s = EN(nc.sbuf_tensor("xp_s", [128, 2 * (T + 3)], BF16))
        cxp = EN(nc.sbuf_tensor("cxp", [128, NE * 3], BF16))
        mixA = EN(nc.sbuf_tensor("mixA", [128, 16 * T], BF16))
        mixB = EN(nc.sbuf_tensor("mixB", [128, 16 * T], BF16))
        ygb = EN(nc.sbuf_tensor("ygb", [128, 4 * T], BF16))
        wgtA = EN(nc.sbuf_tensor("wgtA", [128, 16384], BF16))
        wgtB = EN(nc.sbuf_tensor("wgtB", [128, 16384], BF16))
        pjb = EN(nc.sbuf_tensor("pjb", [128, 4 * DM], BF16))
        out_s = EN(nc.sbuf_tensor("out_s", [128, ND * T], F32))
        tf1 = EN(nc.sbuf_tensor("tf1", [128, T], F32))
        tf2 = EN(nc.sbuf_tensor("tf2", [128, T], F32))
        tb1 = EN(nc.sbuf_tensor("tb1", [128, 2 * T], BF16))
        tb2 = EN(nc.sbuf_tensor("tb2", [128, 2 * T], BF16))
        carry = EN(nc.sbuf_tensor("carry", [128, 1], F32))
        dgb = EN(nc.sbuf_tensor("dgb", [128, 2 * 4 * 128], BF16))
        bca = EN(nc.sbuf_tensor("bca", [128, T], F32))
        bcb = EN(nc.sbuf_tensor("bcb", [128, T], F32))
        colf = EN(nc.sbuf_tensor("colf", [128, 96], F32))
        colb = EN(nc.sbuf_tensor("colb", [128, 128], BF16))
        colf2 = EN(nc.sbuf_tensor("colf2", [128, 8], F32))
        cwf = EN(nc.sbuf_tensor("cwf", [128, 64], F32))
        wxbf = EN(nc.sbuf_tensor("wxbf", [128, 16 * 192], BF16))
        lnb3c = EN(nc.sbuf_tensor("lnb3c", [128, 32], F32))
        rwf = EN(nc.sbuf_tensor("rwf", [1, 3 * T], F32))
        rwr = EN(nc.sbuf_tensor("rwr", [1, 2 * T], F32R))
        rsm = EN(nc.sbuf_tensor("rsm", [1, 320], F32))
        rsr = EN(nc.sbuf_tensor("rsr", [1, 288], F32R))
        rbf = EN(nc.sbuf_tensor("rbf", [1, 1024], BF16))
        ucrow = EN(nc.sbuf_tensor("ucrow", [1, 2048], BF16))
        gac = EN(nc.sbuf_tensor("gac", [128, 40], F32))
        gacr = EN(nc.sbuf_tensor("gacr", [128, 32], F32R))
        ones128r = EN(nc.sbuf_tensor("ones128r", [128, 128], F32R))
        eyef = EN(nc.sbuf_tensor("eyef", [128, 128], F32))
        xlbf = EN(nc.sbuf_tensor("xlbf", [128, ND * 3], F32))
        xlbb = EN(nc.sbuf_tensor("xlbb", [128, ND * 3], BF16))
        xlnp = EN(nc.sbuf_tensor("xlnp", [128, ND * 3], BF16))
        uccol = EN(nc.sbuf_tensor("uccol", [128, 16], BF16))

        ps = [EN(nc.psum_tensor("psb%d" % i, [128, T], F32)) for i in range(8)]

        L1 = Sem(EN(nc.semaphore("L1")))
        L2 = [Sem(EN(nc.semaphore("L2%d" % i))) for i in range(2)]
        LF = [Sem(EN(nc.semaphore("LF%d" % i))) for i in range(4)]
        LP = [Sem(EN(nc.semaphore("LP%d" % i))) for i in range(4)]
        Lwo = [Sem(EN(nc.semaphore("Lwo%d" % i))) for i in range(2)]
        Sx = Sem(EN(nc.semaphore("Sx")))
        Sw = Sem(EN(nc.semaphore("Sw")))
        Swo = Sem(EN(nc.semaphore("Swo")))
        Sfc = Sem(EN(nc.semaphore("Sfc")))
        SfA = Sem(EN(nc.semaphore("SfA")))
        Spj = [Sem(EN(nc.semaphore("Spj%d" % i))) for i in range(2)]
        Tw = [Sem(EN(nc.semaphore("Tw%d" % i))) for i in range(2)]
        Two = [Sem(EN(nc.semaphore("Two%d" % i))) for i in range(2)]
        TqF = [Sem(EN(nc.semaphore("TqF%d" % i))) for i in range(2)]
        TqP = [Sem(EN(nc.semaphore("TqP%d" % i))) for i in range(2)]
        Tout = Sem(EN(nc.semaphore("Tout")))
        Sub = [Sem(EN(nc.semaphore("Sub%d" % i))) for i in range(2)]
        Lsh = [Sem(EN(nc.semaphore("Lsh%d" % i))) for i in range(4)]
        TSG = EN(nc.semaphore("TSG"))
        CC = EN(nc.semaphore("CC"))
        LPQ = [Sem(EN(nc.semaphore("LPQ%d" % i))) for i in range(8)]
        PEs = Sem(EN(nc.semaphore("PEs")))
        ACs = Sem(EN(nc.semaphore("ACs")))
        DVs = Sem(EN(nc.semaphore("DVs")))
        GPs = Sem(EN(nc.semaphore("GPs")))

        block = EN(nc.Block())

        TRK = Tracker()
        WM = {"PE": {}, "AC": {}, "DV": {}, "GP": {}, "SP": {}}

        def BT(f):
            block.tensor(lambda e, f=f: f(Wrap(e, PEs, TRK, WM["PE"])))

        def BA(f):
            block.scalar(lambda e, f=f: f(Wrap(e, ACs, TRK, WM["AC"])))

        def BV(f):
            block.vector(lambda e, f=f: f(Wrap(e, DVs, TRK, WM["DV"])))

        def BG(f):
            block.gpsimd(lambda e, f=f: f(Wrap(e, GPs, TRK, WM["GP"])))

        def BS(f):
            block.sync(lambda e, f=f: f(Wrap(e, GPs, TRK, WM["SP"])))

        P0Q = []

        def drain(n):
            for _ in range(n):
                if P0Q:
                    P0Q.pop(0)()

        def drain_all():
            drain(len(P0Q))

        D_c = lambda c: colf[:, c:c + 1]
        G_c = lambda c: colf[:, 16 + c:16 + c + 1]
        Bb_c = lambda c: colf[:, 32 + c:32 + c + 1]
        CB_c = lambda c: colf[:, 48 + c:48 + c + 1]
        LNW_c = lambda k: colf[:, 64 + k:64 + k + 1]
        Dbf_c = lambda c: colb[:, c:c + 1]
        D2bf_c = lambda c: colb[:, 16 + c:16 + c + 1]
        LNWB = lambda k: colb[:, 32 + 2 * k:32 + 2 * k + 2]
        WDT0B = colb[0:R, 96:97]
        WINBB = lambda k: colb[:, 97 + k:97 + k + 1]
        ONESC = colb[:, 105:106]
        THRF, NEGF, THRP, NEGP, RS_C, BD0_C, EPS_C = (colf2[:, i:i + 1] for i in range(7))
        WDT0F = gac[0:R, 32:33]
        ONES_R = rsr[0:1, 0:128]
        AEXP_R = rsr[0:1, 128:256]
        RSTDP_R = rsr[0:1, 256:260]
        M2P_R = rsr[0:1, 260:264]
        A0_S = rsm[0:1, 128:256]
        MUP = rsm[0:1, 256:260]
        E2P = rsm[0:1, 260:264]
        ONEST_BF = rbf[0:1, 0:T]
        M3_BF = rbf[0:1, T:2 * T]
        U_ROW = lambda oc: ucrow[0:1, oc * 128:(oc + 1) * 128]
        C_ROW = lambda oc: ucrow[0:1, 1024 + oc * 128:1024 + (oc + 1) * 128]

        RF = lambda i: rwf[0:1, i * T:(i + 1) * T] if i < 3 else \
            (bca if i == 3 else bcb)[0:1, :]
        RRR = lambda i: rwr[0:1, i * T:(i + 1) * T]

        xs_c = lambda c: mixA[:, c * T:(c + 1) * T]
        sz_c = lambda c: mixB[:, c * T:(c + 1) * T]
        hg_c = lambda j: (mixA if j < 16 else mixB)[:, (j % 16) * T:((j % 16) + 1) * T]
        yg_c = lambda c: ygb[:, (c % 4) * T:(c % 4 + 1) * T]
        xfm_c = lambda k: x_fm[:, k * T:(k + 1) * T]
        xln_c = lambda k: xln[:, k * T:(k + 1) * T]
        outs_c = lambda k: out_s[:, k * T:(k + 1) * T]
        xpb = lambda p: xp_s[:, p * (T + 3):(p + 1) * (T + 3)]
        dg_t = lambda c, k: dgb[:, ((c % 2) * 4 + k) * 128:((c % 2) * 4 + k + 1) * 128]
        win_sl = lambda k: wgtA[:, k * 2048:(k + 1) * 2048]
        wout_sl = lambda k: wgtB[:, k * DM:(k + 1) * DM]
        fc_lhs = lambda k, oc: (wgtA if k < 4 else wgtB)[:, (k % 4) * 4096 + oc * 128:
                                                         (k % 4) * 4096 + (oc + 1) * 128]
        pj_sl = lambda k: pjb[:, (k % 4) * DM:((k % 4) + 1) * DM]
        pj_pair = lambda k: pjb[:, (k % 4) * DM:((k % 4) + 2) * DM]
        stgf = out_s[:, 0:4096]
        stgf_h = lambda p: out_s[:, p * 2048:(p + 1) * 2048]
        stg_bf = wgtB[:, 0:4096]
        stgb_h = lambda p: wgtB[:, p * 2048:(p + 1) * 2048]
        stgb_q = lambda p: wgtB[:, p * 1024:(p + 1) * 1024]
        sq_b = lambda p: tb1[:, p * T:(p + 1) * T]
        Cbf = tb1[:, 0:T]
        drbf = tb1[:, T:2 * T]
        cb_b = lambda p: tb2[:, p * T:(p + 1) * T]
        sq4 = lambda j: (tb1 if j < 2 else tb2)[:, (j % 2) * T:(j % 2 + 1) * T]
        cb4 = lambda j: mixB[:, j * T:(j + 1) * T]

        tk = {}

        def tw(key, default=0):
            return tk.get(key, default)

        def _p0v(sync):
            last = None
            for dst, src, slow in (
                (eyef[:], eye, False),
                (colf[:, 0:16], dvec.rearrange("(c p) -> p c", p=128), True),
                (colf[:, 16:32], lngv.rearrange("(c p) -> p c", p=128), True),
                (colf[:, 32:48], lnbv.rearrange("(c p) -> p c", p=128), True),
                (colf[:, 48:64], convb.rearrange("(c p) -> p c", p=128), True),
                (colf[:, 64:96], lnw3v.rearrange("(c p) -> p c", p=128), True),
                (lnb3c[:], lnb3v.rearrange("(c p) -> p c", p=128), True),
                (WDT0F, wdt0, True),
                (BD0_C, bdt0c, True),
                (RS_C, rsc128, True),
                (rsm[0:1, 128:256], a0row, False),
                (gac[:, 24:32].rearrange("p (k one) -> p k one", one=1),
                 winbc.rearrange("(k p) one -> p k one", p=128), True),
                (x_fm[:, 0:64].rearrange("p (c k) -> p c k", k=4),
                 convw.rearrange("(c p) k -> p c k", p=128), True),
                (x_fm[:, 64:64 + 16 * 192].rearrange("p (c r) -> p c r", r=192),
                 wxw.rearrange("(c p) r -> p c r", p=128), True),
                (xlbf[:].rearrange("p (k c) -> p k c", c=3),
                 xLB.rearrange("(k p) c -> p k c", p=128), True),
            ):
                L1.inc(sync.dma_start(out=dst, in_=src, allow_slow_non_contiguous=slow), 16)
            tk["p0v"] = L1.n
        BS(_p0v)

        def _p0c_dv(dve):
            dve.wait_ge(L1.h, tk["p0v"])
            dve.memset(EPS_C, 1e-5)
            dve.memset(ONEST_BF, 1.0)
            dve.memset(ONESC, 1.0)
            dve.memset(carry[:], 0.0)
            dve.memset(cxp[:], 0.0)
            dve.tensor_scalar(out=ONES_R, in0=eyef[0:1, 0:128], scalar1=0.0, scalar2=1.0,
                              op0=ALU.mult, op1=ALU.add)
            dve.tensor_scalar(out=ones128r[:], in0=eyef[:], scalar1=0.0, scalar2=1.0,
                              op0=ALU.mult, op1=ALU.add)
            dve.tensor_scalar(out=rsr[0:1, 256:264], in0=eyef[0:1, 0:8], scalar1=0.0,
                              scalar2=None, op0=ALU.mult)
            dve.tensor_copy(out=cwf[:], in_=x_fm[:, 0:64])
            dve.tensor_copy(out=wxbf[:], in_=x_fm[:, 64:64 + 16 * 192])
            dve.tensor_copy(out=colb[:, 0:16], in_=colf[:, 0:16])
            dve.tensor_copy(out=WDT0B, in_=WDT0F)
            dve.tensor_copy(out=colb[:, 97:105], in_=gac[:, 24:32])
            dve.tensor_copy(out=colb[:, 32:96:2], in_=colf[:, 64:96])
            i = dve.tensor_copy(out=colb[:, 33:96:2], in_=lnb3c[:])
            tk["p0cdv"] = DVs.n
        BV(_p0c_dv)

        def _p0c_ac(act):
            act.wait_ge(DVs.h, tk["p0cdv"])
            act.square(out=colb[:, 16:32], in_=colf[:, 0:16])
            i = act.activation(out=AEXP_R, in_=A0_S, func=AF.Exp)
            i = act.copy(out=xlbb[:], in_=xlbf[:])
            tk["p0cac"] = ACs.n
        BA(_p0c_ac)

        xlnF = xln[:].bitcast(F32)
        pnP_ = mixA[:, 0:4096]
        pnN_ = mixA[:, 4096:8192]
        RG = [list(range(NCORES))]

        def _shld(sync):
            sync.wait_ge(DVs.h, tk["p0cdv"])
            Lsh[0].inc(sync.dma_start(out=x_fm[:, 0:4096], in_=winshard), 16)
            Lsh[1].inc(sync.dma_start(out=out_s[:, 0:4096], in_=fcshard), 16)
            Lsh[2].inc(sync.dma_start(out=x_fm2[:, 0:4096], in_=pjshard), 16)
            Lsh[3].inc(sync.dma_start(out=xlnF[:, 0:2048], in_=woshard), 16)
        BS(_shld)

        def _wcast(act):
            act.wait_ge(Lsh[0].h, 16)
            i = act.copy(out=wgtB[:, 0:4096], in_=x_fm[:, 0:4096])
            tk["wincast"] = ACs.n
        BA(_wcast)

        def _x0(sync):
            sync.wait_ge(ACs.h, tk["wincast"])
            Sx.inc(sync.dma_start(
                out=x_fm[:].rearrange("p (k t) -> p k t", t=T),
                in_=xT.rearrange("(k p) t -> p k t", p=128)[:, :, 0:T]), 16)
            tk["xload_t0"] = Sx.n
        BS(_x0)

        def _gpart(dve):
            dve.wait_ge(Lsh[1].h, 16)
            dve.reduce_sum(out=gac[:, 0:1], in_=out_s[:, 0:4096], axis=AX.X,
                           apply_absolute_value=True)
            dve.wait_ge(Lsh[2].h, 16)
            i = dve.reduce_sum(out=gac[:, 1:2], in_=x_fm2[:, 0:4096], axis=AX.X,
                               apply_absolute_value=True)
            tk["gpart"] = DVs.n
        BV(_gpart)

        def _cc1(gp):
            gp.wait_ge(DVs.h, tk["gpart"])
            gp.dma_start(out=grin, in_=gac[:, 0:2]).then_inc(TSG, 16)
            gp.wait_ge(TSG, 16)
            gp.collective_compute("AllReduce", ALU.add, replica_groups=RG,
                                  ins=[grin.opt()], outs=[grout.opt()]).then_inc(CC, 1)
            gp.wait_ge(CC, 1)
            gp.dma_start(out=gac[:, 4:6], in_=grout).then_inc(TSG, 16)
            gp.wait_ge(TSG, 32)
            gp.wait_ge(ACs.h, tk["wincast"])
            gp.dma_start(out=ccw, in_=wgtB[:, 0:4096]).then_inc(TSG, 16)
            gp.wait_ge(TSG, 48)
            gp.collective_compute("AllGather", ALU.bypass, replica_groups=RG,
                                  ins=[ccw.opt()], outs=[st_win.opt()]).then_inc(CC, 1)
        BG(_cc1)

        def _gcp(dve):
            dve.wait_ge(TSG, 32)
            i = dve.tensor_scalar(out=gacr[:, 0:2], in0=gac[:, 4:6], scalar1=1.0,
                                  scalar2=None, op0=ALU.mult)
            tk["gcp"] = DVs.n
        BV(_gcp)

        def _gmm(pe):
            pe.wait_ge(DVs.h, tk["gcp"])
            i = pe.matmul(skip_group_check=True, out=ps[6][:, 0:2], lhsT=ones128r[:],
                          rhs=gacr[:, 0:2], start=True, stop=True)
            tk["gmm"] = PEs.n
        BT(_gmm)

        def _thr(dve):
            dve.wait_ge(PEs.h, tk["gmm"])
            dve.tensor_scalar(out=THRF, in0=ps[6][:, 0:1], scalar1=0.5 / 4194304.0,
                              scalar2=5e-7, op0=ALU.mult, op1=ALU.add)
            dve.tensor_scalar(out=NEGF, in0=ps[6][:, 0:1], scalar1=-0.5 / 4194304.0,
                              scalar2=-5e-7, op0=ALU.mult, op1=ALU.add)
            dve.tensor_scalar(out=THRP, in0=ps[6][:, 1:2], scalar1=0.5 / 4194304.0,
                              scalar2=5e-7, op0=ALU.mult, op1=ALU.add)
            i = dve.tensor_scalar(out=NEGP, in0=ps[6][:, 1:2], scalar1=-0.5 / 4194304.0,
                                  scalar2=-5e-7, op0=ALU.mult, op1=ALU.add)
            tk["thrF"] = DVs.n
            tk["thrP"] = DVs.n
        BV(_thr)

        def _qF(dve):
            dve.tensor_scalar(out=pnP_, in0=out_s[:, 0:4096], scalar1=THRF,
                              scalar2=None, op0=ALU.is_gt)
            dve.tensor_scalar(out=pnN_, in0=out_s[:, 0:4096], scalar1=NEGF,
                              scalar2=None, op0=ALU.is_lt)
            i = dve.tensor_tensor(out=pjb[:, 0:4096], in0=pnP_, in1=pnN_,
                                  op=ALU.subtract)
            tk["qFsub"] = DVs.n
        BV(_qF)

        def _qP(dve):
            dve.tensor_scalar(out=pnP_, in0=x_fm2[:, 0:4096], scalar1=THRP,
                              scalar2=None, op0=ALU.is_gt)
            dve.tensor_scalar(out=pnN_, in0=x_fm2[:, 0:4096], scalar1=NEGP,
                              scalar2=None, op0=ALU.is_lt)
            i = dve.tensor_tensor(out=mixB[:, 0:4096], in0=pnP_, in1=pnN_,
                                  op=ALU.subtract)
            tk["qPsub"] = DVs.n
        BV(_qP)

        def _wocast(act):
            act.wait_ge(Lsh[3].h, 16)
            act.wait_ge(L1.h, tk["p0v"])
            i = act.activation(out=mixB[:, 4096:6144], in_=xlnF[:, 0:2048],
                               func=AF.Identity, scale=RS_C)
            tk["wocast"] = ACs.n
        BA(_wocast)

        def _cc2(gp):
            gp.wait_ge(DVs.h, tk["qFsub"])
            gp.dma_start(out=ccf, in_=pjb[:, 0:4096]).then_inc(TSG, 16)
            gp.wait_ge(TSG, 64)
            gp.collective_compute("AllGather", ALU.bypass, replica_groups=RG,
                                  ins=[ccf.opt()], outs=[st_fc.opt()]).then_inc(CC, 1)
            gp.wait_ge(DVs.h, tk["qPsub"])
            gp.dma_start(out=ccp, in_=mixB[:, 0:4096]).then_inc(TSG, 16)
            gp.wait_ge(TSG, 80)
            gp.collective_compute("AllGather", ALU.bypass, replica_groups=RG,
                                  ins=[ccp.opt()], outs=[st_proj.opt()]).then_inc(CC, 1)
            gp.wait_ge(ACs.h, tk["wocast"])
            gp.dma_start(out=cco, in_=mixB[:, 4096:6144]).then_inc(TSG, 16)
            gp.wait_ge(TSG, 96)
            gp.collective_compute("AllGather", ALU.bypass, replica_groups=RG,
                                  ins=[cco.opt()], outs=[st_wout.opt()]).then_inc(CC, 1)
        BG(_cc2)

        def _w0(sync):
            sync.wait_ge(CC, 2)
            Sw.inc(sync.dma_start(
                out=wgtA[:].rearrange("p (k f) -> p k f", f=2048),
                in_=st_win.rearrange("(k p) f -> p k f", p=128)[:, :, 0:2048]), 16)
            tk["winxp_t0"] = Sw.n
        BS(_w0)

        tk["p0_quant_done"] = tk["qPsub"]
        tk["pnP%d" % (NH - 1)] = tk["qPsub"]
        tk["qF13"] = tk["qPsub"]
        tk["woc%d" % (NE - 1)] = tk["wocast"]

        def qp_drain(n):
            pass

        def qp_flush_gamma():
            pass

        def qp_finish():
            pass

        def _pro_dv(dve):
            dve.wait_ge(L1.h, tk["p0v"])
            i = dve.tensor_tensor(out=tb1[:, 0:ND * 3], in0=xlbf[:], in1=xlbf[:],
                                  op=ALU.mult)
            tk["pro_sq"] = DVs.n
        BV(_pro_dv)

        def _pro_pe(pe):
            pe.wait_ge(DVs.h, tk["pro_sq"])
            for k in range(ND):
                pe.matmul(skip_group_check=True, out=ps[4][0:1, 0:3], lhsT=ONESC, rhs=xlbb[:, k * 3:(k + 1) * 3],
                          start=(k == 0), stop=(k == ND - 1))
            last = None
            for k in range(ND):
                last = pe.matmul(skip_group_check=True, out=ps[5][0:1, 0:3], lhsT=ONESC,
                                 rhs=tb1[:, k * 3:(k + 1) * 3],
                                 start=(k == 0), stop=(k == ND - 1))
            tk["pro_mm"] = PEs.n
        BT(_pro_pe)

        def _pro_rows_dv(dve):
            dve.wait_ge(PEs.h, tk["pro_mm"])
            dve.tensor_scalar(out=MUP[0:1, 0:3], in0=ps[4][0:1, 0:3], scalar1=INV_D,
                              scalar2=None, op0=ALU.mult)
            dve.tensor_scalar(out=E2P[0:1, 0:3], in0=ps[5][0:1, 0:3], scalar1=INV_D,
                              scalar2=None, op0=ALU.mult)
            dve.tensor_tensor(out=rsm[0:1, 264:267], in0=MUP[0:1, 0:3], in1=MUP[0:1, 0:3],
                              op=ALU.mult)
            i = dve.tensor_tensor(out=E2P[0:1, 0:3], in0=E2P[0:1, 0:3],
                                  in1=rsm[0:1, 264:267], op=ALU.subtract)
            tk["pro_r1"] = DVs.n
        BV(_pro_rows_dv)

        def _pro_rows_ac(act):
            act.wait_ge(DVs.h, tk["pro_r1"])
            act.activation(out=rsm[0:1, 264:267], in_=E2P[0:1, 0:3], func=AF.Ln,
                           bias=EPS_C[0:1, :])
            i = act.activation(out=RSTDP_R[0:1, 0:3], in_=rsm[0:1, 264:267], func=AF.Exp,
                               scale=-0.5)
            tk["pro_r2"] = ACs.n
        BA(_pro_rows_ac)

        def _pro_rows_dv2(dve):
            dve.wait_ge(ACs.h, tk["pro_r2"])
            i = dve.tensor_scalar(out=rsm[0:1, 268:271], in0=MUP[0:1, 0:3], scalar1=-1.0,
                                  scalar2=None, op0=ALU.mult)
            i = dve.tensor_tensor(out=M2P_R[0:1, 0:3], in0=rsm[0:1, 268:271],
                                  in1=RSTDP_R[0:1, 0:3], op=ALU.mult)
            tk["pro_r3"] = DVs.n
        BV(_pro_rows_dv2)

        def _pro_bc(pe):
            pe.wait_ge(DVs.h, tk["pro_r3"])
            pe.matmul(skip_group_check=True, out=ps[2][:, 0:4], lhsT=ONES_R, rhs=RSTDP_R, start=True, stop=True)
            i = pe.matmul(skip_group_check=True, out=ps[3][:, 0:4], lhsT=ONES_R, rhs=M2P_R, start=True, stop=True)
            tk["pro_bc"] = PEs.n
        BT(_pro_bc)

        def _pro_bce(act):
            act.wait_ge(PEs.h, tk["pro_bc"])
            act.copy(out=bca[:, 0:4], in_=ps[2][:, 0:4])
            i = act.copy(out=bcb[:, 0:4], in_=ps[3][:, 0:4])
            tk["pro_bce"] = ACs.n
        BA(_pro_bce)

        def _pro_ap(dve):
            dve.wait_ge(DVs.h, tk["pro_bce"])
            last = None
            for k in range(ND):
                dve.tensor_tensor(out=tf1[:, 0:3], in0=xlbf[:, k * 3:(k + 1) * 3],
                                  in1=bca[:, 0:3], op=ALU.mult)
                last = dve.tensor_tensor(out=xlnp[:, k * 3:(k + 1) * 3], in0=tf1[:, 0:3],
                                         in1=bcb[:, 0:3], op=ALU.add)
            tk["pro_ap"] = DVs.n
        BV(_pro_ap)

        for t in range(NT):
            p = "t%d_" % t
            q = "t%d_" % (t - 1)
            xfm_c = lambda k, _xb=(x_fm if t % 2 == 0 else x_fm2): _xb[:, k * T:(k + 1) * T]

            if t > 0:
                def _w(sync, t=t):
                    sync.wait_ge(PEs.h, tk[q + "fcmm"])
                    Sw.inc(sync.dma_start(
                        out=wgtA[:].rearrange("p (k f) -> p k f", f=2048),
                        in_=st_win.rearrange("(k p) f -> p k f", p=128)[:, :, 0:2048]), 16)
                    tk[p + "winxp"] = Sw.n
                BS(_w)
            else:
                tk[p + "xload"] = tk["xload_t0"]
                tk[p + "winxp"] = tk["winxp_t0"]

            for g in range(2):
                def _cst(dve, t=t, g=g):
                    dve.wait_ge(Sx.h, tk[p + "xload"])
                    if g == 0 and t > 0:
                        dve.wait_ge(GPs.h, tw(q + "yg%d" % (NE - 1)))
                    if g == 0 and t == 0:
                        dve.wait_ge(DVs.h, tk["pro_ap"])
                        dve.wait_ge(PEs.h, tk["pro_mm"])
                    dve.wait_ge(PEs.h, tw(p + "s1mm%d" % (g - 1)))
                    for k in range(4 * g, 4 * g + 4):
                        i = dve.tensor_copy(out=sq4(k % 4), in_=xfm_c(k))
                    tk[p + "s1c%d" % g] = DVs.n
                BV(_cst)

                def _csq(act, t=t, g=g):
                    act.wait_ge(Sx.h, tk[p + "xload"])
                    if g == 0 and t == 0:
                        act.wait_ge(PEs.h, tk["pro_mm"])
                        act.wait_ge(TSG, 80)
                    act.wait_ge(PEs.h, tw(p + "s1mm%d" % (g - 1)))
                    if g == 0:
                        act.wait_ge(PEs.h, tw(q + "projmm"))
                    for k in range(4 * g, 4 * g + 4):
                        i = act.square(out=cb4(k % 4), in_=xfm_c(k))
                    tk[p + "s1a%d" % g] = ACs.n
                BA(_csq)

                def _mm(pe, t=t, g=g):
                    pe.wait_ge(DVs.h, tk[p + "s1c%d" % g])
                    pe.wait_ge(ACs.h, tk[p + "s1a%d" % g])
                    if g == 0:
                        pe.wait_ge(DVs.h, tw(q + "stt"))
                        if t == 0:
                            pe.wait_ge(DVs.h, tk["pro_r1"])
                    for k in range(4 * g, 4 * g + 4):
                        pe.matmul(skip_group_check=True, out=ps[4][0:1, :], lhsT=ONESC,
                                  rhs=sq4(k % 4), start=(k == 0), stop=(k == ND - 1))
                        i = pe.matmul(skip_group_check=True, out=ps[5][0:1, :], lhsT=ONESC,
                                      rhs=cb4(k % 4), start=(k == 0), stop=(k == ND - 1))
                    tk[p + "s1mm%d" % g] = PEs.n
                BT(_mm)
                if t == 0:
                    qp_drain(4)

            def _r1dv(dve, t=t):
                dve.wait_ge(PEs.h, tk[p + "s1mm%d" % 1])
                dve.tensor_scalar(out=RF(0), in0=ps[4][0:1, :], scalar1=INV_D,
                                  scalar2=None, op0=ALU.mult)
                dve.tensor_scalar(out=RF(1), in0=ps[5][0:1, :], scalar1=INV_D,
                                  scalar2=None, op0=ALU.mult)
                dve.tensor_tensor(out=RF(2), in0=RF(0), in1=RF(0), op=ALU.mult)
                i = dve.tensor_tensor(out=RF(1), in0=RF(1), in1=RF(2), op=ALU.subtract)
                tk[p + "r1dv"] = DVs.n
            BV(_r1dv)

            def _r1ac(act, t=t):
                act.wait_ge(DVs.h, tk[p + "r1dv"])
                act.activation(out=RF(2), in_=RF(1), func=AF.Ln, bias=EPS_C[0:1, :])
                i = act.activation(out=RRR(0), in_=RF(2), func=AF.Exp, scale=-0.5)
                tk[p + "r1ac"] = ACs.n
            BA(_r1ac)

            def _r1dv2(dve, t=t):
                dve.wait_ge(ACs.h, tk[p + "r1ac"])
                dve.tensor_scalar(out=RF(2), in0=RF(0), scalar1=-1.0, scalar2=None,
                                  op0=ALU.mult)
                i = dve.tensor_tensor(out=RRR(1), in0=RF(2), in1=RRR(0), op=ALU.mult)
                tk[p + "r1dv2"] = DVs.n
            BV(_r1dv2)

            def _bc1(pe, t=t):
                pe.wait_ge(DVs.h, tk[p + "r1dv2"])
                pe.wait_ge(DVs.h, tw(q + "bc3e"))
                pe.matmul(skip_group_check=True, out=ps[2][:], lhsT=ONES_R, rhs=RRR(0), start=True, stop=True)
                i = pe.matmul(skip_group_check=True, out=ps[3][:], lhsT=ONES_R, rhs=RRR(1), start=True, stop=True)
                tk[p + "bc1"] = PEs.n
            BT(_bc1)

            def _bc1e(dve, t=t):
                dve.wait_ge(PEs.h, tk[p + "bc1"])
                dve.tensor_copy(out=bca[:], in_=ps[2][:])
                i = dve.tensor_copy(out=bcb[:], in_=ps[3][:])
                tk[p + "bc1e"] = DVs.n
            BV(_bc1e)

            def _ap1(dve, t=t):
                dve.wait_ge(DVs.h, tk[p + "bc1e"])
                dve.wait_ge(PEs.h, tw(q + "fcmm"))
                dve.wait_ge(PEs.h, tw(q + "hs%d" % (NH // 4 - 1)))
                if t == 0:
                    dve.wait_ge(ACs.h, tk["woc%d" % (NE - 1)])
                for k in range(ND):
                    dve.tensor_tensor(out=tf1[:], in0=xfm_c(k), in1=bca[:], op=ALU.mult)
                    i = dve.tensor_tensor(out=xln_c(k), in0=tf1[:], in1=bcb[:],
                                          op=ALU.add)
                    if k == 3:
                        tk[p + "ap1h"] = DVs.n
                tk[p + "ap1"] = DVs.n
            BV(_ap1)

            for g in range(NE // 2 + 1):
                cA, cB = 2 * g, 2 * g + 1

                def _pe(pe, t=t, g=g, cA=cA, cB=cB):
                    if cA < NE:
                        pe.wait_ge(Sw.h, tk[p + "winxp"])
                        pe.wait_ge(DVs.h, tk[p + "ap1h"])
                        pe.wait_ge(DVs.h, tw(p + "xpe%d" % (g - 1)))
                        for c in (cA, cB):
                            if t == 0:
                                pe.wait_ge(DVs.h, tk["pro_ap"])
                                if c < 2:
                                    pe.wait_ge(DVs.h, tk[p + "r1dv"])
                                for k in range(ND):
                                    pe.matmul(skip_group_check=True, out=ps[4 + c % 2][:, 0:3],
                                              lhsT=win_sl(k)[:, c * 128:(c + 1) * 128],
                                              rhs=xlnp[:, k * 3:(k + 1) * 3],
                                              start=(k == 0), stop=(k == ND - 1))
                            for k in range(ND):
                                if k == 4:
                                    pe.wait_ge(DVs.h, tk[p + "ap1"])
                                pe.matmul(skip_group_check=True, out=ps[c % 2][:],
                                          lhsT=win_sl(k)[:, c * 128:(c + 1) * 128],
                                          rhs=xln_c(k), start=(k == 0), stop=(k == ND - 1))
                        tk[p + "xp%d" % g] = PEs.n
                    if g > 0:
                        for cc in (2 * g - 2, 2 * g - 1):
                            pe.wait_ge(DVs.h, tk[p + "xpe%d" % (g - 1)])
                            pe.wait_ge(DVs.h, tk[p + "dg%d" % (g - 1)])
                            if cc < 2:
                                pe.wait_ge(DVs.h, tk[p + "bc1e"])
                            last = None
                            for kk in range(4):
                                last = pe.matmul(skip_group_check=True, out=ps[2 + cc % 2][:],
                                                 lhsT=dg_t(cc, kk),
                                                 rhs=xpb(cc % 2)[:, kk:kk + T],
                                                 start=(kk == 0), stop=(kk == 3))
                        tk[p + "cv%d" % (g - 1)] = PEs.n
                BT(_pe)

                def _dve_e(dve, t=t, g=g, cA=cA, cB=cB):
                    if cA < NE:
                        dve.wait_ge(PEs.h, tk[p + "xp%d" % g])
                        dve.wait_ge(PEs.h, tw(p + "cv%d" % (g - 1)))
                        for c in (cA, cB):
                            if t == 0:
                                dve.tensor_copy(out=cxp[:, c * 3:(c + 1) * 3],
                                                in_=ps[4 + c % 2][:, 0:3])
                            dve.tensor_copy(out=xpb(c % 2)[:, 0:3], in_=cxp[:, c * 3:(c + 1) * 3])
                            dve.tensor_copy(out=xpb(c % 2)[:, 3:T + 3], in_=ps[c % 2][:])
                            i = dve.tensor_copy(out=cxp[:, c * 3:(c + 1) * 3],
                                                in_=xpb(c % 2)[:, T:T + 3])
                        tk[p + "xpe%d" % g] = DVs.n
                        for cc in (cA, cB):
                            for kk in range(4):
                                i = dve.tensor_scalar(out=dg_t(cc, kk), in0=eyef[:],
                                                      scalar1=cwf[:, cc * 4 + kk:cc * 4 + kk + 1],
                                                      scalar2=None, op0=ALU.mult)
                        tk[p + "dg%d" % g] = DVs.n
                BV(_dve_e)

                def _ac(act, t=t, g=g):
                    if g > 0:
                        act.wait_ge(PEs.h, tk[p + "cv%d" % (g - 1)])
                        act.wait_ge(PEs.h, tw(q + "projmm"))
                        if t == 0 and g == 1:
                            act.wait_ge(DVs.h, tk["qF13"])
                        for cc in (2 * g - 2, 2 * g - 1):
                            i = act.activation(out=xs_c(cc), in_=ps[2 + cc % 2][:], func=AF.Silu,
                                               bias=CB_c(cc))
                            tk[p + "silu%d" % cc] = ACs.n
                BA(_ac)
                if t == 0:
                    qp_drain(5)
            tk[p + "xpmm_all"] = tk[p + "xp%d" % (NE // 2 - 1)]

            def _wz(sync, t=t):
                sync.wait_ge(PEs.h, tk[p + "xpmm_all"])
                Sw.inc(sync.dma_start(
                    out=wgtA[:].rearrange("p (k f) -> p k f", f=2048),
                    in_=st_win.rearrange("(k p) f -> p k f", p=128)[:, :, 2048:4096]), 16)
                tk[p + "winz"] = Sw.n
            BS(_wz)

            if t + 1 < NT:
                def _xn(sync, t=t):
                    tn = t + 1
                    if tn == 1:
                        sync.wait_ge(DVs.h, tk["p0_quant_done"])
                    else:
                        sync.wait_ge(DVs.h, tk["t%d_stt" % (tn - 2)])
                    Sx.inc(sync.dma_start(
                        out=(x_fm if tn % 2 == 0 else x_fm2)[:].rearrange(
                            "p (k t) -> p k t", t=T),
                        in_=xT.rearrange("(k p) t -> p k t", p=128)[:, :, tn * T:(tn + 1) * T]), 16)
                    tk["t%d_xload" % tn] = Sx.n
                BS(_xn)

            for g in range(NE // 2):
                cA, cB = 2 * g, 2 * g + 1

                def _pe(pe, t=t, g=g, cA=cA, cB=cB):
                    pe.wait_ge(Sw.h, tk[p + "winz"])
                    if g >= 1:
                        pe.wait_ge(ACs.h, tw(p + "sze%d" % (g - 1)))
                    else:
                        pe.wait_ge(DVs.h, tk[p + "xpe%d" % (NE // 2 - 1)])
                    for c in (cA, cB):
                        for k in range(ND):
                            i = pe.matmul(skip_group_check=True, out=ps[c % 2][:],
                                          lhsT=win_sl(k)[:, c * 128:(c + 1) * 128],
                                          rhs=xln_c(k), start=(k == 0), stop=(k == ND - 1))
                    tk[p + "z%d" % g] = PEs.n
                BT(_pe)

                def _ac(act, t=t, g=g, cA=cA, cB=cB):
                    act.wait_ge(PEs.h, tk[p + "z%d" % g])
                    if g == 0:
                        act.wait_ge(PEs.h, tw(q + "projmm"))
                        if t == 0:
                            act.wait_ge(TSG, 96)
                    for c in (cA, cB):
                        i = act.activation(out=sz_c(c), in_=ps[c % 2][:], func=AF.Sigmoid)
                    tk[p + "sze%d" % g] = ACs.n
                BA(_ac)
                if t == 0:
                    qp_drain(5)

            def _bp0(pe, t=t):
                pe.wait_ge(DVs.h, tk[p + "r1dv"])
                last = None
                for k in range(ND):
                    last = pe.matmul(skip_group_check=True, out=ps[5][0:1, :], lhsT=WINBB(k),
                                     rhs=xln_c(k), start=(k == 0), stop=(k == ND - 1))
                tk[p + "bp0"] = PEs.n
            BT(_bp0)

            if t == 0:
                qp_flush_gamma()

            def _wx(pe, t=t):
                pe.wait_ge(ACs.h, tk[p + "silu%d" % (NE - 1)])
                pe.wait_ge(DVs.h, tw(q + "rows3"))
                if t == 0:
                    pe.wait_ge(DVs.h, tk["thrF"])
                    pe.wait_ge(DVs.h, tk["thrP"])
                if t == 0:
                    pe.wait_ge(DVs.h, tw(p + "xpe0"))
                for c in range(NE):
                    pe.matmul(skip_group_check=True, out=ps[6][0:R, :], lhsT=wxbf[:, c * 192:c * 192 + R],
                              rhs=xs_c(c), start=(c == 0), stop=(c == NE - 1))
                last = None
                for c in range(NE):
                    last = pe.matmul(skip_group_check=True, out=ps[7][:], lhsT=wxbf[:, c * 192 + R:(c + 1) * 192],
                                     rhs=xs_c(c), start=(c == 0), stop=(c == NE - 1))
                tk[p + "wx"] = PEs.n
            BT(_wx)

            def _wxe(dve, t=t):
                dve.wait_ge(PEs.h, tk[p + "wx"])
                dve.tensor_copy(out=drbf[0:R, :], in_=ps[6][0:R, :])
                i = dve.tensor_copy(out=Cbf[:], in_=ps[7][:])
                tk[p + "wxe"] = DVs.n
            BV(_wxe)

            sq13 = lambda c: tb2[:, (c % 2) * T:(c % 2 + 1) * T]

            def emit_s13sq(g13):
                def _sq(dve, t=t, g13=g13):
                    if g13 == 0:
                        dve.wait_ge(PEs.h, tk[p + "s1mm1"])
                    else:
                        dve.wait_ge(PEs.h, tk[p + "sv%d" % (g13 - 1)])
                    for c in (2 * g13, 2 * g13 + 1):
                        i = dve.tensor_tensor(out=sq13(c), in0=xs_c(c), in1=xs_c(c),
                                              op=ALU.mult)
                    tk[p + "vsq%d" % g13] = DVs.n
                BV(_sq)

            def emit_s13sv(g13):
                def _mm(pe, t=t, g13=g13):
                    pe.wait_ge(DVs.h, tk[p + "vsq%d" % g13])
                    if g13 == 0:
                        pe.wait_ge(DVs.h, tk[p + "r11c"])
                    for c in (2 * g13, 2 * g13 + 1):
                        pe.matmul(skip_group_check=True, out=ps[4][0:1, :], lhsT=Dbf_c(c),
                                  rhs=xs_c(c), start=(c == 0), stop=(c == NE - 1))
                        i = pe.matmul(skip_group_check=True, out=ps[5][0:1, :], lhsT=D2bf_c(c),
                                      rhs=sq13(c), start=(c == 0), stop=(c == NE - 1))
                    tk[p + "sv%d" % g13] = PEs.n
                BT(_mm)
            emit_s13sq(0)

            def _d0(pe, t=t):
                pe.wait_ge(DVs.h, tk[p + "wxe"])
                pe.wait_ge(DVs.h, tw(p + "r1dv"))
                i = pe.matmul(skip_group_check=True, out=ps[4][0:1, :], lhsT=WDT0B, rhs=drbf[0:R, :],
                              start=True, stop=True)
                tk[p + "d0"] = PEs.n
            BT(_d0)

            def _r11a(dve, t=t):
                dve.wait_ge(PEs.h, tk[p + "d0"])
                dve.wait_ge(PEs.h, tk[p + "bp0"])
                i = dve.tensor_scalar(out=RF(0), in0=ps[4][0:1, :], scalar1=BD0_C[0:1, :],
                                      scalar2=None, op0=ALU.add)
                tk[p + "r11a"] = DVs.n
            BV(_r11a)

            def _r11b(act, t=t):
                act.wait_ge(DVs.h, tk[p + "r11a"])
                act.activation(out=RF(3), in_=RF(0), func=AF.Abs)
                act.activation(out=RF(2), in_=RF(3), func=AF.Exp, scale=-1.0)
                act.activation(out=RF(3), in_=RF(2), func=AF.Ln, bias=1.0)
                act.activation(out=RF(4), in_=ps[5][0:1, :], func=AF.Abs)
                act.activation(out=RF(2), in_=RF(4), func=AF.Exp, scale=-1.0)
                i = act.activation(out=RF(4), in_=RF(2), func=AF.Ln, bias=1.0)
                tk[p + "r11b"] = ACs.n
            BA(_r11b)

            def _r11c(dve, t=t):
                dve.wait_ge(ACs.h, tk[p + "r11b"])
                dve.tensor_scalar(out=RF(2), in0=RF(0), scalar1=0.0, scalar2=None,
                                  op0=ALU.max)
                dve.tensor_tensor(out=RF(0), in0=RF(2), in1=RF(3), op=ALU.add)
                dve.tensor_scalar(out=RF(0), in0=RF(0), scalar1=1e-6, scalar2=100.0,
                                  op0=ALU.max, op1=ALU.min)
                dve.tensor_scalar(out=RRR(0), in0=RF(0), scalar1=-1.0, scalar2=None,
                                  op0=ALU.mult)
                dve.tensor_scalar(out=RF(2), in0=ps[5][0:1, :], scalar1=0.0, scalar2=None,
                                  op0=ALU.max)
                dve.tensor_tensor(out=RF(2), in0=RF(2), in1=RF(4), op=ALU.add)
                dve.tensor_tensor(out=RF(3), in0=RF(0), in1=RF(2), op=ALU.mult)
                i = dve.tensor_tensor(out=RRR(1), in0=RF(3), in1=mixA[0:1, 0:T],
                                      op=ALU.mult)
                tk[p + "r11c"] = DVs.n
            BV(_r11c)

            emit_s13sv(0)
            for g13 in range(1, 8):
                emit_s13sq(g13)
                emit_s13sv(g13)
                if t == 0:
                    qp_drain(2)

            def _abd(pe, t=t):
                pe.wait_ge(DVs.h, tk[p + "r11c"])
                pe.wait_ge(ACs.h, tk[p + "silu%d" % (NE - 1)])
                pe.matmul(skip_group_check=True, out=ps[2][:], lhsT=AEXP_R, rhs=RRR(0), start=True, stop=True)
                i = pe.matmul(skip_group_check=True, out=ps[3][:], lhsT=ONES_R, rhs=RRR(1), start=True, stop=True)
                tk[p + "abd"] = PEs.n
            BT(_abd)

            def _aexp(act, t=t):
                act.wait_ge(PEs.h, tk[p + "abd"])
                i = act.activation(out=tf1[:], in_=ps[2][:], func=AF.Exp)
                tk[p + "aexp"] = ACs.n
            BA(_aexp)

            def _scan(dve, t=t):
                dve.wait_ge(ACs.h, tk[p + "aexp"])
                dve.tensor_tensor_scan(out=tf2[:], data0=tf1[:], data1=ps[3][:],
                                       initial=carry[:], op0=ALU.mult, op1=ALU.add)
                dve.tensor_copy(out=carry[:], in_=tf2[:, T - 1:T])
                i = dve.tensor_tensor(out=Cbf[:], in0=Cbf[:], in1=tf2[:], op=ALU.mult)
                tk[p + "scan"] = DVs.n
            BV(_scan)

            def _ysc(pe, t=t):
                pe.wait_ge(DVs.h, tk[p + "scan"])
                i = pe.matmul(skip_group_check=True, out=ps[6][0:1, :], lhsT=ONESC, rhs=Cbf[:],
                              start=True, stop=True)
                tk[p + "ysc"] = PEs.n
            BT(_ysc)


            def _r13(dve, t=t):
                dve.wait_ge(PEs.h, tk[p + "sv%d" % 7])
                dve.wait_ge(PEs.h, tk[p + "ysc"])
                dve.tensor_scalar(out=RF(0), in0=ps[4][0:1, :], scalar1=INV_E,
                                  scalar2=None, op0=ALU.mult)
                dve.tensor_scalar(out=RF(1), in0=ps[5][0:1, :], scalar1=INV_E,
                                  scalar2=None, op0=ALU.mult)
                dve.tensor_scalar(out=RF(3), in0=ps[6][0:1, :], scalar1=1.0,
                                  scalar2=None, op0=ALU.mult)
                dve.tensor_tensor(out=RF(4), in0=RF(0), in1=RF(3), op=ALU.add)
                dve.tensor_tensor(out=RF(2), in0=RF(0), in1=RF(0), op=ALU.mult)
                i = dve.tensor_tensor(out=RF(1), in0=RF(1), in1=RF(2), op=ALU.subtract)
                tk[p + "r13"] = DVs.n
            BV(_r13)

            def _r13b(act, t=t):
                act.wait_ge(DVs.h, tk[p + "r13"])
                act.activation(out=RF(2), in_=RF(1), func=AF.Ln, bias=EPS_C[0:1, :])
                i = act.activation(out=RRR(0), in_=RF(2), func=AF.Exp, scale=-0.5)
                tk[p + "r13b"] = ACs.n
            BA(_r13b)

            def _r13c(dve, t=t):
                dve.wait_ge(ACs.h, tk[p + "r13b"])
                dve.tensor_tensor(out=RF(2), in0=RF(3), in1=RF(4), op=ALU.subtract)
                i = dve.tensor_scalar(out=RRR(1), in0=RF(2), scalar1=1.0, scalar2=None,
                                      op0=ALU.mult)
                tk[p + "r13c"] = DVs.n
            BV(_r13c)

            def _bc2(pe, t=t):
                pe.wait_ge(DVs.h, tk[p + "r13c"])
                pe.wait_ge(DVs.h, tk[p + "scan"])
                pe.matmul(skip_group_check=True, out=ps[2][:], lhsT=ONES_R, rhs=RRR(0), start=True, stop=True)
                i = pe.matmul(skip_group_check=True, out=ps[3][:], lhsT=ONES_R, rhs=RRR(1), start=True, stop=True)
                tk[p + "bc2"] = PEs.n
            BT(_bc2)

            def _bc2e(dve, t=t):
                dve.wait_ge(PEs.h, tk[p + "bc2"])
                dve.tensor_copy(out=bca[:], in_=ps[2][:])
                i = dve.tensor_copy(out=bcb[:], in_=ps[3][:])
                tk[p + "bc2e"] = DVs.n
            BV(_bc2e)

            def _wol(sync, t=t):
                sync.wait_ge(PEs.h, tw(q + "fcmm"))
                sync.wait_ge(CC, 5)
                if t == 0:
                    sync.wait_ge(TSG, 48)
                Swo.inc(sync.dma_start(
                    out=wgtB[:].rearrange("p (c two f) -> p c two f", two=2, f=DM),
                    in_=st_wout.rearrange("(c p) (two f) -> p c two f", p=128, two=2)), 16)
                tk[p + "wol"] = Swo.n
            BS(_wol)

            def _fcla(sync, t=t):
                sync.wait_ge(PEs.h, tk[p + "z%d" % (NE // 2 - 1)])
                sync.wait_ge(CC, 3)
                SfA.inc(sync.dma_start(
                    out=wgtA[:].rearrange("p (k f) -> p k f", f=4096),
                    in_=st_fc.rearrange("(k p) f -> p k f", p=128)[:, 0:4, :]), 16)
                tk[p + "fcla"] = SfA.n
            BS(_fcla)

            if t == 0:
                qp_finish()

            for c in range(NE):
                def _yg(dve, t=t, c=c):
                    if c == 0:
                        dve.wait_ge(DVs.h, tk[p + "bc2e"])
                        dve.wait_ge(ACs.h, tk[p + "sze%d" % (NE // 2 - 1)])
                        if t == 0:
                            dve.wait_ge(DVs.h, tk["pnP%d" % (NH - 1)])
                        else:
                            dve.wait_ge(PEs.h, tk[q + "woutmm"])
                    dve.scalar_tensor_tensor(out=tf1[:], in0=xs_c(c), scalar=D_c(c),
                                             in1=bcb[:], op0=ALU.mult, op1=ALU.add)
                    dve.tensor_tensor(out=tf2[:], in0=tf1[:], in1=bca[:], op=ALU.mult)
                    i = dve.tensor_scalar(out=(tb1 if c % 2 == 0 else tb2)[:].bitcast(F32),
                                          in0=tf2[:], scalar1=G_c(c), scalar2=Bb_c(c),
                                          op0=ALU.mult, op1=ALU.add)
                    tk[p + "ygn%d" % c] = DVs.n
                BV(_yg)

                def _ygp(gp, t=t, c=c):
                    gp.wait_ge(DVs.h, tk[p + "ygn%d" % c])
                    i = gp.tensor_tensor(out=yg_c(c),
                                         in0=(tb1 if c % 2 == 0 else tb2)[:].bitcast(F32),
                                         in1=sz_c(c), op=ALU.mult)
                    tk[p + "yg%d" % c] = GPs.n
                BG(_ygp)

                def _mm(pe, t=t, c=c):
                    pe.wait_ge(Swo.h, tk[p + "wol"])
                    pe.wait_ge(GPs.h, tk[p + "yg%d" % c])
                    if c == 0:
                        pe.wait_ge(ACs.h, tk[p + "sze%d" % (NE // 2 - 1)])
                        pe.wait_ge(DVs.h, tk[p + "r13"])
                        pe.wait_ge(DVs.h, tk[p + "scan"])
                    for oc in range(ND):
                        i = pe.matmul(skip_group_check=True, out=ps[oc][:],
                                      lhsT=wout_sl(c)[:, oc * 128:(oc + 1) * 128],
                                      rhs=yg_c(c), start=(c == 0), stop=(c == NE - 1))
                    tk[p + "mixk%d" % c] = PEs.n
                BT(_mm)
            tk[p + "yg"] = tk[p + "yg%d" % (NE - 1)]

            def _x2(dve, t=t):
                dve.wait_ge(PEs.h, tk[p + "mixk%d" % (NE - 1)])
                for oc in range(ND):
                    i = dve.tensor_tensor(out=xfm_c(oc), in0=xfm_c(oc), in1=ps[oc][:],
                                          op=ALU.add)
                    tk[p + "x2_%d" % oc] = DVs.n
            BV(_x2)
            tk[p + "woutmm"] = tk[p + "mixk%d" % (NE - 1)]

            for g in range(2):
                def _cst(dve, t=t, g=g):
                    for k in range(4 * g, 4 * g + 4):
                        dve.wait_ge(DVs.h, tk[p + "x2_%d" % k])
                    dve.wait_ge(PEs.h, tw(p + "s2mm%d" % (g - 1)))
                    for k in range(4 * g, 4 * g + 4):
                        i = dve.tensor_copy(out=sq4(k % 4), in_=xfm_c(k))
                    tk[p + "s2c%d" % g] = DVs.n
                BV(_cst)

                def _csq(act, t=t, g=g):
                    for k in range(4 * g, 4 * g + 4):
                        act.wait_ge(DVs.h, tk[p + "x2_%d" % k])
                    act.wait_ge(PEs.h, tw(p + "s2mm%d" % (g - 1)))
                    for k in range(4 * g, 4 * g + 4):
                        i = act.square(out=cb4(k % 4), in_=xfm_c(k))
                    tk[p + "s2a%d" % g] = ACs.n
                BA(_csq)

                def _mm(pe, t=t, g=g):
                    pe.wait_ge(DVs.h, tk[p + "s2c%d" % g])
                    pe.wait_ge(ACs.h, tk[p + "s2a%d" % g])
                    if g == 0:
                        pe.wait_ge(DVs.h, tk[p + "x2_%d" % (ND - 1)])
                        pe.wait_ge(DVs.h, tk[p + "x2_%d" % (ND - 2)])
                    for k in range(4 * g, 4 * g + 4):
                        pe.matmul(skip_group_check=True, out=ps[4][0:1, :], lhsT=ONESC,
                                  rhs=sq4(k % 4), start=(k == 0), stop=(k == ND - 1))
                        i = pe.matmul(skip_group_check=True, out=ps[5][0:1, :], lhsT=ONESC,
                                      rhs=cb4(k % 4), start=(k == 0), stop=(k == ND - 1))
                    tk[p + "s2mm%d" % g] = PEs.n
                BT(_mm)

            def _r2dv(dve, t=t):
                dve.wait_ge(PEs.h, tk[p + "s2mm%d" % 1])
                dve.tensor_scalar(out=RF(0), in0=ps[4][0:1, :], scalar1=INV_D,
                                  scalar2=None, op0=ALU.mult)
                dve.tensor_scalar(out=RF(1), in0=ps[5][0:1, :], scalar1=INV_D,
                                  scalar2=None, op0=ALU.mult)
                dve.tensor_tensor(out=RF(2), in0=RF(0), in1=RF(0), op=ALU.mult)
                i = dve.tensor_tensor(out=RF(1), in0=RF(1), in1=RF(2), op=ALU.subtract)
                tk[p + "r2dv"] = DVs.n
            BV(_r2dv)

            def _r2ac(act, t=t):
                act.wait_ge(DVs.h, tk[p + "r2dv"])
                act.activation(out=RF(2), in_=RF(1), func=AF.Ln, bias=EPS_C[0:1, :])
                i = act.activation(out=RRR(0), in_=RF(2), func=AF.Exp, scale=-0.5)
                tk[p + "r2ac"] = ACs.n
            BA(_r2ac)

            def _r2dv2(dve, t=t):
                dve.wait_ge(ACs.h, tk[p + "r2ac"])
                dve.tensor_scalar(out=RF(2), in0=RF(0), scalar1=-1.0, scalar2=None,
                                  op0=ALU.mult)
                i = dve.tensor_tensor(out=RRR(1), in0=RF(2), in1=RRR(0), op=ALU.mult)
                tk[p + "r2dv2"] = DVs.n
            BV(_r2dv2)

            def _bc2b(pe, t=t):
                pe.wait_ge(DVs.h, tk[p + "r2dv2"])
                pe.wait_ge(DVs.h, tk[p + "yg"])
                pe.matmul(skip_group_check=True, out=ps[2][:], lhsT=ONES_R, rhs=RRR(0), start=True, stop=True)
                i = pe.matmul(skip_group_check=True, out=ps[3][:], lhsT=ONES_R, rhs=RRR(1), start=True, stop=True)
                tk[p + "bcL2"] = PEs.n
            BT(_bc2b)

            def _bc2be(dve, t=t):
                dve.wait_ge(PEs.h, tk[p + "bcL2"])
                dve.tensor_copy(out=bca[:], in_=ps[2][:])
                i = dve.tensor_copy(out=bcb[:], in_=ps[3][:])
                tk[p + "bcL2e"] = DVs.n
            BV(_bc2be)

            def _ap2(dve, t=t):
                dve.wait_ge(DVs.h, tk[p + "bcL2e"])
                dve.wait_ge(PEs.h, tk[p + "bp0"])
                for k in range(ND):
                    dve.tensor_tensor(out=tf1[:], in0=xfm_c(k), in1=bca[:], op=ALU.mult)
                    i = dve.tensor_tensor(out=xln_c(k), in0=tf1[:], in1=bcb[:],
                                          op=ALU.add)
                    if k == 3:
                        tk[p + "ap2h"] = DVs.n
                tk[p + "ap2"] = DVs.n
            BV(_ap2)


            def _fclb(sync, t=t):
                sync.wait_ge(PEs.h, tk[p + "woutmm"])
                if t == 0:
                    sync.wait_ge(CC, 3)
                Sfc.inc(sync.dma_start(
                    out=wgtB[:].rearrange("p (k f) -> p k f", f=4096),
                    in_=st_fc.rearrange("(k p) f -> p k f", p=128)[:, 4:8, :]), 16)
                tk[p + "fclb"] = Sfc.n
            BS(_fclb)

            sq17 = lambda j: (tb1 if j % 8 < 2 else tb2 if j % 8 < 4 else
                              tf1[:].bitcast(BF16) if j % 8 < 6 else
                              tf2[:].bitcast(BF16))[:, (j % 2) * T:(j % 2 + 1) * T]
            for g in range(NH // 4 + 2):
                ocs = [oc for oc in range(4 * g, 4 * g + 4) if oc < NH]

                def _pe(pe, t=t, g=g, ocs=tuple(ocs)):
                    for oc in ocs:
                        pe.wait_ge(SfA.h, tk[p + "fcla"])
                        pe.wait_ge(DVs.h, tk[p + "ap2h"])
                        pe.wait_ge(ACs.h, tw(p + "gel%d" % (g - 1)))
                        if oc == 0:
                            pe.wait_ge(DVs.h, tk[p + "bcL2e"])
                            pe.wait_ge(DVs.h, tk[p + "yg"])
                        for k in range(ND):
                            if k == ND // 2:
                                pe.wait_ge(Sfc.h, tk[p + "fclb"])
                                pe.wait_ge(DVs.h, tk[p + "ap2"])
                            i = pe.matmul(skip_group_check=True, out=ps[oc % 4][:],
                                          lhsT=fc_lhs(k, oc), rhs=xln_c(k),
                                          start=(k == 0), stop=(k == ND - 1))
                    if ocs:
                        tk[p + "h%d" % g] = PEs.n
                    if g > 1:
                        for j in range(4 * (g - 2), 4 * (g - 1)):
                            pe.wait_ge(DVs.h, tk[p + "hsq%d" % (g - 2)])
                            if j == 0:
                                pe.wait_ge(DVs.h, tk[p + "r13c"])
                                pe.wait_ge(DVs.h, tk[p + "scan"])
                            pe.matmul(skip_group_check=True, out=ps[6][0:1, :], lhsT=ONESC,
                                      rhs=hg_c(j), start=(j == 0), stop=(j == NH - 1))
                            i = pe.matmul(skip_group_check=True, out=ps[7][0:1, :], lhsT=ONESC,
                                          rhs=sq17(j), start=(j == 0), stop=(j == NH - 1))
                        tk[p + "hs%d" % (g - 2)] = PEs.n
                BT(_pe)

                def _ac(act, t=t, g=g, ocs=tuple(ocs)):
                    if ocs:
                        act.wait_ge(PEs.h, tk[p + "h%d" % g])
                        for oc in ocs:
                            i = act.activation(out=hg_c(oc), in_=ps[oc % 4][:], func=AF.Gelu)
                        tk[p + "gel%d" % g] = ACs.n
                BA(_ac)

                def _sqd(dve, t=t, g=g, ocs=tuple(ocs)):
                    if ocs:
                        dve.wait_ge(ACs.h, tk[p + "gel%d" % g])
                        dve.wait_ge(PEs.h, tw(p + "hs%d" % (g - 2)))
                        for oc in ocs:
                            i = dve.tensor_tensor(out=sq17(oc), in0=hg_c(oc),
                                                  in1=hg_c(oc), op=ALU.mult)
                        tk[p + "hsq%d" % g] = DVs.n
                BV(_sqd)
            tk[p + "fcmm"] = tk[p + "h%d" % (NH // 4 - 1)]

            def _r3dv(dve, t=t):
                dve.wait_ge(PEs.h, tk[p + "hs%d" % (NH // 4 - 1)])
                dve.tensor_scalar(out=RF(0), in0=ps[6][0:1, :], scalar1=INV_H,
                                  scalar2=None, op0=ALU.mult)
                dve.tensor_scalar(out=RF(1), in0=ps[7][0:1, :], scalar1=INV_H,
                                  scalar2=None, op0=ALU.mult)
                dve.tensor_tensor(out=RF(2), in0=RF(0), in1=RF(0), op=ALU.mult)
                i = dve.tensor_tensor(out=RF(1), in0=RF(1), in1=RF(2), op=ALU.subtract)
                tk[p + "rows3"] = DVs.n
            BV(_r3dv)

            def _r3ac(act, t=t):
                act.wait_ge(DVs.h, tk[p + "rows3"])
                act.activation(out=RF(2), in_=RF(1), func=AF.Ln, bias=EPS_C[0:1, :])
                i = act.activation(out=RRR(0), in_=RF(2), func=AF.Exp, scale=-0.5)
                tk[p + "r3ac"] = ACs.n
            BA(_r3ac)

            def _r3dv2(dve, t=t):
                dve.wait_ge(ACs.h, tk[p + "r3ac"])
                dve.tensor_scalar(out=RF(2), in0=RF(0), scalar1=-1.0, scalar2=None,
                                  op0=ALU.mult)
                i = dve.tensor_tensor(out=M3_BF, in0=RF(2), in1=RRR(0), op=ALU.mult)
                tk[p + "r3dv2"] = DVs.n
            BV(_r3dv2)

            def _bc3(pe, t=t):
                pe.wait_ge(DVs.h, tk[p + "r3dv2"])
                pe.wait_ge(DVs.h, tk[p + "ap2"])
                i = pe.matmul(skip_group_check=True, out=ps[2][:], lhsT=ONES_R, rhs=RRR(0), start=True, stop=True)
                tk[p + "bc3"] = PEs.n
            BT(_bc3)

            def _bc3e(dve, t=t):
                dve.wait_ge(PEs.h, tk[p + "bc3"])
                i = dve.tensor_copy(out=bca[:], in_=ps[2][:])
                tk[p + "bc3e"] = DVs.n
            BV(_bc3e)

            for gh in range(8):
                def _hs(dve, t=t, gh=gh):
                    if gh == 0:
                        dve.wait_ge(DVs.h, tk[p + "bc3e"])
                    for j in range(4 * gh, 4 * gh + 4):
                        i = dve.scalar_tensor_tensor(out=hg_c(j), in0=hg_c(j),
                                                     scalar=LNW_c(j), in1=bca[:],
                                                     op0=ALU.mult, op1=ALU.mult)
                    tk[p + "hsg%d" % gh] = DVs.n
                BV(_hs)
            tk[p + "hs"] = tk[p + "hsg7"]

            if t == 0:
                for k in range(NH):
                    if k % 2 == 0:
                        def _ld(sync, k=k):
                            sync.wait_ge(PEs.h, tw("ucmm%d" % (k - 3)))
                            if k < 4:
                                sync.wait_ge(CC, 4)
                                sync.wait_ge(TSG, 64)
                            i = sync.dma_start(out=pj_pair(k).rearrange("p (two f) -> p two f", f=DM),
                                               in_=st_proj[(k // 4) * 128:(k // 4 + 1) * 128,
                                                           (k % 4) * DM:(k % 4 + 2) * DM]
                                               .rearrange("p (two f) -> p two f", f=DM))
                            tk["ucl%d" % k] = Spj[(k // 2) % 2].inc(i, 16)
                        BS(_ld)

                    def _mm(pe, k=k):
                        pe.wait_ge(Spj[(k // 2) % 2].h, tk["ucl%d" % (k - k % 2)])
                        if k == 0:
                            pe.wait_ge(ACs.h, tk[p + "gel%d" % (NH // 4 - 1)])
                            pe.wait_ge(DVs.h, tk[p + "bc3e"])
                            pe.wait_ge(DVs.h, tk[p + "bcL2e"])
                            pe.wait_ge(DVs.h, tk[p + "r2dv"])
                            pe.wait_ge(DVs.h, tk[p + "rows3"])
                        last = None
                        for oc in range(ND):
                            last = pe.matmul(skip_group_check=True, out=ps[oc][:, 0:2],
                                             lhsT=pj_sl(k)[:, oc * 128:(oc + 1) * 128],
                                             rhs=LNWB(k), start=(k == 0), stop=(k == NH - 1))
                        tk["ucmm%d" % k] = PEs.n
                    BT(_mm)

                def _uce(dve):
                    dve.wait_ge(PEs.h, tk["ucmm%d" % (NH - 1)])
                    for oc in range(ND):
                        i = dve.tensor_copy(out=uccol[:, oc * 2:oc * 2 + 2],
                                            in_=ps[oc][:, 0:2])
                    tk["uce"] = DVs.n
                BV(_uce)

                def _ucw(gp):
                    gp.wait_ge(DVs.h, tk["uce"])
                    i = gp.dma_start(out=d_uc, in_=uccol[:])
                    tk["ucw"] = Tout.inc(i, 16)
                BG(_ucw)

                def _ucr(gp):
                    gp.wait_ge(Tout.h, tk["ucw"])
                    Sub[0].inc(gp.dma_start(out=ucrow[0:1, 0:1024].rearrange("one (oc j) -> one oc j", j=128),
                               in_=d_uc.rearrange("j (oc two) -> two oc j", two=2)[0:1],
                               allow_slow_non_contiguous=True), 16)
                    Sub[1].inc(gp.dma_start(out=ucrow[0:1, 1024:2048].rearrange("one (oc j) -> one oc j", j=128),
                               in_=d_uc.rearrange("j (oc two) -> two oc j", two=2)[1:2],
                               allow_slow_non_contiguous=True), 16)
                BG(_ucr)

            for k in range(NH):
                if k % 2 == 0:
                    def _ld(sync, t=t, k=k):
                        if k >= 4:
                            sync.wait_ge(PEs.h, tk[p + "pj%d" % (k - 3)])
                        elif t == 0:
                            sync.wait_ge(PEs.h, tk["ucmm%d" % (NH - 1)])
                        else:
                            sync.wait_ge(PEs.h, tk[q + "pj%d" % (NH - 1)])
                        i = sync.dma_start(out=pj_pair(k).rearrange("p (two f) -> p two f", f=DM),
                                           in_=st_proj[(k // 4) * 128:(k // 4 + 1) * 128,
                                                       (k % 4) * DM:(k % 4 + 2) * DM]
                                           .rearrange("p (two f) -> p two f", f=DM))
                        tk[p + "pjl%d" % k] = Spj[(k // 2) % 2].inc(i, 16)
                    BS(_ld)

                def _mm(pe, t=t, k=k):
                    pe.wait_ge(Spj[(k // 2) % 2].h, tk[p + "pjl%d" % (k - k % 2)])
                    pe.wait_ge(DVs.h, tk[p + "hsg%d" % (k // 4)])
                    if k == 0:
                        pe.wait_ge(ACs.h, tk[p + "gel%d" % (NH // 4 - 1)])
                        pe.wait_ge(DVs.h, tk[p + "bc3e"])
                        pe.wait_ge(DVs.h, tk[p + "r2dv"])
                        pe.wait_ge(DVs.h, tk[p + "rows3"])
                        if t == 0:
                            pe.wait_ge(DVs.h, tk["uce"])
                    last = None
                    for oc in range(ND):
                        last = pe.matmul(skip_group_check=True, out=ps[oc][:],
                                         lhsT=pj_sl(k)[:, oc * 128:(oc + 1) * 128],
                                         rhs=hg_c(k), start=(k == 0), stop=False)
                    tk[p + "pj%d" % k] = PEs.n
                BT(_mm)

            def _rk1(pe, t=t):
                pe.wait_ge(DVs.h, tk[p + "r3dv2"])
                if t == 0:
                    pe.wait_ge(Sub[0].h, 16)
                    pe.wait_ge(Sub[1].h, 16)
                last = None
                for oc in range(ND):
                    pe.matmul(skip_group_check=True, out=ps[oc][:], lhsT=U_ROW(oc), rhs=M3_BF,
                              start=False, stop=False)
                    last = pe.matmul(skip_group_check=True, out=ps[oc][:], lhsT=C_ROW(oc), rhs=ONEST_BF,
                                     start=False, stop=True)
                tk[p + "rk1"] = PEs.n
            BT(_rk1)

            def _stt(dve, t=t):
                dve.wait_ge(PEs.h, tk[p + "rk1"])
                dve.wait_ge(Tout.h, tw(q + "out"))
                if t == 0:
                    dve.wait_ge(DVs.h, tk["pnP%d" % (NH - 1)])
                last = None
                for oc in range(ND):
                    last = dve.scalar_tensor_tensor(out=outs_c(oc), in0=ps[oc][:],
                                                    scalar=1.0, in1=xfm_c(oc),
                                                    op0=ALU.mult, op1=ALU.add)
                tk[p + "stt"] = DVs.n
            BV(_stt)

            def _out(gp, t=t):
                gp.wait_ge(DVs.h, tk[p + "stt"])
                i = gp.dma_start(
                    out=yT.rearrange("(oc p) t -> p oc t", p=128)[:, :, t * T:(t + 1) * T],
                    in_=out_s[:].rearrange("p (oc t) -> p oc t", t=T))
                Tout.inc(i, 16)
                tk[p + "out"] = Tout.n
            BG(_out)

        def _fin(gp):
            gp.wait_ge(Tout.h, Tout.n)
        BG(_fin)

    return nc


_NC_CACHE = {}


def _get_program():
    if "nc" not in _NC_CACHE:
        _NC_CACHE["nc"] = build_program()
    return _NC_CACHE["nc"]


def kernel(L0_attn_x, L0_attn_Win, L0_attn_conv_w, L0_attn_conv_b, L0_attn_Wx,
           L0_attn_Wdt, L0_attn_bdt, L0_attn_A_log, L0_attn_D, L0_attn_ln_g,
           L0_attn_ln_b, L0_attn_Wout, L0_attn_res_scale, L0_ffn_W_fc, L0_ffn_ln_w,
           L0_ffn_ln_b, L0_ffn_W_proj):
    f32 = np.float32
    x = np.asarray(L0_attn_x, f32)[0]
    xTf = np.ascontiguousarray(x.T)
    win = np.asarray(L0_attn_Win, f32)
    winx = np.ascontiguousarray(win[:, 0:4096])
    winbc = np.ascontiguousarray(win[:, 4096:4097])
    convw = np.ascontiguousarray(np.asarray(L0_attn_conv_w, f32))
    convb = np.ascontiguousarray(np.asarray(L0_attn_conv_b, f32))
    wxw = np.ascontiguousarray(np.asarray(L0_attn_Wx, f32))
    wdt0 = np.ascontiguousarray(np.asarray(L0_attn_Wdt, f32)[:, 0:1])
    bdt0c = np.ascontiguousarray(np.broadcast_to(np.asarray(L0_attn_bdt, f32)[0:1],
                                                 (128,))).reshape(128, 1).copy()
    a0row = np.ascontiguousarray(np.asarray(L0_attn_A_log, f32)[0:1, :])
    dvec = np.ascontiguousarray(np.asarray(L0_attn_D, f32))
    lngv = np.ascontiguousarray(np.asarray(L0_attn_ln_g, f32))
    lnbv = np.ascontiguousarray(np.asarray(L0_attn_ln_b, f32))
    woutw = np.ascontiguousarray(np.asarray(L0_attn_Wout, f32))
    rs = np.asarray(L0_attn_res_scale, f32).reshape(-1)[0:1]
    rsc128 = np.ascontiguousarray(np.broadcast_to(rs, (128,))).reshape(128, 1).copy()
    fcT = np.ascontiguousarray(np.asarray(L0_ffn_W_fc, f32).T)
    lnw3v = np.ascontiguousarray(np.asarray(L0_ffn_ln_w, f32))
    lnb3v = np.ascontiguousarray(np.asarray(L0_ffn_ln_b, f32))
    projT = np.ascontiguousarray(np.asarray(L0_ffn_W_proj, f32).T)
    eyem = np.eye(128, dtype=f32)

    in_maps = []
    for c in range(NCORES):
        lo = c * TOK
        xs = np.ascontiguousarray(xTf[:, lo:lo + TOK])
        if c == 0:
            xlb = np.zeros((DM, 3), f32)
        else:
            xlb = np.ascontiguousarray(xTf[:, lo - 3:lo])
        in_maps.append(dict(
            xT=xs, xLB=xlb, winbc=winbc, convw=convw, convb=convb,
            wxw=wxw, wdt0=wdt0, bdt0c=bdt0c, a0row=a0row, dvec=dvec, lngv=lngv,
            lnbv=lnbv, rsc128=rsc128, lnw3v=lnw3v, lnb3v=lnb3v, eye=eyem,
            winshard=np.ascontiguousarray(winx[c * 128:(c + 1) * 128, :]),
            fcshard=np.ascontiguousarray(fcT[c * 128:(c + 1) * 128, :]),
            pjshard=np.ascontiguousarray(
                np.concatenate([projT[(4 * c + j) * 128:(4 * c + j + 1) * 128, :]
                                for j in range(4)], axis=1)),
            woshard=np.ascontiguousarray(
                np.concatenate([woutw[(2 * c + j) * 128:(2 * c + j + 1) * 128, :]
                                for j in range(2)], axis=1)),
        ))

    nc = _get_program()
    res = run_bass_kernel_spmd(nc, in_maps, core_ids=list(range(NCORES)))
    out = np.empty((DM, NCORES * TOK), f32)
    for c in range(NCORES):
        out[:, c * TOK:(c + 1) * TOK] = res.results[c]["yT"]
    return np.ascontiguousarray(out.T).reshape(1, NCORES * TOK, DM)
